# version 21
# baseline (speedup 1.0000x reference)
import numpy as np

import math
import ml_dtypes
import concourse.bass as bass
import concourse.mybir as mybir

F32 = mybir.dt.float32
BF16 = mybir.dt.bfloat16
I32 = mybir.dt.int32
ALU = mybir.AluOpType
ACT = mybir.ActivationFunctionType
AX = mybir.AxisListType.X

D = 1024
H = 16
DH = 64
LR = 64
FI = 3072
FD = 128
L = 64
NF = 8
EPS = 1e-5
QMAX16 = 32767.0


class Reg:
    __slots__ = ("name", "w", "r")

    def __init__(self, name="r"):
        self.name = name
        self.w = None
        self.r = []


class Em:
    ENG = ("t", "v", "s", "g", "y")
    QUEUES = {"y": "qy", "s": "qs", "g": "qg"}

    KSLOT = 6

    def __init__(self, nc):
        self.nc = nc
        self.plan = {k: [] for k in self.ENG}
        self.planned_inc = {k: {} for k in self.ENG}
        self.ninc = {k: 0 for k in self.ENG}
        self.ndma = {q: 0 for q in ("qy", "qs", "qg")}
        self.ncc = 0
        self.waited = {k: {} for k in self.ENG}
        self.nop = 0
        self._pending_waits = {}
        self.sems = {}
        self.semkeys = list(self.ENG) + ["cc"]
        for q in ("qy", "qs", "qg"):
            for i in range(self.KSLOT):
                self.semkeys.append(f"{q}{i}")
        for k in self.semkeys:
            self.sems[k] = nc.alloc_semaphore(f"sem_{k}")
        self.slot_uses = {k: 0 for k in self.semkeys}

    def _ensure_inc(self, eng, idx):
        ks = [j for j in self.planned_inc[eng] if j >= idx]
        if ks:
            return self.planned_inc[eng][min(ks)]
        self.ninc[eng] += 1
        val = self.ninc[eng]
        self.planned_inc[eng][idx] = val
        self.plan[eng][idx][2].append((eng, 1))
        return val

    def _wait(self, consumer, token):
        if token is None:
            return
        kind, x = token
        if kind in self.ENG:
            val = self._ensure_inc(kind, x)
            skey = kind
        elif kind == "cc":
            val, skey = self.ncc, kind
        else:
            val, skey = x, kind
        prev = self.waited[consumer].get(skey, -1)
        if val <= prev:
            return
        self._pending_waits.setdefault(consumer, []).append((skey, val))
        self.waited[consumer][skey] = val

    def _deps(self, consumer, reads, writes):
        for r in reads:
            self._wait(consumer, r.w)
        for w in writes:
            self._wait(consumer, w.w)
            for t in w.r:
                self._wait(consumer, t)

    def _update(self, token, reads, writes):
        for r in reads:
            r.r.append(token)
        for w in writes:
            w.w = token
            w.r = []

    def _push(self, eng, fn, kind="op"):
        waits = self._pending_waits.pop(eng, [])
        idx = len(self.plan[eng])
        self.plan[eng].append([fn, waits, [], kind])
        self.nop += 1
        return idx

    def op(self, eng, fn, reads=(), writes=(), kind="op"):
        self._deps(eng, reads, writes)
        idx = self._push(eng, fn, kind)
        self._update((eng, idx), reads, writes)
        return (eng, idx)

    def dma(self, eng, out, in_, reads=(), writes=(), **kw):
        q = self.QUEUES[eng]
        self._deps(eng, reads, writes)
        n = self.ndma[q]
        self.ndma[q] += 1
        skey = f"{q}{n % self.KSLOT}"
        self.slot_uses[skey] += 1
        uses = self.slot_uses[skey]
        if uses > 1:
            prev = 16 * (uses - 1)
            if prev > self.waited[eng].get(skey, -1):
                self._pending_waits.setdefault(eng, []).append((skey, prev))
                self.waited[eng][skey] = prev
        idx = self._push(eng, lambda e, out=out, in_=in_, kw=kw: e.dma_start(
            out=out, in_=in_, **kw))
        self.plan[eng][idx][2].append((skey, 16))
        self._update((skey, 16 * uses), reads, writes)
        return (skey, 16 * uses)

    def collective(self, kind, op, ins, outs, reads=(), writes=()):
        self._deps("g", reads, writes)
        self.ncc += 1
        n = self.ncc
        idx = self._push(
            "g", lambda e, k=kind, o=op, i=ins, u=outs: e.collective_compute(
                k, o, replica_groups=[list(range(8))], ins=i, outs=u))
        self.plan["g"][idx][2].append(("cc", 1))
        self._update(("cc", n), reads, writes)
        return ("cc", n)

    def sync_engine(self, eng, regs):
        for r in regs:
            self._wait(eng, r.w)
            for t in r.r:
                self._wait(eng, t)

    def final_waits(self, eng):
        for k in self.semkeys:
            if k in self.ENG or k == "cc":
                continue
            v = 16 * self.slot_uses[k]
            if v > self.waited[eng].get(k, -1) and v > 0:
                self._pending_waits.setdefault(eng, []).append((k, v))

    def emit(self, block):
        sems = self.sems

        def run(e, eng):
            for fn, waits, incs, kind in self.plan[eng]:
                for skey, val in waits:
                    e.wait_ge(sems[skey], val)
                inst = fn(e)
                for skey, n in incs:
                    inst.then_inc(sems[skey], n)
            for skey, val in self._pending_waits.pop(eng, []):
                e.wait_ge(sems[skey], val)

        block.tensor(lambda e: run(e, "t"))
        block.vector(lambda e: run(e, "v"))
        block.scalar(lambda e: run(e, "s"))
        block.gpsimd(lambda e: run(e, "g"))
        block.sync(lambda e: run(e, "y"))


class Arena:
    def __init__(self, nc):
        self.nc = nc
        self.base = (nc.sbuf_base + 31) // 32 * 32
        self.top = nc.sbuf_top
        self.off = self.base
        self.marks = []
        self.peak = self.base

    def alloc_h(self, name, shape, dtype):
        esz = 2 if dtype in (BF16, mybir.dt.float16) else 4
        per = (int(shape[1]) * esz + 31) // 32 * 32
        h = self.nc.alloc_sbuf_tensor_at(name, list(shape), dtype, offset=self.off)
        self.off += per
        self.peak = max(self.peak, self.off)
        assert self.off <= self.top, f"SBUF overflow at {name}: {(self.off-self.base)/1024:.0f}KB"
        return h

    def mark(self):
        self.marks.append(self.off)

    def release(self):
        self.off = self.marks.pop()

    def kb(self):
        return (self.off - self.base) / 1024.0


class T:
    __slots__ = ("h", "reg")

    def __init__(self, h, reg):
        self.h = h
        self.reg = reg

    def __getitem__(self, key):
        return self.h[key]

    @property
    def ap(self):
        return self.h[:, :] if not isinstance(self.h, bass.AP) else self.h


class MK:

    def __init__(self, Tc=2048, blocks=("l0a", "l0f", "l1a", "l1f")):
        self.Tc = Tc
        self.blocks = blocks
        self.NT = Tc // 128
        self.NCH = Tc // L
        self.nc = bass.Bass("TRN2", target_bir_lowering=False, debug=False)
        self.em = Em(self.nc)
        self.ar = Arena(self.nc)
        self.inputs = {}
        self.n_dram = 0

    def din(self, name, shape):
        ap = self.nc.dram_tensor(name, list(shape), F32, kind="ExternalInput").ap()
        t = T(ap, Reg(name))
        self.inputs[name] = t
        return t

    def dout(self, name, shape, dtype=F32):
        ap = self.nc.dram_tensor(name, list(shape), dtype, kind="ExternalOutput").ap()
        return T(ap, Reg(name))

    def dscratch(self, name, shape, dtype=F32):
        ap = self.nc.dram_tensor(name, list(shape), dtype).ap()
        return T(ap, Reg(name))

    def sb(self, name, shape, dtype):
        t = T(self.ar.alloc_h(name, shape, dtype), Reg(name))
        if not hasattr(self, "_scopes"):
            self._scopes = []
            self._fence = []
        t.reg.r = list(self._fence)
        if self._scopes:
            self._scopes[-1].append(t)
        return t

    def mark(self):
        if not hasattr(self, "_scopes"):
            self._scopes = []
            self._fence = []
        self.ar.mark()
        self._scopes.append([])

    def release(self):
        ts = self._scopes.pop()
        for t in ts:
            if t.reg.w is not None:
                self._fence.append(t.reg.w)
            self._fence.extend(t.reg.r)
        best = {}
        for k, v in self._fence:
            if k not in best or v > best[k]:
                best[k] = v
        self._fence = [(k, v) for k, v in best.items()]
        self.ar.release()

    def sbs(self, name, shape, dtype, n):
        return [self.sb(f"{name}{i}", shape, dtype) for i in range(n)]

    def psum_init(self):
        self.pb = []
        for b in range(8):
            h = self.nc.alloc_psum_tensor(f"pbank{b}", [128, 512], F32)
            self.pb.append(T(h, Reg(f"pb{b}")))

    def mm(self, out_ap, lhsT, lhsT_ap, rhs, rhs_ap, bank, start, stop):
        self.em.op("t", lambda t: t.matmul(out_ap, lhsT_ap, rhs_ap,
                                           start=start, stop=stop),
                   reads=[lhsT.reg, rhs.reg], writes=[bank.reg])

    def tt(self, eng, out_t, out_ap, a_t, a_ap, b_t, b_ap, op, extra_reads=()):
        self.em.op(eng, lambda e: e.tensor_tensor(out_ap, a_ap, b_ap, op),
                   reads=[a_t.reg, b_t.reg, *extra_reads], writes=[out_t.reg])

    def ts(self, eng, out_t, out_ap, a_t, a_ap, s1, op0, s2=None, op1=ALU.bypass,
           s_regs=()):
        self.em.op(eng, lambda e: e.tensor_scalar(out_ap, a_ap, s1, s2, op0, op1),
                   reads=[a_t.reg, *s_regs], writes=[out_t.reg])

    def act(self, eng, out_t, out_ap, a_t, a_ap, func, bias=0.0, scale=1.0,
            extra_reads=()):
        self.em.op(eng, lambda e: e.activation(out_ap, a_ap, func, bias=bias,
                                               scale=scale),
                   reads=[a_t.reg, *extra_reads], writes=[out_t.reg])

    def copy(self, eng, out_t, out_ap, a_t, a_ap):
        self.em.op(eng, lambda e: e.tensor_copy(out_ap, a_ap),
                   reads=[a_t.reg], writes=[out_t.reg])

    def ccol(self, val):
        if not hasattr(self, "_ccols"):
            self._ccols = {}
            self._ccol_t = self.sb("cconst", (128, 16), F32)
            self._ccol_n = 0
        if val not in self._ccols:
            k = self._ccol_n
            self._ccol_n += 1
            assert k < 16
            self.em.op("g", lambda g, k=k, val=val: g.memset(
                self._ccol_t[:, k:k + 1], float(val)), writes=[self._ccol_t.reg])
            self._ccols[val] = k
        k = self._ccols[val]
        return self._ccol_t[:, k:k + 1]

    @property
    def ccol_reg(self):
        return self._ccol_t.reg

    def emit_consts(self):
        em = self.em
        self.ones_col = self.sb("ones_col", (128, 2), BF16)
        self.ident = self.sb("ident", (128, 128), BF16)
        self.onesf = self.sb("onesf", (128, 128), F32)
        self.mstrict = self.sb("mstrict", (128, 64), F32)
        self.mdiag = self.sb("mdiag", (128, 64), F32)
        em.op("g", lambda g: g.memset(self.ones_col[:, :], 1.0),
              writes=[self.ones_col.reg])
        em.op("g", lambda g: g.memset(self.onesf[:, :], 1.0), writes=[self.onesf.reg])
        self.onesb = self.sb("onesb", (128, 128), BF16)
        self.identf = self.sb("identf", (128, 128), F32)
        em.op("g", lambda g: g.memset(self.onesb[:, :], 1.0), writes=[self.onesb.reg])
        em.op("g", lambda g: g.affine_select(
            self.ident[:, :], self.onesb[:, :], [[1, 128]],
            ALU.is_equal, 0.0, base=0, channel_multiplier=-1),
            reads=[self.onesb.reg], writes=[self.ident.reg])
        em.op("g", lambda g: g.affine_select(
            self.identf[:, :], self.onesf[:, :], [[1, 128]],
            ALU.is_equal, 0.0, base=0, channel_multiplier=-1),
            reads=[self.onesf.reg], writes=[self.identf.reg])
        em.op("g", lambda g: g.affine_select(
            self.mstrict[0:64, :], self.onesf[0:64, 0:64], [[1, 64]],
            ALU.is_ge, 0.0, base=-1, channel_multiplier=-1),
            reads=[self.onesf.reg], writes=[self.mstrict.reg])
        em.op("g", lambda g: g.affine_select(
            self.mdiag[0:64, :], self.onesf[0:64, 0:64], [[1, 64]],
            ALU.is_equal, 0.0, base=0, channel_multiplier=-1),
            reads=[self.onesf.reg], writes=[self.mdiag.reg])
        em.op("g", lambda g: g.tensor_copy(self.mstrict[64:128, :],
                                           self.mstrict[0:64, :]),
              reads=[self.mstrict.reg], writes=[self.mstrict.reg])
        em.op("g", lambda g: g.tensor_copy(self.mdiag[64:128, :],
                                           self.mdiag[0:64, :]),
              reads=[self.mdiag.reg], writes=[self.mdiag.reg])

    def emit_load_x(self):
        xin = self.din("x_fm", (D, self.Tc))
        self.xdram = xin
        self.xping = [self.dscratch("x_ping0", (D, self.Tc), F32),
                      self.dscratch("x_ping1", (D, self.Tc), F32)]
        self.nping = 0

    def xdram_next(self):
        t = self.xping[self.nping % 2]
        self.nping += 1
        return t

    def rsqrt_refine(self, out_t, out_ap, s_t, s0_ap, d_ap):
        em = self.em
        em.op("v", lambda v: v.reciprocal(out_ap, s0_ap),
              reads=[s_t.reg], writes=[out_t.reg])
        em.op("v", lambda v: v.scalar_tensor_tensor(
            out_ap, d_ap, 0.5, out_ap, ALU.mult, ALU.mult),
            reads=[s_t.reg, out_t.reg], writes=[out_t.reg])
        em.op("v", lambda v: v.scalar_tensor_tensor(
            out_ap, s0_ap, 0.5, out_ap, ALU.mult, ALU.add),
            reads=[s_t.reg, out_t.reg], writes=[out_t.reg])
        em.op("v", lambda v: v.reciprocal(out_ap, out_ap),
              reads=[out_t.reg], writes=[out_t.reg])

    def emit_ln(self, tag, xln=None, xlnf=None, want_f32=False,
                want_bf16=True, X_pre=None, f32_inplace=False):
        em, Tc = self.em, self.Tc
        if xln is None and want_bf16:
            xln = self.sbs(f"xln_{tag}", (128, Tc), BF16, NF)
        if want_f32 and xlnf is None and not f32_inplace:
            xlnf = self.sbs(f"xlnf_{tag}", (128, Tc), F32, NF)
        self.mark()
        X = X_pre if X_pre is not None else self.sbs(
            f"ln_x_{tag}", (128, Tc), F32, NF)
        for f in range(NF):
            self.em.dma("y", X[f][:, :],
                        self.xdram.h[128 * f:128 * (f + 1), :],
                        reads=[self.xdram.reg], writes=[X[f].reg])
        self.X = X
        if f32_inplace:
            xlnf = X
        xb = self.sb(f"ln_xb_{tag}", (128, Tc), BF16)
        sq = self.sb(f"ln_sq_{tag}", (128, Tc), BF16)
        tdiff = self.sb(f"ln_td_{tag}", (128, Tc), F32)
        rows = self.sb(f"ln_rows_{tag}", (1, 2 * Tc), F32)
        bounce = self.dscratch(f"lnb_{tag}", (2 * Tc,), F32)
        mu_b = self.sb(f"ln_mub_{tag}", (128, Tc), F32)
        rs_b = self.sb(f"ln_rsb_{tag}", (128, Tc), F32)

        nch = (Tc + 511) // 512
        assert nch <= 4
        B = [Bank(t) for t in self.pb]
        for f in range(NF):
            self.copy("v", xb, xb[:, :], self.X[f], self.X[f][:, :])
            self.tt("v", sq, sq[:, :], xb, xb[:, :], xb, xb[:, :], ALU.mult)
            for c in range(nch):
                sl = slice(512 * c, min(512 * (c + 1), Tc))
                w = sl.stop - sl.start
                self.mmx(B[2 * c], B[2 * c].h[0:1, 0:w], self.ones_col,
                         self.ones_col[:, 0:1], xb, xb[:, sl],
                         stop=(f == NF - 1))
                self.mmx(B[2 * c + 1], B[2 * c + 1].h[0:1, 0:w], self.ones_col,
                         self.ones_col[:, 0:1], sq, sq[:, sl],
                         stop=(f == NF - 1))
        for c in range(nch):
            sl = slice(512 * c, min(512 * (c + 1), Tc))
            w = sl.stop - sl.start
            self.evac(B[2 * c], rows, rows[0:1, sl], in_ap=B[2 * c].h[0:1, 0:w],
                      func=ACT.Copy, scale=1.0 / D)
            self.evac(B[2 * c + 1], rows, rows[0:1, Tc + sl.start:Tc + sl.stop],
                      in_ap=B[2 * c + 1].h[0:1, 0:w], func=ACT.Copy,
                      scale=1.0 / D)
        self.em.dma("y", bounce.h[:], rows[0:1, :], reads=[rows.reg],
                    writes=[bounce.reg])
        W = Tc // 128
        bounce2 = self.dscratch(f"lnb2_{tag}", (Tc,), F32)
        mu_t = self.sb(f"ln_mut_{tag}", (128, W), F32)
        ms_t = self.sb(f"ln_mst_{tag}", (128, W), F32)
        vt2 = self.sb(f"ln_vt2_{tag}", (128, 2 * W), F32)
        va_t, va_ap = vt2, vt2[:, 0:W]
        rs_t = self.sb(f"ln_rst_{tag}", (128, W), F32)
        self.em.dma("y", mu_t[:, :],
                    bounce.h[0:Tc].rearrange("(p w) -> p w", p=128),
                    reads=[bounce.reg], writes=[mu_t.reg])
        self.em.dma("y", ms_t[:, :],
                    bounce.h[Tc:2 * Tc].rearrange("(p w) -> p w", p=128),
                    reads=[bounce.reg], writes=[ms_t.reg])
        self.em.op("v", lambda v, va_ap=va_ap: v.scalar_tensor_tensor(
            va_ap, mu_t[:, :], 1.0, mu_t[:, :], ALU.mult, ALU.mult),
            reads=[mu_t.reg], writes=[va_t.reg])
        self.tt("v", va_t, va_ap, ms_t, ms_t[:, :], va_t, va_ap, ALU.subtract)
        self.em.op("v", lambda v, va_ap=va_ap: v.tensor_scalar(
            va_ap, va_ap, EPS, None, ALU.add),
            reads=[va_t.reg], writes=[va_t.reg])
        self.act("s", vt2, vt2[:, W:2 * W], va_t, va_ap, ACT.Sqrt)
        self.rsqrt_refine(rs_t, rs_t[:, :], vt2, vt2[:, W:2 * W], va_ap)
        self.em.dma("y", bounce2.h[0:Tc].rearrange("(p w) -> p w", p=128),
                    rs_t[:, :], reads=[rs_t.reg], writes=[bounce2.reg])
        self.em.dma("y", mu_b[:, :],
                    bounce.h[0:Tc].unsqueeze(0).partition_broadcast(128),
                    reads=[bounce.reg], writes=[mu_b.reg])
        self.em.dma("y", rs_b[:, :],
                    bounce2.h[0:Tc].unsqueeze(0).partition_broadcast(128),
                    reads=[bounce2.reg], writes=[rs_b.reg])
        for f in range(NF):
            self.tt("v", tdiff, tdiff[:, :], self.X[f], self.X[f][:, :],
                    mu_b, mu_b[:, :], ALU.subtract)
            if want_bf16:
                self.tt("v", xln[f], xln[f][:, :], tdiff, tdiff[:, :],
                        rs_b, rs_b[:, :], ALU.mult)
            if want_f32:
                self.tt("v", xlnf[f], xlnf[f][:, :], tdiff, tdiff[:, :],
                        rs_b, rs_b[:, :], ALU.mult)
        self.release()
        self.X = None
        return xln, xlnf

    def emit_store_x(self):
        xout = self.dout("y_fm", (D, self.Tc))
        self.em.dma("y", xout.h[:, :], self.xdram.h[:, :],
                    reads=[self.xdram.reg], writes=[xout.reg])

    def finish(self):
        self.em.final_waits("y")
        with self.nc.Block() as block:
            self.em.emit(block)
        return self.nc


def store_tiles(mk, name, tiles, rows=128):
    n = len(tiles)
    cols = tiles[0].h.shape[1]
    out = mk.dout(name, (n * rows, cols))
    for i, t in enumerate(tiles):
        mk.em.dma("g", out.h[rows * i:rows * (i + 1), :], t[:, :],
                  reads=[t.reg], writes=[out.reg])
    return out



class Bank:
    __slots__ = ("t", "started")

    def __init__(self, t):
        self.t = t
        self.started = set()

    @property
    def fresh(self):
        return not self.started

    @fresh.setter
    def fresh(self, val):
        if val:
            self.started = set()

    @property
    def h(self):
        return self.t.h

    @property
    def reg(self):
        return self.t.reg


def _mk_weight_precast(self, names_shapes):
    if not hasattr(self, "_wcast"):
        self._wcast = {}
    for name, (rows, cols) in names_shapes:
        ap = self.inputs[name]
        d = self.dscratch(f"wc_{name}", (rows, cols), BF16)
        self.em.dma("g", d.h[:, :], ap.h[:, :], reads=[ap.reg],
                    writes=[d.reg])
        self._wcast[name] = d


def _mk_load_w_bf16(self, name, rows, cols):
    wc = getattr(self, "_wcast", {}).get(name)
    ap = wc if wc is not None else self.inputs[name]
    q = "y" if wc is not None else "g"
    nt = (rows + 127) // 128
    tiles = self.sbs(f"w_{name}", (min(128, rows), cols), BF16, nt)
    for i in range(nt):
        self.em.dma(q, tiles[i][:, :], ap.h[128 * i:min(128 * (i + 1), rows), :],
                    reads=[ap.reg], writes=[tiles[i].reg])
    return tiles


def _mk_load_cols(self, name):
    ap = self.inputs[name]
    t = self.sb(f"c_{name}", (128, NF), F32)
    for f in range(NF):
        self.em.dma("g", t[:, f:f + 1], ap.h[128 * f:128 * (f + 1), :],
                    reads=[ap.reg], writes=[t.reg])
    return t


def _mk_mmx(self, bank, out_ap, lhsT_t, lhsT_ap, rhs_t, rhs_ap, stop=False,
            tp=None, pbase=0):
    start = pbase not in bank.started
    bank.started.add(pbase)
    self.em.op("t", lambda t: t.matmul(out_ap, lhsT_ap, rhs_ap, start=start,
                                       stop=stop, tile_position=tp,
                                       skip_group_check=True),
               reads=[lhsT_t.reg, rhs_t.reg], writes=[bank.reg], kind="mm")


def _mk_evac(self, bank, out_t, out_ap, in_ap=None, func=None, bias=0.0,
             scale=1.0, extra_reads=()):
    ia = in_ap if in_ap is not None else bank.h[:, :]
    if func is None:
        self.em.op("s", lambda s: s.copy(out_ap, ia),
                   reads=[bank.reg, *extra_reads], writes=[out_t.reg])
    else:
        self.em.op("s", lambda s: s.activation(out_ap, ia, func, bias=bias,
                                               scale=scale),
                   reads=[bank.reg, *extra_reads], writes=[out_t.reg])
    bank.fresh = True


MK.load_w_bf16 = _mk_load_w_bf16
MK.weight_precast = _mk_weight_precast
MK.load_cols = _mk_load_cols
MK.mmx = _mk_mmx
MK.evac = _mk_evac



def emit_l0a_sweep1(self, xln, upto="all", filler=None):
    em, Tc, NCH = self.em, self.Tc, self.NCH
    NG = NCH // 4
    NT = self.NT
    CCW = 512 + 8

    st = self.l0a = {}
    st["outacc"] = outacc = self.dscratch("outacc", (Tc, D), F32)
    st["cc_in"] = self.dscratch("cc_l0a_in", (128, CCW), F32)
    st["cc_out"] = self.dscratch("cc_l0a_out", (8, 128, CCW), F32)

    st["rt"] = rt = self.sbs("rt", (128, Tc), BF16, NF)
    st["eALt"] = eALt = self.sb("eALt", (128, NCH * 8), F32)
    st["Bcol"] = Bcol = [self.sb(f"Bcol{f}", (128, NCH), F32) for f in range(NF)]
    st["carry"] = carry = self.sb("carry", (128, NF), F32)
    st["S_cur"] = S_cur = self.sb("S_cur", (128, 512), F32)
    st["S_tmp"] = S_tmp = self.sb("S_tmp", (128, 512), F32)
    st["S_bf"] = S_bf = self.sb("S_bf", (128, 512), BF16)
    st["S_in"] = self.sb("S_in", (128, 512), F32)
    st["S_inb"] = self.sb("S_inb", (128, 512), BF16)
    st["P_run"] = self.sb("P_run", (128, 512), F32)
    st["Dg"] = self.sb("Dg", (128, 8), F32)

    em.op("v", lambda v: v.memset(S_cur[:, :], 0.0), writes=[S_cur.reg])
    em.op("v", lambda v: v.memset(S_bf[:, :], 0.0), writes=[S_bf.reg])
    em.op("v", lambda v: v.memset(carry[:, :], 0.0), writes=[carry.reg])

    self.mark()
    if upto == "spill":
        self.release()
        return
    wrT = self.load_w_bf16("wrT", D, D)
    wkT = self.load_w_bf16("wkT", D, D)
    wvT = self.load_w_bf16("wvT", D, D)
    ldT = self.load_w_bf16("ldT", D, LR)
    luT = self.load_w_bf16("luT", LR, D)
    u_t = self.load_cols("u_col")
    wb_t = self.load_cols("wbase_col")

    if upto == "weights":
        self.release()
        return
    spA = [self.sb(f"spA{f}", (128, 256), F32) for f in range(NF)]
    spB = [self.sb(f"spB{f}", (128, 256), F32) for f in range(NF)]
    Acum = [self.sb(f"Acum{f}", (128, 256), F32) for f in range(NF)]
    Ek = [self.sb(f"Ek{f}", (128, 256), F32) for f in range(NF)]
    Arel = spA
    Er = Acum
    bases = self.sb("bases", (128, 4 * NF), F32)
    r_g = [self.sb(f"r_g{f}", (128, 256), BF16) for f in range(NF)]
    k_g = [self.sb(f"k_g{f}", (128, 256), BF16) for f in range(NF)]
    k1_g = [self.sb(f"k1g{f}", (128, 256), BF16) for f in range(NF)]
    ku_g = [self.sb(f"kug{f}", (128, 256), BF16) for f in range(NF)]
    v_g = [self.sb(f"v_g{t}", (128, D), BF16) for t in range(2)]
    ktm_g = [self.sb(f"ktmg{t}", (128, D), BF16) for t in range(2)]
    z1 = self.sb("z1", (64, 256), BF16)
    attC2 = [self.sb(f"attC{i}", (128, D), BF16) for i in range(2)]
    mevA = [self.sb("mev0", (128, 512), F32)] * 2
    mevB = [self.sb("mev20", (128, 512), F32)] * 2
    o_ev2 = [self.sb("o_ev0", (128, D), F32)] * 2

    B = [Bank(t) for t in self.pb]
    BA0, BA1, B20, B21, BM, BO0, BO1, BTR = range(8)

    def r3(ap, c=4, l=64):
        return ap.rearrange("p (c l) -> p c l", c=c)

    for g in range(NG):
        gsl = slice(256 * g, 256 * (g + 1))
        for dst, wT in ((r_g, wrT), (k_g, wkT)):
            for o in range(NF):
                for k in range(NF):
                    self.mmx(B[BO0], B[BO0].h[:, 0:256], wT[k],
                             wT[k][:, 128 * o:128 * (o + 1)], xln[k],
                             xln[k][:, gsl], stop=(k == NF - 1))
                self.evac(B[BO0], dst[o], dst[o][:, :],
                          in_ap=B[BO0].h[:, 0:256])
        for ts in range(2):
            tsl = slice(256 * g + 128 * ts, 256 * g + 128 * (ts + 1))
            for n in range(2):
                nsl = slice(512 * n, 512 * (n + 1))
                for k in range(NF):
                    self.mmx(B[BO1], B[BO1].h[:, 0:512], xln[k], xln[k][:, tsl],
                             wvT[k], wvT[k][:, nsl], stop=(k == NF - 1))
                self.evac(B[BO1], v_g[ts], v_g[ts][:, nsl],
                          in_ap=B[BO1].h[:, 0:512])
        if upto == "proj":
            continue
        for k in range(NF):
            self.mmx(B[BM], B[BM].h[0:64, 0:256], ldT[k], ldT[k][:, :],
                     xln[k], xln[k][:, gsl], stop=(k == NF - 1))
        self.evac(B[BM], z1, z1[:, :], in_ap=B[BM].h[0:64, 0:256], func=ACT.Tanh)
        for o in range(NF):
            self.mmx(B[BTR], B[BTR].h[:, 0:256], luT[0],
                     luT[0][:, 128 * o:128 * (o + 1)], z1, z1[:, :], stop=True)
            self.evac(B[BTR], spA[o], spA[o][:, :], in_ap=B[BTR].h[:, 0:256],
                      func=ACT.Exp, bias=wb_t[:, o:o + 1],
                      extra_reads=[wb_t.reg])
            self.act("s", spB[o], spB[o][:, :], spA[o], spA[o][:, :], ACT.Ln,
                     bias=self.ccol(1.0), extra_reads=[self.ccol_reg])
        for f in range(NF):
            em.op("v", lambda v, f=f: v.tensor_tensor_scan(
                Acum[f][:, :], spB[f][:, :], spB[f][:, :], carry[:, f:f + 1],
                ALU.add, ALU.bypass),
                reads=[spB[f].reg, carry.reg], writes=[Acum[f].reg])
            bsl = slice(4 * f, 4 * f + 4)
            self.copy("v", bases, bases[:, 4 * f:4 * f + 1], carry,
                      carry[:, f:f + 1])
            em.op("v", lambda v, f=f: v.tensor_copy(
                bases[:, 4 * f + 1:4 * f + 4], r3(Acum[f][:, :])[:, 0:3, 63]),
                reads=[Acum[f].reg], writes=[bases.reg])
            self.copy("v", carry, carry[:, f:f + 1], Acum[f], Acum[f][:, 255:256])
            em.op("v", lambda v, f=f, bsl=bsl: v.tensor_tensor(
                spB[f][:, 0:4], r3(Acum[f][:, :])[:, :, 63], bases[:, bsl],
                ALU.subtract),
                reads=[Acum[f].reg, bases.reg], writes=[spB[f].reg])
            em.op("s", lambda s, f=f, g=g: s.activation(
                eALt[:, 8 * 4 * g:8 * 4 * (g + 1)].rearrange(
                    "p (c q) -> p c q", q=8)[:, :, f],
                spB[f][:, 0:4], ACT.Exp, scale=-1.0),
                reads=[spB[f].reg], writes=[eALt.reg])
            self.copy("v", Bcol[f], Bcol[f][:, 4 * g:4 * g + 4], bases,
                      bases[:, bsl])
            em.op("v", lambda v, f=f, bsl=bsl: v.tensor_tensor(
                r3(Arel[f][:, :]), r3(Acum[f][:, :]),
                bases[:, bsl].unsqueeze(2).broadcast_to([128, 4, 64]),
                ALU.subtract),
                reads=[Acum[f].reg, bases.reg], writes=[Arel[f].reg])
            self.act("s", Ek[f], Ek[f][:, :], Arel[f], Arel[f][:, :], ACT.Exp)
            self.act("s", Er[f], Er[f][:, :], Arel[f], Arel[f][:, :], ACT.Exp,
                     scale=-1.0)
        if upto == "scan":
            continue
        for f in range(NF):
            em.op("v", lambda v, f=f, gsl=gsl: v.tensor_tensor(
                r3(rt[f][:, gsl])[:, :, 1:64], r3(r_g[f][:, :])[:, :, 1:64],
                r3(Er[f][:, :])[:, :, 0:63], ALU.mult),
                reads=[r_g[f].reg, Er[f].reg], writes=[rt[f].reg])
            em.op("v", lambda v, f=f, gsl=gsl: v.tensor_copy(
                r3(rt[f][:, gsl])[:, :, 0], r3(r_g[f][:, :])[:, :, 0]),
                reads=[r_g[f].reg], writes=[rt[f].reg])
            self.tt("v", k1_g[f], k1_g[f][:, :], k_g[f], k_g[f][:, :],
                    Ek[f], Ek[f][:, :], ALU.mult)
            self.ts("v", ku_g[f], ku_g[f][:, :], k_g[f], k_g[f][:, :],
                    u_t[:, f:f + 1], ALU.mult, s_regs=[u_t.reg])
            em.op("v", lambda v, f=f: v.tensor_tensor(
                r3(ku_g[f][:, :])[:, :, 1:64], r3(ku_g[f][:, :])[:, :, 1:64],
                r3(Ek[f][:, :])[:, :, 0:63], ALU.mult),
                reads=[ku_g[f].reg, Ek[f].reg], writes=[ku_g[f].reg])
        if upto == "trans":
            continue
        for ts in range(2):
            for f in range(NF):
                pv = B[BTR].h[:, 0:64].bitcast(BF16)
                em.op("t", lambda t, f=f, ts=ts, pv=pv: t.transpose(
                    pv, k1_g[f][:, 128 * ts:128 * (ts + 1)], self.ident[:, :]),
                    reads=[k1_g[f].reg, self.ident.reg], writes=[B[BTR].reg])
                self.em.op("s", lambda s, f=f, ts=ts, pv=pv: s.copy(
                    ktm_g[ts][:, 128 * f:128 * (f + 1)], pv),
                    reads=[B[BTR].reg], writes=[ktm_g[ts].reg])
                B[BTR].fresh = True
        if upto == "ktm":
            continue
        for cl in range(4):
            c = 4 * g + cl
            cpar = c % 2
            attC, mev, mev2 = attC2[cpar], mevA[cpar], mevB[cpar]
            o_ev = o_ev2[(c // 2) % 2]
            ts = cl // 2
            csl = slice(64 * cl, 64 * (cl + 1))
            vrow = slice(64 * cpar, 64 * (cpar + 1))
            srow = slice(64 * (1 - cpar), 64 * (2 - cpar))
            for h in range(H):
                ft, par = h // 2, h % 2
                hb = slice(64 * par, 64 * (par + 1))
                ba = B[BA0 + (h // 8)]
                col = slice(64 * (h % 8), 64 * (h % 8 + 1))
                rt_ap = rt[ft][hb, 64 * c:64 * (c + 1)]
                if c > 0:
                    self.mmx(ba, ba.h[srow, col], S_bf,
                             S_bf[hb, 64 * ft:64 * ft + 64], rt[ft], rt_ap,
                             tp=(64 * par, 64 * (1 - cpar)), stop=True,
                             pbase=64 * (1 - cpar))
                self.mmx(ba, ba.h[vrow, col], k1_g[ft], k1_g[ft][hb, csl],
                         rt[ft], rt_ap, tp=(64 * par, 64 * cpar), stop=True,
                         pbase=64 * cpar)
                b2 = B[B20 + (h // 8)]
                self.mmx(b2, b2.h[vrow, col], ku_g[ft], ku_g[ft][hb, csl],
                         rt[ft], rt_ap, tp=(64 * par, 64 * cpar), stop=True,
                         pbase=64 * cpar)
            for b in range(2):
                ba, b2 = B[BA0 + b], B[B20 + b]
                hc = slice(512 * b, 512 * (b + 1))
                if c > 0:
                    self.evac(ba, attC, attC[srow, hc], in_ap=ba.h[srow, :])
                    ba.fresh = False
                else:
                    em.op("v", lambda v, hc=hc, srow=srow, attC=attC: v.memset(
                        attC[srow, hc], 0.0), writes=[attC.reg])
                em.op("v", lambda v, ba=ba, vrow=vrow, mev=mev: v.tensor_tensor(
                    r3(mev[vrow, 0:512], c=8), r3(ba.h[vrow, :], c=8),
                    self.mstrict[vrow, :].unsqueeze(1).broadcast_to([64, 8, 64]),
                    ALU.mult),
                    reads=[ba.reg, self.mstrict.reg], writes=[mev.reg])
                em.op("v", lambda v, b2=b2, vrow=vrow, mev2=mev2: v.tensor_tensor(
                    r3(mev2[vrow, 0:512], c=8), r3(b2.h[vrow, :], c=8),
                    self.mdiag[vrow, :].unsqueeze(1).broadcast_to([64, 8, 64]),
                    ALU.mult),
                    reads=[b2.reg, self.mdiag.reg], writes=[mev2.reg])
                em.op("v", lambda v, hc=hc, vrow=vrow, attC=attC, mev=mev,
                      mev2=mev2: v.tensor_add(
                    attC[vrow, hc], mev[vrow, 0:512], mev2[vrow, 0:512]),
                    reads=[mev.reg, mev2.reg], writes=[attC.reg])
                ba.fresh = True
                b2.fresh = True
            for h in range(H):
                ft, par = h // 2, h % 2
                ob = B[BO0 + (h // 8)]
                ocol = slice(64 * (h % 8), 64 * (h % 8 + 1))
                hcol = slice(64 * h, 64 * (h + 1))
                if c > 0:
                    self.mmx(ob, ob.h[vrow, ocol], attC, attC[srow, hcol],
                             self.ident, self.ident[srow, srow],
                             tp=(64 * (1 - cpar), 64 * cpar), pbase=64 * cpar)
                self.mmx(ob, ob.h[vrow, ocol], attC, attC[vrow, hcol],
                         v_g[ts], v_g[ts][vrow, 64 * h:64 * h + 64],
                         tp=(64 * cpar, 64 * cpar), stop=(cpar == 1),
                         pbase=64 * cpar)
            for h in range(H):
                ft, par = h // 2, h % 2
                self.mmx(B[BM], B[BM].h[64 * par:64 * (par + 1),
                                        64 * ft:64 * ft + 64],
                         ktm_g[ts], ktm_g[ts][vrow, 64 * h:64 * h + 64],
                         v_g[ts], v_g[ts][vrow, 64 * h:64 * h + 64],
                         tp=(64 * cpar, 64 * par), stop=True, pbase=64 * par)
            em.op("v", lambda v: v.tensor_add(S_tmp[:, :], S_cur[:, :],
                                              B[BM].h[:, :]),
                  reads=[S_cur.reg, B[BM].reg], writes=[S_tmp.reg])
            B[BM].fresh = True
            em.op("v", lambda v, c=c: v.tensor_tensor(
                S_cur[:, :].rearrange("p (q l) -> p q l", q=8),
                S_tmp[:, :].rearrange("p (q l) -> p q l", q=8),
                eALt[:, 8 * c:8 * c + 8].unsqueeze(2).broadcast_to([128, 8, 64]),
                ALU.mult),
                reads=[S_tmp.reg, eALt.reg], writes=[S_cur.reg])
            self.copy("v", S_bf, S_bf[:, :], S_cur, S_cur[:, :])
            if cpar == 1:
                tt_idx = c // 2
                for b in range(2):
                    ob = B[BO0 + b]
                    self.evac(ob, o_ev, o_ev[:, 512 * b:512 * (b + 1)])
                em.dma("y", outacc.h[128 * tt_idx:128 * (tt_idx + 1), :],
                       o_ev[:, :], reads=[o_ev.reg], writes=[outacc.reg])
                if filler is not None:
                    filler(-1)
        if filler is not None:
            filler(g)
    self.release()


def emit_l0a_collective(self):
    em = self.em
    st = self.l0a
    S_cur, carry = st["S_cur"], st["carry"]
    S_in, S_inb, P_run, Dg = st["S_in"], st["S_inb"], st["P_run"], st["Dg"]
    cc_in, cc_out = st["cc_in"], st["cc_out"]
    self.act("s", Dg, Dg[:, :], carry, carry[:, :], ACT.Exp, scale=-1.0)
    em.dma("y", cc_in.h[:, 0:512], S_cur[:, :], reads=[S_cur.reg],
           writes=[cc_in.reg])
    em.dma("y", cc_in.h[:, 512:520], Dg[:, :], reads=[Dg.reg],
           writes=[cc_in.reg])
    self.em.collective("AllGather", ALU.bypass, ins=[cc_in.h[:, :]],
                       outs=[cc_out.h[:, :, :]], reads=[cc_in.reg],
                       writes=[cc_out.reg])
    self.mark()
    Sr = self.sb("cc_Sr", (128, 512), F32)
    Dr = self.sb("cc_Dr", (128, 8), F32)
    tmp = self.sb("cc_tmp", (128, 512), F32)
    oh = self.inputs["ohprev_mat"]
    oh_t = self.sb("cc_oh", (128, 8), F32)
    em.dma("y", oh_t[:, :], oh.h[:, :], reads=[oh.reg], writes=[oh_t.reg])
    em.op("v", lambda v: v.memset(P_run[:, :], 0.0), writes=[P_run.reg])
    em.op("v", lambda v: v.memset(S_in[:, :], 0.0), writes=[S_in.reg])
    for r in range(8):
        em.dma("y", Sr[:, :], cc_out.h[r, :, 0:512], reads=[cc_out.reg],
               writes=[Sr.reg])
        em.dma("y", Dr[:, :], cc_out.h[r, :, 512:520], reads=[cc_out.reg],
               writes=[Dr.reg])
        self.ts("v", tmp, tmp[:, :], P_run, P_run[:, :], oh_t[:, r:r + 1],
                ALU.mult, s_regs=[oh_t.reg])
        self.tt("v", S_in, S_in[:, :], S_in, S_in[:, :], tmp, tmp[:, :], ALU.add)
        em.op("v", lambda v: v.tensor_tensor(
            tmp[:, :].rearrange("p (q l) -> p q l", q=8),
            P_run[:, :].rearrange("p (q l) -> p q l", q=8),
            Dr[:, :].unsqueeze(2).broadcast_to([128, 8, 64]), ALU.mult),
            reads=[P_run.reg, Dr.reg], writes=[tmp.reg])
        self.tt("v", P_run, P_run[:, :], tmp, tmp[:, :], Sr, Sr[:, :], ALU.add)
    self.copy("v", S_inb, S_inb[:, :], S_in, S_in[:, :])
    self.release()


def emit_l0a_sweep2(self, xln, upto='all'):
    em, Tc, NCH, NT = self.em, self.Tc, self.NCH, self.NT
    st = self.l0a
    rt, Bcol = st["rt"], st["Bcol"]
    S_inb = st["S_inb"]
    outacc = st["outacc"]
    xsrc = self.xdram
    xdst = self.xdram_next()

    self.mark()
    if upto == "p1":
        self.release()
        return
    wgT = self.load_w_bf16("wgT", D, D)
    woT = self.load_w_bf16("woT", D, D)
    if upto == "p2":
        self.release()
        return
    lng = self.inputs["lng_rep"]
    lnb = self.inputs["lnb_rep"]
    lng_t = self.sb("lng_t", (128, D), F32)
    lnb_t = self.sb("lnb_t", (128, D), F32)
    em.dma("y", lng_t[:, :], lng.h[:, :], reads=[lng.reg], writes=[lng_t.reg])
    em.dma("y", lnb_t[:, :], lnb.h[:, :], reads=[lnb.reg], writes=[lnb_t.reg])
    if upto == "p3":
        self.release()
        return
    ebcT = [self.sb(f"ebcT{f}", (128, NCH), F32) for f in range(NF)]
    for f in range(NF):
        self.act("s", ebcT[f], ebcT[f][:, :], Bcol[f], Bcol[f][:, :], ACT.Exp,
                 scale=-1.0)
    if upto == "p4":
        self.release()
        return
    rtl_t = self.sb("rtl_t", (128, NF * 128), BF16)
    xt_t = self.sb("xt_t", (128, NF * 128), F32)
    xu_t = self.sb("xu_t", (128, NF * 128), F32)
    if upto == "prep":
        self.release()
        return
    DBG = False
    if DBG:
        d_ofull = self.dscratch("d_ofull", (128, D), F32)
        d_lnout = self.dscratch("d_lnout", (128, D), F32)
        d_g = self.dscratch("d_g", (128, D), F32)
        d_sin = self.dscratch("d_sin", (128, 512), F32)
        d_stat = self.dscratch("d_stat", (128, 64), F32)
        d_rtil = self.dscratch("d_rtil", (128, D), F32)
        d_lngT = self.dscratch("d_lngT", (128, 8 * 128), F32)
        d_psum = self.dscratch("d_psum", (128, 512), F32)
        dbg_sb = self.sb("dbg_sb", (128, 512), F32)
    oacc_t = self.sb("oacc_t", (128, D), F32)
    ofull = self.sb("ofull", (128, D), F32)
    g_t = self.sb("g_sig", (128, D), F32)
    stat = self.sb("stat", (128, 64), F32)
    sqv = self.sb("sqv", (128, D), F32)
    lnout = self.sb("lnout", (128, D), BF16)
    lngT = self.sb("lngT", (128, D), BF16)

    B = [Bank(t) for t in self.pb]
    BOUT0, BOUT1, BG0, BG1, BTR, BP0, BP1, BXX = range(8)

    for tt in range(NT):
        tsl = slice(128 * tt, 128 * (tt + 1))
        em.dma("y", oacc_t[:, :], outacc.h[tsl, :], reads=[outacc.reg],
               writes=[oacc_t.reg])
        em.dma("y", xt_t[:, :],
               xsrc.h.rearrange("(f p) t -> p f t", p=128)[:, :, tsl],
               reads=[xsrc.reg], writes=[xt_t.reg])
        for f in range(NF):
            em.op("v", lambda v, f=f, tt=tt: v.tensor_tensor(
                rtl_t[:, 128 * f:128 * (f + 1)].rearrange(
                    "p (c l) -> p c l", l=64),
                rt[f][:, 128 * tt:128 * (tt + 1)].rearrange(
                    "p (c l) -> p c l", l=64),
                ebcT[f][:, 2 * tt:2 * tt + 2].unsqueeze(2).broadcast_to(
                    [128, 2, 64]), ALU.mult),
                reads=[rt[f].reg, ebcT[f].reg], writes=[rtl_t.reg])
        for cl in range(2):
            c = 2 * tt + cl
            vrow = slice(64 * (c % 2), 64 * (c % 2 + 1))
            for h in range(H):
                ft, par = h // 2, h % 2
                hb = slice(64 * par, 64 * (par + 1))
                ob = B[BOUT0 + (h // 8)]
                ocol = slice(64 * (h % 8), 64 * (h % 8 + 1))
                self.mmx(ob, ob.h[vrow, ocol], rtl_t,
                         rtl_t[hb, 128 * ft + 64 * cl:128 * ft + 64 * cl + 64],
                         S_inb, S_inb[hb, 64 * ft:64 * ft + 64],
                         tp=(64 * par, 64 * (c % 2)), stop=True,
                         pbase=64 * (c % 2))
        if upto == "d1b":
            continue
        for n in range(2):
            nsl = slice(512 * n, 512 * (n + 1))
            for k in range(NF):
                self.mmx(B[BG0 + n], B[BG0 + n].h[:, 0:512], xln[k],
                         xln[k][:, tsl], wgT[k], wgT[k][:, nsl],
                         stop=(k == NF - 1))
            self.evac(B[BG0 + n], g_t, g_t[:, nsl], func=ACT.Sigmoid,
                      in_ap=B[BG0 + n].h[:, 0:512])
        for b in range(2):
            ob = B[BOUT0 + b]
            em.op("v", lambda v, b=b, ob=ob: v.tensor_add(
                ofull[:, 512 * b:512 * (b + 1)], ob.h[:, :],
                oacc_t[:, 512 * b:512 * (b + 1)]),
                reads=[ob.reg, oacc_t.reg], writes=[ofull.reg])
            ob.fresh = True
        if upto == "g":
            continue
        if DBG and tt == 0:
            em.dma("g", d_ofull.h[:, :], ofull[:, :], reads=[ofull.reg],
                   writes=[d_ofull.reg])
            em.dma("g", d_g.h[:, :], g_t[:, :], reads=[g_t.reg],
                   writes=[d_g.reg])
            em.dma("g", d_sin.h[:, :], st["S_inb"][:, :], reads=[st["S_inb"].reg],
                   writes=[d_sin.reg])
            em.dma("g", d_rtil.h[:, 0:Tc], rtil[0][:, :], reads=[rtil[0].reg],
                   writes=[d_rtil.reg])
        em.op("v", lambda v: v.reduce_sum(
            stat[:, 0:16], ofull[:, :].rearrange("p (h j) -> p h j", j=64), AX),
            reads=[ofull.reg], writes=[stat.reg])
        self.tt("v", sqv, sqv[:, :], ofull, ofull[:, :], ofull, ofull[:, :],
                ALU.mult)
        em.op("v", lambda v: v.reduce_sum(
            stat[:, 16:32], sqv[:, :].rearrange("p (h j) -> p h j", j=64), AX),
            reads=[sqv.reg], writes=[stat.reg])
        self.ts("v", stat, stat[:, 0:16], stat, stat[:, 0:16], 1.0 / 64, ALU.mult)
        self.ts("v", stat, stat[:, 16:32], stat, stat[:, 16:32], 1.0 / 64,
                ALU.mult)
        em.op("v", lambda v: v.scalar_tensor_tensor(
            stat[:, 32:48], stat[:, 0:16], 1.0, stat[:, 0:16], ALU.mult,
            ALU.mult), reads=[stat.reg], writes=[stat.reg])
        self.tt("v", stat, stat[:, 16:32], stat, stat[:, 16:32], stat,
                stat[:, 32:48], ALU.subtract)
        self.ts("v", stat, stat[:, 16:32], stat, stat[:, 16:32], EPS, ALU.add)
        self.act("s", stat, stat[:, 32:48], stat, stat[:, 16:32], ACT.Sqrt)
        self.rsqrt_refine(stat, stat[:, 48:64], stat, stat[:, 32:48],
                          stat[:, 16:32])
        self.tt("v", oacc_t, oacc_t[:, :], lnb_t, lnb_t[:, :], g_t, g_t[:, :],
                ALU.mult)
        self.tt("v", g_t, g_t[:, :], lng_t, lng_t[:, :], g_t, g_t[:, :],
                ALU.mult)
        em.op("v", lambda v: v.tensor_tensor(
            sqv[:, :].rearrange("p (h j) -> p h j", j=64),
            ofull[:, :].rearrange("p (h j) -> p h j", j=64),
            stat[:, 0:16].unsqueeze(2).broadcast_to([128, 16, 64]),
            ALU.subtract), reads=[ofull.reg, stat.reg], writes=[sqv.reg])
        em.op("v", lambda v: v.tensor_tensor(
            sqv[:, :].rearrange("p (h j) -> p h j", j=64),
            sqv[:, :].rearrange("p (h j) -> p h j", j=64),
            stat[:, 48:64].unsqueeze(2).broadcast_to([128, 16, 64]),
            ALU.mult), reads=[sqv.reg, stat.reg], writes=[sqv.reg])
        em.op("v", lambda v: v.tensor_tensor(
            sqv[:, :], sqv[:, :], g_t[:, :], ALU.mult),
            reads=[sqv.reg, g_t.reg], writes=[sqv.reg])
        self.tt("v", lnout, lnout[:, :], sqv, sqv[:, :], oacc_t, oacc_t[:, :],
                ALU.add)
        if upto == "ln":
            continue
        if DBG and tt == 0:
            em.dma("g", d_lnout.h[:, :], lnout[:, :], reads=[lnout.reg],
                   writes=[d_lnout.reg])
            em.dma("g", d_stat.h[:, :], stat[:, :], reads=[stat.reg],
                   writes=[d_stat.reg])
        for k in range(NF):
            pv = B[BTR].h[:, 0:64].bitcast(BF16)
            em.op("t", lambda t, k=k, pv=pv: t.transpose(
                pv, lnout[:, 128 * k:128 * (k + 1)], self.ident[:, :]),
                reads=[lnout.reg, self.ident.reg], writes=[B[BTR].reg])
            self.em.op("s", lambda s, k=k, pv=pv: s.copy(
                lngT[:, 128 * k:128 * (k + 1)], pv),
                reads=[B[BTR].reg], writes=[lngT.reg])
        if upto == "trans":
            continue
        if DBG and tt == 0:
            em.dma("g", d_lngT.h[:, :], lngT[:, :], reads=[lngT.reg],
                   writes=[d_lngT.reg])
        for o in range(NF):
            pbk = B[BP0 + (o // 4)]
            ocol = slice(128 * (o % 4), 128 * (o % 4 + 1))
            for k in range(NF):
                self.mmx(pbk, pbk.h[:, ocol], woT[k],
                         woT[k][:, 128 * o:128 * (o + 1)], lngT,
                         lngT[:, 128 * k:128 * (k + 1)], stop=(k == NF - 1))
        if DBG and tt == 0:
            self.em.op("s", lambda s: s.copy(dbg_sb[:, :], B[BP0].h[:, :]),
                       reads=[B[BP0].reg], writes=[dbg_sb.reg])
            em.dma("g", d_psum.h[:, :], dbg_sb[:, :], reads=[dbg_sb.reg],
                   writes=[d_psum.reg])
        for b in range(2):
            pbk = B[BP0 + b]
            em.op("v", lambda v, b=b, pbk=pbk: v.tensor_tensor(
                xu_t[:, 512 * b:512 * (b + 1)], pbk.h[:, :],
                xt_t[:, 512 * b:512 * (b + 1)], ALU.add),
                reads=[pbk.reg, xt_t.reg], writes=[xu_t.reg])
        em.dma("y", xdst.h.rearrange("(f p) t -> p f t", p=128)[:, :, tsl],
               xu_t[:, :], reads=[xu_t.reg], writes=[xdst.reg])
        B[BP0].fresh = True
        B[BP1].fresh = True
    self.xdram = xdst
    self.release()


MK.emit_l0a_sweep1 = emit_l0a_sweep1
MK.emit_l0a_collective = emit_l0a_collective
MK.emit_l0a_sweep2 = emit_l0a_sweep2


QMAX = 32767.0
MAGIC = 12582912.0


def _gmax(self, tag, loc_col, loc_t, nvals, cc_sem_note=""):
    em = self.em
    b1 = self.dscratch(f"gm_b1_{tag}", (128,), F32)
    cin = self.dscratch(f"gm_ci_{tag}", (nvals,), F32)
    cout = self.dscratch(f"gm_co_{tag}", (8, nvals), F32)
    t2 = self.sb(f"gm_t2_{tag}", (128, 128), F32)
    res = self.sb(f"gm_res_{tag}", (128, max(nvals, 8)), F32)
    for v in range(nvals):
        em.dma("y", b1.h[:], loc_col[:, v:v + 1], reads=[loc_t.reg],
               writes=[b1.reg])
        em.dma("y", t2[:, :], b1.h[:].unsqueeze(0).partition_broadcast(128),
               reads=[b1.reg], writes=[t2.reg])
        em.op("v", lambda v_, v2=v: v_.reduce_max(res[:, v2:v2 + 1], t2[:, :],
                                                  AX),
              reads=[t2.reg], writes=[res.reg])
    em.dma("y", cin.h[:], res[0:1, 0:nvals], reads=[res.reg], writes=[cin.reg])
    self.em.collective("AllGather", ALU.bypass, ins=[cin.h[:]],
                       outs=[cout.h[:, :]], reads=[cin.reg], writes=[cout.reg])
    t3 = self.sb(f"gm_t3_{tag}", (128, 8 * nvals), F32)
    em.dma("y", t3[:, :],
           cout.h[:, :].flatten().unsqueeze(0).partition_broadcast(128),
           reads=[cout.reg], writes=[t3.reg])
    out = self.sb(f"gm_out_{tag}", (128, nvals), F32)
    em.op("v", lambda v_: v_.reduce_max(
        out[:, 0:nvals].unsqueeze(2),
        t3[:, :].rearrange("p (r v) -> p v r", v=nvals), AX),
        reads=[t3.reg], writes=[out.reg])
    return out


def _qdq_scales(self, sc, c0, gmax_t, col, qmax=QMAX):
    self.em.op("v", lambda v: v.tensor_scalar(
        sc[:, c0:c0 + 1], col, 1.0 / qmax, 1e-12, ALU.mult, ALU.max),
        reads=[gmax_t.reg], writes=[sc.reg])
    self.em.op("v", lambda v: v.reciprocal(sc[:, c0 + 1:c0 + 2],
                                           sc[:, c0:c0 + 1]),
               reads=[sc.reg], writes=[sc.reg])


DQW = {"g": ("wgateN", "wgateT"), "u": ("wupN", "wupT"),
       "d": ("wdownN", "wdownT")}


def emit_dq_scales(self, keys=("g", "u", "d")):
    em = self.em
    if not hasattr(self, "dqctx"):
        self.dqg_d = self.dscratch("dqm_g", (D, FI), BF16)
        self.dqu_d = self.dscratch("dqm_u", (D, FI), BF16)
        self.dqd_d = self.dscratch("dqm_d", (FI, D), BF16)
        self.dqctx = {}
    bank = Bank(self.pb[7])
    self.mark()
    shapes = {"g": (self.dqg_d, FI, D), "u": (self.dqu_d, FI, D),
              "d": (self.dqd_d, D, FI)}
    IMX = max(shapes[k][2] for k in keys)
    wb2 = [self.sb(f"dqs_w{i}_{keys[0]}", (128, IMX), F32) for i in range(2)]
    mbuf = self.sb(f"dqs_m_{keys[0]}", (128, IMX // 32), F32)
    sall = self.sb(f"dqs_s_{keys[0]}", (128, 768), F32)
    s2all = self.sb(f"dqs_s2_{keys[0]}", (128, 768), F32)
    sb_ev = self.sb(f"dqs_ev_{keys[0]}", (128, 128), F32)
    for wkey in keys:
        wnat_n, wT_n = DQW[wkey]
        out_d, O, I = shapes[wkey]
        nat = self.inputs[wnat_n]
        nb = I // 32
        not_ = O // 128
        invT_d = self.dscratch(f"dqs_iT_{wkey}", (nb, O), F32)
        sT_d = self.dscratch(f"dqs_sT_{wkey}", (nb, O), F32)
        self.dqctx[wkey] = (self.inputs[wT_n], invT_d, sT_d, out_d, O, I)
        for ot in range(not_):
            rows = slice(128 * ot, 128 * (ot + 1))
            wbuf = wb2[ot % 2]
            em.dma("y", wbuf[:, 0:I], nat.h[rows, :], reads=[nat.reg],
                   writes=[wbuf.reg])
            em.op("v", lambda v, nb=nb, I=I, wbuf=wbuf: v.reduce_max(
                mbuf[:, 0:nb],
                wbuf[:, 0:I].rearrange("p (b q) -> p b q", q=32), AX,
                apply_absolute_value=True),
                reads=[wbuf.reg], writes=[mbuf.reg])
            em.op("v", lambda v, ot=ot, nb=nb: v.tensor_scalar(
                s2all[:, nb * ot:nb * (ot + 1)], mbuf[:, 0:nb], 1.0 / 7.0,
                1e-12, ALU.mult, ALU.max),
                reads=[mbuf.reg], writes=[s2all.reg])
            em.op("v", lambda v, ot=ot, nb=nb: v.reciprocal(
                sall[:, nb * ot:nb * (ot + 1)],
                s2all[:, nb * ot:nb * (ot + 1)]),
                reads=[s2all.reg], writes=[sall.reg])
        for buf, dst in ((sall, invT_d), (s2all, sT_d)):
            for ot in range(not_):
                rows = slice(128 * ot, 128 * (ot + 1))
                em.op("t", lambda t, buf=buf, ot=ot, nb=nb: t.transpose(
                    bank.h[0:nb, 0:128], buf[0:128, nb * ot:nb * (ot + 1)],
                    self.identf[:, :]),
                    reads=[buf.reg, self.identf.reg], writes=[bank.reg])
                em.op("s", lambda s, nb=nb: s.copy(sb_ev[0:nb, 0:128],
                                                   bank.h[0:nb, 0:128]),
                      reads=[bank.reg], writes=[sb_ev.reg])
                bank.fresh = True
                em.dma("y", dst.h[:, rows], sb_ev[0:nb, 0:128],
                       reads=[sb_ev.reg], writes=[dst.reg])
    self.release()


def emit_dq_piece_bufs(self):
    OCW = 512
    self.dqpb = (self.sb("dqp_wt", (128, OCW), F32),
                 self.sb("dqp_iv", (128, OCW), F32),
                 self.sb("dqp_sb", (128, OCW), F32),
                 self.sb("dqp_ob", (128, OCW), BF16), OCW)


def emit_dq_wt_pieces(self, wkey):
    em = self.em
    wT, invT_d, sT_d, out_d, O, I = self.dqctx[wkey]
    wtb, invb, sbb, ob, OCW = self.dqpb
    nkt = I // 128
    noc = O // OCW
    nbk = 128 // 32
    pieces = []

    def mk(k, oc):
        def go():
            rows = slice(128 * k, 128 * (k + 1))
            ocs = slice(OCW * oc, OCW * (oc + 1))
            em.dma("y", wtb[:, :], wT.h[rows, ocs], reads=[wT.reg],
                   writes=[wtb.reg])
            em.dma("y", invb[:, :],
                   invT_d.h[nbk * k:nbk * (k + 1), ocs].unsqueeze(1)
                   .broadcast_to([nbk, 32, OCW]),
                   reads=[invT_d.reg], writes=[invb.reg])
            em.dma("y", sbb[:, :],
                   sT_d.h[nbk * k:nbk * (k + 1), ocs].unsqueeze(1)
                   .broadcast_to([nbk, 32, OCW]),
                   reads=[sT_d.reg], writes=[sbb.reg])
            em.op("v", lambda v: v.tensor_mul(wtb[:, :], wtb[:, :],
                                              invb[:, :]),
                  reads=[wtb.reg, invb.reg], writes=[wtb.reg])
            em.op("v", lambda v: v.tensor_scalar(
                invb[:, :], wtb[:, :], MAGIC, MAGIC, ALU.add, ALU.subtract),
                reads=[wtb.reg], writes=[invb.reg])
            em.op("v", lambda v: v.tensor_tensor(
                ob[:, :], invb[:, :], sbb[:, :], ALU.mult),
                reads=[invb.reg, sbb.reg], writes=[ob.reg])
            em.dma("y", out_d.h[rows, ocs], ob[:, :], reads=[ob.reg],
                   writes=[out_d.reg])
        return go

    for k in range(nkt):
        for oc in range(noc):
            pieces.append(mk(k, oc))
    return pieces


def emit_l0f(self, xq, wt_pieces=None):
    em, Tc, NT = self.em, self.Tc, self.NT
    B = [Bank(t) for t in self.pb]
    gate_d = self.dscratch("gate_raw", (FI, Tc), F32)
    up_d = self.dscratch("up_raw", (FI, Tc), F32)
    g2_d = self.dscratch("g2_raw", (FI, Tc), F32)
    o_d = self.dscratch("o_raw", (FI, Tc), F32)
    xdst = self.xdram_next()
    xsrc = self.xdram

    NO = FI // 128
    nch = (Tc + 511) // 512
    CW = min(512, Tc)
    SC = self.sb("l0f_scales", (128, 16), F32)
    sc_g, sc_u, sc_g2, sc_o = SC[:, 0:2], SC[:, 2:4], SC[:, 4:6], SC[:, 6:8]

    self.mark()
    dqT = self.sbs("dq_gT", (128, FI), BF16, NF)
    locmax = self.sb("l0f_locmax", (128, 4), F32)
    em.op("v", lambda v: v.memset(locmax[:, :], 0.0), writes=[locmax.reg])
    ev2 = [self.sb(f"l0f_ev{i}", (128, Tc), F32) for i in range(2)]
    mcol = self.sb("l0f_mcol", (128, NO), F32)
    mcol2 = self.sb("l0f_mcol2", (128, NO), F32)
    t_f2 = [self.sb(f"l0f_tf{i}", (128, Tc), F32) for i in range(2)]
    kg2 = [self.sb(f"l0f_kg{i}", (128, Tc), F32) for i in range(2)]
    ms2 = [self.sb(f"l0f_ms{i}", (128, Tc), F32) for i in range(2)]
    sigf2 = [self.sb(f"l0f_sig{i}", (128, Tc), F32) for i in range(2)]

    def sig_pass_tile(o):
        t_f, kg, ms, sigf = t_f2[o % 2], kg2[o % 2], ms2[o % 2], sigf2[o % 2]
        rows = slice(128 * o, 128 * (o + 1))
        em.dma("s", sigf[:, :], gate_d.h[rows, :], reads=[gate_d.reg],
               writes=[sigf.reg])
        self.ts("v", t_f, t_f[:, :], sigf, sigf[:, :], SC[:, 1:2], ALU.mult,
                s_regs=[SC.reg])
        self.ts("v", kg, kg[:, :], t_f, t_f[:, :], MAGIC, ALU.add, MAGIC,
                ALU.subtract)
        em.op("s", lambda s, sigf=sigf, kg=kg: s.activation(
            sigf[:, :], kg[:, :], ACT.Sigmoid, scale=SC[:, 0:1]),
              reads=[kg.reg, SC.reg], writes=[sigf.reg])
        self.act("s", t_f, t_f[:, :], sigf, sigf[:, :], ACT.Relu,
                 scale=65536.0, bias=self.ccol(MAGIC),
                 extra_reads=[self.ccol_reg])
        self.act("s", ms, ms[:, :], t_f, t_f[:, :], ACT.Relu,
                 bias=self.ccol(-MAGIC), extra_reads=[self.ccol_reg])
        self.ts("v", t_f, t_f[:, :], ms, ms[:, :], 65535.0, ALU.min)
        em.op("v", lambda v, ms=ms, t_f=t_f, kg=kg: v.scalar_tensor_tensor(
            ms[:, :], t_f[:, :], SC[:, 8:9], kg[:, :], ALU.mult, ALU.mult),
            reads=[t_f.reg, SC.reg, kg.reg], writes=[ms.reg])
        em.op("v", lambda v, o=o, ms=ms: v.reduce_max(
            mcol2[:, o:o + 1], ms[:, :], AX, apply_absolute_value=True),
            reads=[ms.reg], writes=[mcol2.reg])
        em.dma("y", g2_d.h[rows, :], ms[:, :], reads=[ms.reg],
               writes=[g2_d.reg])
    up_list, dn_list = wt_pieces if wt_pieces is not None else ([], [])
    for wi, (dq_d, dst_d, mc) in enumerate(
            ((self.dqg_d, gate_d, 0), (self.dqu_d, up_d, 1))):
        if wi == 1:
            while up_list:
                up_list.pop(0)()
            gm2 = _gmax(self, "q2", locmax[:, 0:1], locmax, 1)
            _qdq_scales(self, SC, 0, gm2, gm2[:, 0:1])
            self.em.op("v", lambda v: v.tensor_scalar(
                SC[:, 8:9], SC[:, 0:1], 1.0 / 65536.0, None, ALU.mult),
                reads=[SC.reg], writes=[SC.reg])
        for k in range(NF):
            em.dma("y", dqT[k][:, :], dq_d.h[128 * k:128 * (k + 1), :],
                   reads=[dq_d.reg], writes=[dqT[k].reg])
        drip = up_list if wi == 0 else dn_list
        for o in range(NO):
            ev = ev2[o % 2]
            for c in range(nch):
                tsl = slice(CW * c, CW * (c + 1))
                bk = B[c % 2]
                for k in range(NF):
                    self.mmx(bk, bk.h[:, 0:CW], dqT[k],
                             dqT[k][:, 128 * o:128 * (o + 1)], xq[k],
                             xq[k][:, tsl], stop=(k == NF - 1))
                self.evac(bk, ev, ev[:, tsl], in_ap=bk.h[:, 0:CW])
            em.op("v", lambda v, o=o, ev=ev: v.reduce_max(
                mcol[:, o:o + 1], ev[:, :], AX, apply_absolute_value=True),
                reads=[ev.reg], writes=[mcol.reg])
            em.dma("y", dst_d.h[128 * o:128 * (o + 1), :], ev[:, :],
                   reads=[ev.reg], writes=[dst_d.reg])
            for _ in range(2):
                if drip:
                    drip.pop(0)()
            if wi == 1:
                sig_pass_tile(o)
        em.op("v", lambda v, mc=mc: v.reduce_max(
            locmax[:, mc:mc + 1], mcol[:, :], AX),
            reads=[mcol.reg], writes=[locmax.reg])
        if wi == 0:
            pass
    em.op("v", lambda v: v.reduce_max(locmax[:, 2:3], mcol2[:, :], AX),
          reads=[mcol2.reg], writes=[locmax.reg])
    gm34 = _gmax(self, "q34", locmax[:, 1:3], locmax, 2)
    _qdq_scales(self, SC, 2, gm34, gm34[:, 0:1])
    _qdq_scales(self, SC, 4, gm34, gm34[:, 1:2])
    em.op("v", lambda v: v.tensor_tensor(SC[:, 9:10], SC[:, 4:5],
                                         SC[:, 2:3], ALU.mult),
          reads=[SC.reg], writes=[SC.reg])

    em.op("v", lambda v: v.memset(locmax[:, 0:1], 0.0), writes=[locmax.reg])
    for o in range(NO):
        ev, t_f, kg, ms = ev2[o % 2], t_f2[o % 2], kg2[o % 2], ms2[o % 2]
        rows = slice(128 * o, 128 * (o + 1))
        em.dma("s", ev[:, :], g2_d.h[rows, :], reads=[g2_d.reg], writes=[ev.reg])
        self.act("s", t_f, t_f[:, :], ev, ev[:, :], ACT.Relu,
                 scale=SC[:, 5:6], bias=self.ccol(MAGIC),
                 extra_reads=[SC.reg, self.ccol_reg])
        self.ts("v", kg, kg[:, :], t_f, t_f[:, :], -MAGIC, ALU.add)
        em.dma("s", ev[:, :], up_d.h[rows, :], reads=[up_d.reg], writes=[ev.reg])
        self.ts("v", t_f, t_f[:, :], ev, ev[:, :], SC[:, 3:4], ALU.mult,
                s_regs=[SC.reg])
        self.ts("v", ms, ms[:, :], t_f, t_f[:, :], MAGIC, ALU.add, MAGIC,
                ALU.subtract)
        em.op("v", lambda v, ev=ev, kg=kg, ms=ms: v.scalar_tensor_tensor(
            ev[:, :], kg[:, :], SC[:, 9:10], ms[:, :], ALU.mult, ALU.mult),
            reads=[kg.reg, SC.reg, ms.reg], writes=[ev.reg])
        em.op("v", lambda v, o=o, ev=ev: v.reduce_max(
            mcol[:, o:o + 1], ev[:, :], AX, apply_absolute_value=True),
            reads=[ev.reg], writes=[mcol.reg])
        em.dma("y", o_d.h[rows, :], ev[:, :], reads=[ev.reg], writes=[o_d.reg])
    em.op("v", lambda v: v.reduce_max(locmax[:, 0:1], mcol[:, :], AX),
          reads=[mcol.reg], writes=[locmax.reg])
    gm5 = _gmax(self, "q5", locmax[:, 0:1], locmax, 1)
    _qdq_scales(self, SC, 6, gm5, gm5[:, 0:1])
    self.release()

    self.mark()
    while dn_list:
        dn_list.pop(0)()
    dqdT = self.sbs("dq_dT", (128, D), BF16, FI // 128)
    for k in range(FI // 128):
        em.dma("y", dqdT[k][:, :], self.dqd_d.h[128 * k:128 * (k + 1), :],
               reads=[self.dqd_d.reg], writes=[dqdT[k].reg])
    oq = self.sbs("l0f_oq", (128, CW), BF16, FI // 128)
    tq2 = [self.sb(f"l0f_tq{i}", (128, CW), F32) for i in range(2)]
    q22 = [self.sb(f"l0f_q2{i}", (128, CW), F32) for i in range(2)]
    xt2 = [self.sb(f"l0f_xt{i}", (128, NF * CW), F32) for i in range(2)]
    xu2 = [self.sb(f"l0f_xu{i}", (128, NF * CW), F32) for i in range(2)]
    for c in range(nch):
        tsl = slice(CW * c, CW * (c + 1))
        xt, xu = xt2[c % 2], xu2[c % 2]
        for k in range(FI // 128):
            tq, q2 = tq2[k % 2], q22[k % 2]
            em.dma("s", tq[:, :], o_d.h[128 * k:128 * (k + 1), tsl],
                   reads=[o_d.reg], writes=[tq.reg])
            self.ts("v", tq, tq[:, :], tq, tq[:, :], SC[:, 7:8], ALU.mult,
                    s_regs=[SC.reg])
            self.ts("v", q2, q2[:, :], tq, tq[:, :], MAGIC, ALU.add, MAGIC,
                    ALU.subtract)
            self.ts("v", oq[k], oq[k][:, :], q2, q2[:, :], SC[:, 6:7],
                    ALU.mult, s_regs=[SC.reg])
        em.dma("y", xt[:, :],
               xsrc.h.rearrange("(f p) t -> p f t", p=128)[:, :, tsl],
               reads=[xsrc.reg], writes=[xt.reg])
        for o2 in range(NF):
            bk = B[o2 % 2]
            for k in range(FI // 128):
                self.mmx(bk, bk.h[:, 0:CW], dqdT[k],
                         dqdT[k][:, 128 * o2:128 * (o2 + 1)], oq[k],
                         oq[k][:, :], stop=(k == FI // 128 - 1))
            em.op("v", lambda v, o2=o2, bk=bk, xu=xu, xt=xt: v.tensor_tensor(
                xu[:, CW * o2:CW * (o2 + 1)], bk.h[:, 0:CW],
                xt[:, CW * o2:CW * (o2 + 1)], ALU.add),
                reads=[bk.reg, xt.reg], writes=[xu.reg])
            bk.fresh = True
        em.dma("y",
               xdst.h.rearrange("(f p) t -> p f t", p=128)[:, :, tsl],
               xu[:, :], reads=[xu.reg], writes=[xdst.reg])
    self.xdram = xdst
    self.release()


def emit_q1(self):
    em, Tc = self.em, self.Tc
    xq = self.sbs("xq", (128, Tc), BF16, NF)
    self.mark()
    xlnf = self.sbs("q1_x", (128, Tc), F32, NF)
    self.emit_ln("2", xln=None, want_f32=True, want_bf16=False,
                 X_pre=xlnf, f32_inplace=True)
    locmax = self.sb("q1_locmax", (128, NF), F32)
    for f in range(NF):
        em.op("v", lambda v, f=f: v.reduce_max(
            locmax[:, f:f + 1], xlnf[f][:, :], AX, apply_absolute_value=True),
            reads=[xlnf[f].reg], writes=[locmax.reg])
    lm2 = self.sb("q1_lm2", (128, 1), F32)
    em.op("v", lambda v: v.reduce_max(lm2[:, :], locmax[:, :], AX),
          reads=[locmax.reg], writes=[lm2.reg])
    gm = _gmax(self, "q1", lm2[:, 0:1], lm2, 1)
    sc = self.sb("q1_sc", (128, 2), F32)
    _qdq_scales(self, sc, 0, gm, gm[:, 0:1])
    ti = self.sb("q1_ti", (128, Tc), F32)
    tf = self.sb("q1_tf", (128, Tc), F32)
    for f in range(NF):
        self.ts("v", tf, tf[:, :], xlnf[f], xlnf[f][:, :], sc[:, 1:2],
                ALU.mult, s_regs=[sc.reg])
        self.ts("v", ti, ti[:, :], tf, tf[:, :], MAGIC, ALU.add, MAGIC,
                ALU.subtract)
        self.ts("v", xq[f], xq[f][:, :], ti, ti[:, :], sc[:, 0:1], ALU.mult,
                s_regs=[sc.reg])
    self.release()
    return xq


MK.emit_l0f = emit_l0f
MK.emit_q1 = emit_q1
MK.emit_dq_scales = emit_dq_scales
MK.emit_dq_piece_bufs = emit_dq_piece_bufs
MK.emit_dq_wt_pieces = emit_dq_wt_pieces


INV2PI = 1.0 / (2.0 * np.pi)
TWO_PI = 2.0 * np.pi
MAGIC = 12582912.0


def emit_l1a(self, xln):
    em, Tc, NT = self.em, self.Tc, self.NT
    B = [Bank(t) for t in self.pb]
    nch = (Tc + 511) // 512
    CW = min(512, Tc)
    xsrc = self.xdram
    xdst = self.xdram_next()
    cc_in = self.dscratch("cc_l1a_in", (128, 16 * 65), F32)
    cc_out = self.dscratch("cc_l1a_out", (128, 16 * 65), F32)
    zd = self.dscratch("l1a_zd", (H, Tc), F32)
    zdi = self.dscratch("l1a_zdi", (H, Tc), F32)

    phb = self.sb("l1a_phb", (128, 1), F32)
    em.op("g", lambda g: g.memset(phb[0:64, :], 0.25), writes=[phb.reg])
    em.op("g", lambda g: g.memset(phb[64:128, :], 0.0), writes=[phb.reg])
    pc = self.sb("l1a_pc", (128, 128), F32)
    em.op("g", lambda g: g.memset(pc[:, 0:64], 0.25), writes=[pc.reg])
    em.op("g", lambda g: g.memset(pc[:, 64:128], 0.0), writes=[pc.reg])
    rf2 = self.load_w_bf16("rf2", 128, 128)[0]
    kvs = self.sb("l1a_kvs", (128, H * 65), F32)
    kvb = self.sb("l1a_kvb", (128, H * 65), BF16)
    attn = self.sbs("l1a_attn", (128, Tc), BF16, NF)
    onesb = self.ones_col

    self.mark()
    Qp = self.sbs("l1a_qp", (128, Tc), BF16, H)
    self.mark()
    V_tm = self.sbs("l1a_v", (128, D), BF16, NT)
    self.mark()
    wv = self.load_w_bf16("l1_Wv", D, D)
    for tt in range(NT):
        tsl = slice(128 * tt, 128 * (tt + 1))
        for n in range(2):
            nsl = slice(512 * n, 512 * (n + 1))
            bk = B[6]
            for k in range(NF):
                self.mmx(bk, bk.h[:, 0:512], xln[k], xln[k][:, tsl],
                         wv[k], wv[k][:, nsl], stop=(k == NF - 1))
            self.evac(bk, V_tm[tt], V_tm[tt][:, nsl], in_ap=bk.h[:, 0:512])
    self.release()
    wk = self.load_w_bf16("l1_Wk", D, D)
    Kpair2 = [self.sb(f"l1a_kpair{i}", (128, Tc), BF16) for i in range(2)]
    QN = 4 if NT % 4 == 0 else 2
    PW = Tc // QN
    u1 = self.sb("l1a_u1", (128, PW), F32)
    u22 = [self.sb(f"l1a_u2{i}", (128, PW), F32) for i in range(2)]
    kp_h2 = [self.sb(f"l1a_kph{i}", (128, Tc), BF16) for i in range(2)]
    for hp in range(H // 2):
        Kpair = Kpair2[hp % 2]
        jsl = slice(128 * hp, 128 * (hp + 1))
        for c in range(nch):
            tsl = slice(CW * c, CW * (c + 1))
            bk = B[6]
            for k in range(NF):
                self.mmx(bk, bk.h[:, 0:CW], wk[k], wk[k][:, jsl],
                         xln[k], xln[k][:, tsl], stop=(k == NF - 1))
            self.evac(bk, Kpair, Kpair[:, tsl], in_ap=bk.h[:, 0:CW])
        for par in range(2):
            h = 2 * hp + par
            hb = slice(64 * par, 64 * (par + 1))
            kp_h = kp_h2[par]
            for q in range(QN):
                u2 = u22[q % 2]
                hsl = slice(PW * q, PW * (q + 1))
                for tl in range(NT // QN):
                    tt = q * (NT // QN) + tl
                    tsl = slice(128 * tt, 128 * (tt + 1))
                    bk = B[tl % 2]
                    self.mmx(bk, bk.h[:, 0:128], Kpair, Kpair[hb, tsl],
                             rf2, rf2[hb, :], stop=True, tp=(64 * par, 0))
                    em.op("v", lambda v, tl=tl, bk=bk, u1=u1:
                          v.scalar_tensor_tensor(
                        u1[:, 128 * tl:128 * (tl + 1)], bk.h[:, 0:128], INV2PI,
                        pc[:, :], ALU.mult, ALU.add),
                        reads=[bk.reg, pc.reg], writes=[u1.reg])
                    bk.fresh = True
                em.op("v", lambda v, u1=u1, u2=u2: v.tensor_scalar(
                    u2[:, :], u1[:, :], MAGIC, MAGIC, ALU.add, ALU.subtract),
                    reads=[u1.reg], writes=[u2.reg])
                em.op("v", lambda v, u1=u1, u2=u2: v.tensor_sub(
                    u2[:, :], u1[:, :], u2[:, :]),
                    reads=[u1.reg, u2.reg], writes=[u2.reg])
                em.op("s", lambda s, hsl=hsl, kp_h=kp_h, u2=u2: s.activation(
                    kp_h[:, hsl], u2[:, :], ACT.Sin, scale=TWO_PI),
                    reads=[u2.reg], writes=[kp_h.reg])
            bkv = B[2 + par]
            for tt in range(NT):
                tsl = slice(128 * tt, 128 * (tt + 1))
                self.mmx(bkv, bkv.h[:, 0:64], kp_h, kp_h[:, tsl],
                         V_tm[tt], V_tm[tt][:, 64 * h:64 * (h + 1)],
                         stop=(tt == NT - 1))
                self.mmx(bkv, bkv.h[:, 64:65], kp_h, kp_h[:, tsl],
                         onesb, onesb[:, 0:1], stop=(tt == NT - 1))
            self.evac(bkv, kvs, kvs[:, 65 * h:65 * (h + 1)],
                      in_ap=bkv.h[:, 0:65])
    self.release()

    em.dma("y", cc_in.h[:, :], kvs[:, :], reads=[kvs.reg], writes=[cc_in.reg])
    self.em.collective("AllReduce", ALU.add, ins=[cc_in.h[:, :]],
                       outs=[cc_out.h[:, :]], reads=[cc_in.reg],
                       writes=[cc_out.reg])
    em.dma("y", kvs[:, :], cc_out.h[:, :], reads=[cc_out.reg],
           writes=[kvs.reg])
    self.copy("v", kvb, kvb[:, :], kvs, kvs[:, :])

    self.mark()
    wq = self.load_w_bf16("l1_Wq", D, D)
    Qpair2 = [self.sb(f"l1a_qpair{i}", (128, Tc), BF16) for i in range(2)]
    HT = Tc // 2
    u1q = self.sb("l1a_u1q", (128, HT), F32)
    u2q2 = [self.sb(f"l1a_u2q{i}", (128, HT), F32) for i in range(2)]
    for hp in range(H // 2):
        Qpair = Qpair2[hp % 2]
        jsl = slice(128 * hp, 128 * (hp + 1))
        for c in range(nch):
            tsl = slice(CW * c, CW * (c + 1))
            bk = B[6]
            for k in range(NF):
                self.mmx(bk, bk.h[:, 0:CW], wq[k], wq[k][:, jsl],
                         xln[k], xln[k][:, tsl], stop=(k == NF - 1))
            self.evac(bk, Qpair, Qpair[:, tsl], in_ap=bk.h[:, 0:CW])
        for par in range(2):
            h = 2 * hp + par
            hb = slice(64 * par, 64 * (par + 1))
            for half in range(2):
                u2q = u2q2[half]
                hsl = slice(HT * half, HT * (half + 1))
                ncw = max(1, HT // CW)
                for cl in range(ncw):
                    w = min(CW, HT)
                    tsl = slice(HT * half + w * cl, HT * half + w * (cl + 1))
                    bk = B[cl % 2]
                    self.mmx(bk, bk.h[:, 0:w], rf2, rf2[hb, :],
                             Qpair, Qpair[hb, tsl], stop=True, tp=(64 * par, 0))
                    em.op("v", lambda v, w=w, cl=cl, bk=bk, u1q=u1q:
                          v.tensor_scalar(
                        u1q[:, w * cl:w * (cl + 1)], bk.h[:, 0:w], INV2PI,
                        phb[:, 0:1], ALU.mult, ALU.add),
                        reads=[bk.reg, phb.reg], writes=[u1q.reg])
                    bk.fresh = True
                em.op("v", lambda v, u1q=u1q, u2q=u2q: v.tensor_scalar(
                    u2q[:, :], u1q[:, :], MAGIC, MAGIC, ALU.add, ALU.subtract),
                    reads=[u1q.reg], writes=[u2q.reg])
                em.op("v", lambda v, u1q=u1q, u2q=u2q: v.tensor_sub(
                    u2q[:, :], u1q[:, :], u2q[:, :]),
                    reads=[u1q.reg, u2q.reg], writes=[u2q.reg])
                em.op("s", lambda s, h=h, hsl=hsl, u2q=u2q: s.activation(
                    Qp[h][:, hsl], u2q[:, :], ACT.Sin, scale=TWO_PI),
                    reads=[u2q.reg], writes=[Qp[h].reg])
    self.release()

    self.mark()
    zrow2 = [self.sb(f"l1a_zrow{i}", (1, Tc), F32) for i in range(2)]
    rz2 = [self.sb(f"l1a_rz{i}", (128, CW), F32) for i in range(2)]
    for h in range(H):
        zrow = zrow2[h % 2]
        for c in range(nch):
            tsl = slice(CW * c, CW * (c + 1))
            bk = B[4 + c % 2]
            self.mmx(bk, bk.h[0:1, 0:CW], kvb, kvb[:, 65 * h + 64:65 * h + 65],
                     Qp[h], Qp[h][:, tsl], stop=True)
            self.evac(bk, zrow, zrow[0:1, tsl], in_ap=bk.h[0:1, 0:CW])
        em.dma("y", zd.h[h, :], zrow[0:1, :], reads=[zrow.reg], writes=[zd.reg])
    ZW = H * Tc // 128
    zt = self.sb("l1a_zt", (128, ZW), F32)
    em.dma("y", zt[:, :], zd.h.flatten().rearrange("(p w) -> p w", p=128),
           reads=[zd.reg], writes=[zt.reg])
    em.op("v", lambda v: v.tensor_scalar(
        zt[:, :], zt[:, :], 128e-6, None, ALU.max),
        reads=[zt.reg], writes=[zt.reg])
    em.op("v", lambda v: v.reciprocal(zt[:, :], zt[:, :]),
          reads=[zt.reg], writes=[zt.reg])
    em.dma("y", zdi.h.flatten().rearrange("(p w) -> p w", p=128), zt[:, :],
           reads=[zt.reg], writes=[zdi.reg])
    for h in range(H):
        ft, par = h // 2, h % 2
        hb = slice(64 * par, 64 * (par + 1))
        for c in range(nch):
            rz = rz2[c % 2]
            tsl = slice(CW * c, CW * (c + 1))
            bk = B[6 + c % 2]
            self.mmx(bk, bk.h[hb, 0:CW], kvb, kvb[:, 65 * h:65 * h + 64],
                     Qp[h], Qp[h][:, tsl], stop=True, tp=(0, 64 * par),
                     pbase=64 * par)
            em.dma("y", rz[:, 0:CW],
                   zdi.h[h, tsl].unsqueeze(0).partition_broadcast(128),
                   reads=[zdi.reg], writes=[rz.reg])
            em.op("v", lambda v, ft=ft, hb=hb, tsl=tsl, bk=bk, rz=rz:
                  v.tensor_tensor(attn[ft][hb, tsl], bk.h[hb, 0:CW],
                                  rz[hb, 0:CW], ALU.mult),
                  reads=[bk.reg, rz.reg], writes=[attn[ft].reg])
            bk.fresh = True
    self.release()
    self.release()

    self.mark()
    wo = self.load_w_bf16("l1_Wo", D, D)
    xt2 = [self.sb(f"l1a_xt{i}", (128, NF * CW), F32) for i in range(2)]
    xu2 = [self.sb(f"l1a_xu{i}", (128, NF * CW), F32) for i in range(2)]
    for c in range(nch):
        xt, xu = xt2[c % 2], xu2[c % 2]
        tsl = slice(CW * c, CW * (c + 1))
        em.dma("y", xt[:, :],
               xsrc.h.rearrange("(f p) t -> p f t", p=128)[:, :, tsl],
               reads=[xsrc.reg], writes=[xt.reg])
        for o2 in range(NF):
            bk = B[o2 % 2]
            for k in range(NF):
                self.mmx(bk, bk.h[:, 0:CW], wo[k],
                         wo[k][:, 128 * o2:128 * (o2 + 1)], attn[k],
                         attn[k][:, tsl], stop=(k == NF - 1))
            em.op("v", lambda v, o2=o2, bk=bk, xt=xt, xu=xu: v.tensor_tensor(
                xu[:, CW * o2:CW * (o2 + 1)], bk.h[:, 0:CW],
                xt[:, CW * o2:CW * (o2 + 1)], ALU.add),
                reads=[bk.reg, xt.reg], writes=[xu.reg])
            bk.fresh = True
        em.dma("y", xdst.h.rearrange("(f p) t -> p f t", p=128)[:, :, tsl],
               xu[:, :], reads=[xu.reg], writes=[xdst.reg])
    self.xdram = xdst
    self.release()


MK.emit_l1a = emit_l1a



def emit_l1f(self):
    em, Tc, NT = self.em, self.Tc, self.NT
    B = [Bank(t) for t in self.pb]
    nch = (Tc + 511) // 512
    CW = min(512, Tc)
    xsrc_final = None

    cc_in = self.dscratch("cc_l1f_in", (D,), F32)
    cc_out = self.dscratch("cc_l1f_out", (8, D), F32)

    self.mark()
    xr = self.sbs("l1f_xr", (128, Tc), BF16, NF)
    xk = self.sbs("l1f_xk", (128, Tc), BF16, NF)
    self.mark()
    xlnf = self.sbs("l1f_x", (128, Tc), F32, NF)
    tmpm = self.sb("l1f_tmpm", (128, Tc), F32)
    self.emit_ln("4", xln=None, want_f32=True, want_bf16=False,
                 X_pre=xlnf, f32_inplace=True)
    xsrc = self.xdram
    xdst = self.xdram_next()

    lastc = self.sb("l1f_lastc", (128, NF), F32)
    for f in range(NF):
        self.copy("v", lastc, lastc[:, f:f + 1], xlnf[f],
                  xlnf[f][:, Tc - 1:Tc])
    for f in range(NF):
        em.dma("y", cc_in.h[128 * f:128 * (f + 1)], lastc[:, f:f + 1],
               reads=[lastc.reg], writes=[cc_in.reg])
    self.em.collective("AllGather", ALU.bypass, ins=[cc_in.h[:]],
                       outs=[cc_out.h[:, :]], reads=[cc_in.reg],
                       writes=[cc_out.reg])
    ohbt = self.inputs["ohbt_mat"]
    oh_t = self.sb("l1f_oh", (128, 8), F32)
    em.dma("y", oh_t[:, :], ohbt.h[:, :], reads=[ohbt.reg], writes=[oh_t.reg])
    agt = self.sb("l1f_agt", (128, 8), F32)
    bcol = self.sb("l1f_bcol", (128, NF), F32)
    for f in range(NF):
        em.dma("y", agt[:, :],
               cc_out.h[:, 128 * f:128 * (f + 1)].rearrange("r p -> p r"),
               reads=[cc_out.reg], writes=[agt.reg],
               allow_slow_non_contiguous=True)
        em.op("v", lambda v, f=f: v.tensor_mul(agt[:, :], agt[:, :],
                                               oh_t[:, :]),
              reads=[oh_t.reg, agt.reg], writes=[agt.reg])
        em.op("v", lambda v, f=f: v.reduce_sum(bcol[:, f:f + 1], agt[:, :], AX),
              reads=[agt.reg], writes=[bcol.reg])

    mu_r = self.load_cols("mu_r_col")
    mu_k = self.load_cols("mu_k_col")
    om = self.sb("l1f_om", (128, 2 * NF), F32)
    for f in range(NF):
        self.ts("v", om, om[:, f:f + 1], mu_r, mu_r[:, f:f + 1], -1.0,
                ALU.mult, 1.0, ALU.add, s_regs=[mu_r.reg])
        self.ts("v", om, om[:, NF + f:NF + f + 1], mu_k, mu_k[:, f:f + 1],
                -1.0, ALU.mult, 1.0, ALU.add, s_regs=[mu_k.reg])

    def mix(dst, mu_t, om_off, tag, tmp):
        for f in range(NF):
            em.op("v", lambda v, f=f: v.tensor_scalar(
                tmp[:, 1:Tc], xlnf[f][:, 0:Tc - 1],
                om[:, om_off + f:om_off + f + 1], None, ALU.mult),
                reads=[xlnf[f].reg, om.reg], writes=[tmp.reg])
            em.op("v", lambda v, f=f: v.tensor_scalar(
                tmp[:, 0:1], bcol[:, f:f + 1],
                om[:, om_off + f:om_off + f + 1], None, ALU.mult),
                reads=[bcol.reg, om.reg], writes=[tmp.reg])
            em.op("v", lambda v, f=f, dst=dst: v.scalar_tensor_tensor(
                dst[f][:, :], xlnf[f][:, :], mu_t[:, f:f + 1], tmp[:, :],
                ALU.mult, ALU.add),
                reads=[xlnf[f].reg, mu_t.reg, tmp.reg], writes=[dst[f].reg])

    mix(xk, mu_k, NF, "k", tmpm)
    mix(xr, mu_r, 0, "r", tmpm)
    self.release()

    nchf = (Tc + 511) // 512
    CWF = min(512, Tc)
    self.mark()
    wkT = self.load_w_bf16("l1f_WkT", D, D)
    wrT = self.load_w_bf16("l1f_WrT", D, D)
    wvT = self.load_w_bf16("l1f_WvT", D, D)
    kct2 = [self.sb(f"l1f_kct{i}", (128, NF * CWF), BF16) for i in range(2)]
    rsl2 = [self.sb(f"l1f_rsl{i}", (128, NF * CWF), F32) for i in range(2)]
    kr2 = [self.sb(f"l1f_kr{i}", (128, CWF), F32) for i in range(2)]
    xt2 = [self.sb(f"l1f_xt{i}", (128, CWF), F32) for i in range(2)]
    xu2 = [self.sb(f"l1f_xu{i}", (128, CWF), F32) for i in range(2)]
    for c in range(nchf):
        kct, rsl = kct2[c % 2], rsl2[c % 2]
        tsl = slice(CWF * c, CWF * (c + 1))
        for o in range(NF):
            bk = B[o % 2]
            kr = kr2[o % 2]
            for k in range(NF):
                self.mmx(bk, bk.h[:, 0:CWF], wkT[k],
                         wkT[k][:, 128 * o:128 * (o + 1)], xk[k],
                         xk[k][:, tsl], stop=(k == NF - 1))
            self.evac(bk, kr, kr[:, 0:CWF], in_ap=bk.h[:, 0:CWF], func=ACT.Relu)
            em.op("v", lambda v, o=o, kr=kr, kct=kct: v.tensor_tensor(
                kct[:, CWF * o:CWF * (o + 1)], kr[:, 0:CWF], kr[:, 0:CWF],
                ALU.mult),
                reads=[kr.reg], writes=[kct.reg])
        for o in range(NF):
            bk = B[2 + o % 2]
            for k in range(NF):
                self.mmx(bk, bk.h[:, 0:CWF], wrT[k],
                         wrT[k][:, 128 * o:128 * (o + 1)], xr[k],
                         xr[k][:, tsl], stop=(k == NF - 1))
            self.evac(bk, rsl, rsl[:, CWF * o:CWF * (o + 1)],
                      in_ap=bk.h[:, 0:CWF], func=ACT.Sigmoid)
        for o2 in range(NF):
            bk = B[4 + o2 % 2]
            xt, xu = xt2[o2 % 2], xu2[o2 % 2]
            em.dma("y", xt[:, :],
                   xsrc.h.rearrange("(f p) t -> p f t", p=128)[:, o2, tsl],
                   reads=[xsrc.reg], writes=[xt.reg])
            for k in range(NF):
                self.mmx(bk, bk.h[:, 0:CWF], wvT[k],
                         wvT[k][:, 128 * o2:128 * (o2 + 1)], kct,
                         kct[:, CWF * k:CWF * (k + 1)], stop=(k == NF - 1))
            em.op("v", lambda v, o2=o2, bk=bk, xu=xu, rsl=rsl: v.tensor_tensor(
                xu[:, :], bk.h[:, 0:CWF],
                rsl[:, CWF * o2:CWF * (o2 + 1)], ALU.mult),
                reads=[bk.reg, rsl.reg], writes=[xu.reg])
            em.op("v", lambda v, xu=xu, xt=xt: v.tensor_tensor(
                xu[:, :], xu[:, :], xt[:, :], ALU.add),
                reads=[xt.reg, xu.reg], writes=[xu.reg])
            bk.fresh = True
            em.dma("y",
                   xdst.h.rearrange("(f p) t -> p f t", p=128)[:, o2, tsl],
                   xu[:, :], reads=[xu.reg], writes=[xdst.reg])
    self.xdram = xdst
    self.release()
    self.release()


MK.emit_l1f = emit_l1f


NCORES = 8


def prep_in_maps(inp, Tc):
    x = np.asarray(inp["L0_attn_x"])[0]
    T, Dm = x.shape
    assert T == Tc * NCORES
    base = {
        "wrT": np.ascontiguousarray(inp["L0_attn_w_r"].T),
        "wkT": np.ascontiguousarray(inp["L0_attn_w_k"].T),
        "wvT": np.ascontiguousarray(inp["L0_attn_w_v"].T),
        "wgT": np.ascontiguousarray(inp["L0_attn_w_g"].T),
        "woT": np.ascontiguousarray(inp["L0_attn_w_o"].T),
        "ldT": np.ascontiguousarray(inp["L0_attn_w_lora_down"].T),
        "luT": np.ascontiguousarray(inp["L0_attn_w_lora_up"].T),
        "u_col": np.ascontiguousarray(inp["L0_attn_u"].reshape(-1, 1)),
        "wbase_col": np.ascontiguousarray(
            inp["L0_attn_w_base"].reshape(-1, 1)),
        "lng_rep": np.ascontiguousarray(
            np.tile(np.tile(inp["L0_attn_ln_gamma"], 16)[None, :], (128, 1))),
        "lnb_rep": np.ascontiguousarray(
            np.tile(np.tile(inp["L0_attn_ln_beta"], 16)[None, :], (128, 1))),
        "wgateN": inp["L0_ffn_w_gate"],
        "wgateT": np.ascontiguousarray(inp["L0_ffn_w_gate"].T),
        "wupN": inp["L0_ffn_w_up"],
        "wupT": np.ascontiguousarray(inp["L0_ffn_w_up"].T),
        "wdownN": inp["L0_ffn_w_down"],
        "wdownT": np.ascontiguousarray(inp["L0_ffn_w_down"].T),
        "rf2": np.ascontiguousarray(np.vstack([inp["L1_attn_rf"],
                                               inp["L1_attn_rf"]])),
        "l1_Wq": inp["L1_attn_Wq"],
        "l1_Wk": inp["L1_attn_Wk"],
        "l1_Wv": inp["L1_attn_Wv"],
        "l1_Wo": inp["L1_attn_Wo"],
        "l1f_WrT": np.ascontiguousarray(inp["L1_ffn_W_r"].T),
        "l1f_WkT": np.ascontiguousarray(inp["L1_ffn_W_k"].T),
        "l1f_WvT": np.ascontiguousarray(inp["L1_ffn_W_v"].T),
        "mu_r_col": np.ascontiguousarray(inp["L1_ffn_mu_r"].reshape(-1, 1)),
        "mu_k_col": np.ascontiguousarray(inp["L1_ffn_mu_k"].reshape(-1, 1)),
    }
    base = {k: v.astype(np.float32) for k, v in base.items()}
    maps = []
    for c in range(NCORES):
        m = dict(base)
        m["x_fm"] = np.ascontiguousarray(x[Tc * c:Tc * (c + 1), :].T)
        oh = np.zeros((128, 8), np.float32)
        oh[:, c] = 1.0
        m["ohprev_mat"] = oh
        ohb = np.zeros((128, 8), np.float32)
        if c > 0:
            ohb[:, c - 1] = 1.0
        m["ohbt_mat"] = ohb
        maps.append(m)
    return maps

def build(Tc):
    mk = MK(Tc=Tc)
    mk.psum_init()
    mk.emit_consts()
    for n, shp in [("wrT", (D, D)), ("wkT", (D, D)), ("wvT", (D, D)),
                   ("wgT", (D, D)), ("woT", (D, D)), ("ldT", (D, LR)),
                   ("luT", (LR, D)), ("u_col", (D, 1)), ("wbase_col", (D, 1)),
                   ("lng_rep", (128, D)), ("lnb_rep", (128, D)),
                   ("ohprev_mat", (128, 8)), ("ohbt_mat", (128, 8)),
                   ("wgateN", (FI, D)), ("wgateT", (D, FI)),
                   ("wupN", (FI, D)), ("wupT", (D, FI)),
                   ("wdownN", (D, FI)), ("wdownT", (FI, D)),
                   ("rf2", (128, 128)), ("l1_Wq", (D, D)), ("l1_Wk", (D, D)),
                   ("l1_Wv", (D, D)), ("l1_Wo", (D, D)),
                   ("l1f_WrT", (D, D)), ("l1f_WkT", (D, D)),
                   ("l1f_WvT", (D, D)),
                   ("mu_r_col", (D, 1)), ("mu_k_col", (D, 1))]:
        mk.din(n, shp)
    mk.emit_dq_scales(keys=("g",))
    mk.emit_load_x()
    mk.mark()
    mk.emit_dq_piece_bufs()
    mk.mark()
    xln1 = mk.sbs("xln_1", (128, Tc), BF16, NF)
    gate_p = mk.emit_dq_wt_pieces("g")
    mk.emit_ln("1", xln=xln1)

    def filler(g):
        for _ in range(4):
            if gate_p:
                gate_p.pop(0)()

    mk.emit_l0a_sweep1(xln1, filler=filler)
    while gate_p:
        gate_p.pop(0)()
    mk.weight_precast([("wgT", (D, D)), ("woT", (D, D)),
                       ("l1_Wq", (D, D)), ("l1_Wk", (D, D)),
                       ("l1_Wv", (D, D)), ("l1_Wo", (D, D)),
                       ("l1f_WrT", (D, D)), ("l1f_WkT", (D, D)),
                       ("l1f_WvT", (D, D))])
    mk.emit_l0a_collective()
    mk.emit_l0a_sweep2(xln1)
    mk.release()
    mk.mark()
    xq = mk.emit_q1()
    mk.emit_dq_scales(keys=("u", "d"))
    up_p = mk.emit_dq_wt_pieces("u")
    dn_p = mk.emit_dq_wt_pieces("d")
    mk.emit_l0f(xq, wt_pieces=(up_p, dn_p))
    mk.release()
    mk.release()
    mk.mark()
    xln3 = mk.sbs("xln_3", (128, Tc), BF16, NF)
    mk.emit_ln("3", xln=xln3)
    mk.emit_l1a(xln3)
    mk.release()
    mk.emit_l1f()
    mk.emit_store_x()
    return mk, mk.finish()




_CACHE = {}


def kernel(**inputs):
    import numpy as np
    from concourse.bass_utils import run_bass_kernel_spmd
    Tc = 2048
    if "nc" not in _CACHE:
        _CACHE["mk"], _CACHE["nc"] = build(Tc)
    nc = _CACHE["nc"]
    maps = prep_in_maps(inputs, Tc)
    res = run_bass_kernel_spmd(nc, maps, core_ids=list(range(8)))
    out = np.concatenate([np.asarray(res.results[c]["y_fm"]).T
                          for c in range(8)], axis=0)[None]
    return np.ascontiguousarray(out.astype(np.float32))


# version 22
# speedup vs baseline: 1.0179x; 1.0179x over previous
import numpy as np

import math
import ml_dtypes
import concourse.bass as bass
import concourse.mybir as mybir

F32 = mybir.dt.float32
BF16 = mybir.dt.bfloat16
I32 = mybir.dt.int32
ALU = mybir.AluOpType
ACT = mybir.ActivationFunctionType
AX = mybir.AxisListType.X

D = 1024
H = 16
DH = 64
LR = 64
FI = 3072
FD = 128
L = 64
NF = 8
EPS = 1e-5
QMAX16 = 32767.0


class Reg:
    __slots__ = ("name", "w", "r")

    def __init__(self, name="r"):
        self.name = name
        self.w = None
        self.r = []


class Em:
    ENG = ("t", "v", "s", "g", "y")
    QUEUES = {"y": "qy", "s": "qs", "g": "qg"}

    KSLOT = 8

    def __init__(self, nc):
        self.nc = nc
        self.plan = {k: [] for k in self.ENG}
        self.planned_inc = {k: {} for k in self.ENG}
        self.ninc = {k: 0 for k in self.ENG}
        self.ndma = {q: 0 for q in ("qy", "qs", "qg")}
        self.ncc = 0
        self.waited = {k: {} for k in self.ENG}
        self.nop = 0
        self._pending_waits = {}
        self.sems = {}
        self.semkeys = list(self.ENG) + ["cc"]
        for q in ("qy", "qs", "qg"):
            for i in range(self.KSLOT):
                self.semkeys.append(f"{q}{i}")
        for k in self.semkeys:
            self.sems[k] = nc.alloc_semaphore(f"sem_{k}")
        self.slot_uses = {k: 0 for k in self.semkeys}

    def _ensure_inc(self, eng, idx):
        ks = [j for j in self.planned_inc[eng] if j >= idx]
        if ks:
            return self.planned_inc[eng][min(ks)]
        self.ninc[eng] += 1
        val = self.ninc[eng]
        self.planned_inc[eng][idx] = val
        self.plan[eng][idx][2].append((eng, 1))
        return val

    def _wait(self, consumer, token):
        if token is None:
            return
        kind, x = token
        if kind in self.ENG:
            val = self._ensure_inc(kind, x)
            skey = kind
        elif kind == "cc":
            val, skey = self.ncc, kind
        else:
            val, skey = x, kind
        prev = self.waited[consumer].get(skey, -1)
        if val <= prev:
            return
        self._pending_waits.setdefault(consumer, []).append((skey, val))
        self.waited[consumer][skey] = val

    def _deps(self, consumer, reads, writes):
        for r in reads:
            self._wait(consumer, r.w)
        for w in writes:
            self._wait(consumer, w.w)
            for t in w.r:
                self._wait(consumer, t)

    def _update(self, token, reads, writes):
        for r in reads:
            r.r.append(token)
        for w in writes:
            w.w = token
            w.r = []

    def _push(self, eng, fn, kind="op"):
        waits = self._pending_waits.pop(eng, [])
        idx = len(self.plan[eng])
        self.plan[eng].append([fn, waits, [], kind])
        self.nop += 1
        return idx

    def op(self, eng, fn, reads=(), writes=(), kind="op"):
        self._deps(eng, reads, writes)
        idx = self._push(eng, fn, kind)
        self._update((eng, idx), reads, writes)
        return (eng, idx)

    def dma(self, eng, out, in_, reads=(), writes=(), **kw):
        q = self.QUEUES[eng]
        self._deps(eng, reads, writes)
        n = self.ndma[q]
        self.ndma[q] += 1
        skey = f"{q}{n % self.KSLOT}"
        self.slot_uses[skey] += 1
        uses = self.slot_uses[skey]
        if uses > 1:
            prev = 16 * (uses - 1)
            if prev > self.waited[eng].get(skey, -1):
                self._pending_waits.setdefault(eng, []).append((skey, prev))
                self.waited[eng][skey] = prev
        idx = self._push(eng, lambda e, out=out, in_=in_, kw=kw: e.dma_start(
            out=out, in_=in_, **kw))
        self.plan[eng][idx][2].append((skey, 16))
        self._update((skey, 16 * uses), reads, writes)
        return (skey, 16 * uses)

    def collective(self, kind, op, ins, outs, reads=(), writes=()):
        self._deps("g", reads, writes)
        self.ncc += 1
        n = self.ncc
        idx = self._push(
            "g", lambda e, k=kind, o=op, i=ins, u=outs: e.collective_compute(
                k, o, replica_groups=[list(range(8))], ins=i, outs=u))
        self.plan["g"][idx][2].append(("cc", 1))
        self._update(("cc", n), reads, writes)
        return ("cc", n)

    def sync_engine(self, eng, regs):
        for r in regs:
            self._wait(eng, r.w)
            for t in r.r:
                self._wait(eng, t)

    def final_waits(self, eng):
        for k in self.semkeys:
            if k in self.ENG or k == "cc":
                continue
            v = 16 * self.slot_uses[k]
            if v > self.waited[eng].get(k, -1) and v > 0:
                self._pending_waits.setdefault(eng, []).append((k, v))

    def emit(self, block):
        sems = self.sems

        def run(e, eng):
            for fn, waits, incs, kind in self.plan[eng]:
                for skey, val in waits:
                    e.wait_ge(sems[skey], val)
                inst = fn(e)
                for skey, n in incs:
                    inst.then_inc(sems[skey], n)
            for skey, val in self._pending_waits.pop(eng, []):
                e.wait_ge(sems[skey], val)

        block.tensor(lambda e: run(e, "t"))
        block.vector(lambda e: run(e, "v"))
        block.scalar(lambda e: run(e, "s"))
        block.gpsimd(lambda e: run(e, "g"))
        block.sync(lambda e: run(e, "y"))


class Arena:
    def __init__(self, nc):
        self.nc = nc
        self.base = (nc.sbuf_base + 31) // 32 * 32
        self.top = nc.sbuf_top
        self.off = self.base
        self.marks = []
        self.peak = self.base

    def alloc_h(self, name, shape, dtype):
        esz = 2 if dtype in (BF16, mybir.dt.float16) else 4
        per = (int(shape[1]) * esz + 31) // 32 * 32
        h = self.nc.alloc_sbuf_tensor_at(name, list(shape), dtype, offset=self.off)
        self.off += per
        self.peak = max(self.peak, self.off)
        assert self.off <= self.top, f"SBUF overflow at {name}: {(self.off-self.base)/1024:.0f}KB"
        return h

    def mark(self):
        self.marks.append(self.off)

    def release(self):
        self.off = self.marks.pop()

    def kb(self):
        return (self.off - self.base) / 1024.0


class T:
    __slots__ = ("h", "reg")

    def __init__(self, h, reg):
        self.h = h
        self.reg = reg

    def __getitem__(self, key):
        return self.h[key]

    @property
    def ap(self):
        return self.h[:, :] if not isinstance(self.h, bass.AP) else self.h


class MK:

    def __init__(self, Tc=2048, blocks=("l0a", "l0f", "l1a", "l1f")):
        self.Tc = Tc
        self.blocks = blocks
        self.NT = Tc // 128
        self.NCH = Tc // L
        self.nc = bass.Bass("TRN2", target_bir_lowering=False, debug=False)
        self.em = Em(self.nc)
        self.ar = Arena(self.nc)
        self.inputs = {}
        self.n_dram = 0

    def din(self, name, shape):
        ap = self.nc.dram_tensor(name, list(shape), F32, kind="ExternalInput").ap()
        t = T(ap, Reg(name))
        self.inputs[name] = t
        return t

    def dout(self, name, shape, dtype=F32):
        ap = self.nc.dram_tensor(name, list(shape), dtype, kind="ExternalOutput").ap()
        return T(ap, Reg(name))

    def dscratch(self, name, shape, dtype=F32):
        ap = self.nc.dram_tensor(name, list(shape), dtype).ap()
        return T(ap, Reg(name))

    def sb(self, name, shape, dtype):
        t = T(self.ar.alloc_h(name, shape, dtype), Reg(name))
        if not hasattr(self, "_scopes"):
            self._scopes = []
            self._fence = []
        t.reg.r = list(self._fence)
        if self._scopes:
            self._scopes[-1].append(t)
        return t

    def mark(self):
        if not hasattr(self, "_scopes"):
            self._scopes = []
            self._fence = []
        self.ar.mark()
        self._scopes.append([])

    def release(self):
        ts = self._scopes.pop()
        for t in ts:
            if t.reg.w is not None:
                self._fence.append(t.reg.w)
            self._fence.extend(t.reg.r)
        best = {}
        for k, v in self._fence:
            if k not in best or v > best[k]:
                best[k] = v
        self._fence = [(k, v) for k, v in best.items()]
        self.ar.release()

    def sbs(self, name, shape, dtype, n):
        return [self.sb(f"{name}{i}", shape, dtype) for i in range(n)]

    def psum_init(self):
        self.pb = []
        for b in range(8):
            h = self.nc.alloc_psum_tensor(f"pbank{b}", [128, 512], F32)
            self.pb.append(T(h, Reg(f"pb{b}")))

    def mm(self, out_ap, lhsT, lhsT_ap, rhs, rhs_ap, bank, start, stop):
        self.em.op("t", lambda t: t.matmul(out_ap, lhsT_ap, rhs_ap,
                                           start=start, stop=stop),
                   reads=[lhsT.reg, rhs.reg], writes=[bank.reg])

    def tt(self, eng, out_t, out_ap, a_t, a_ap, b_t, b_ap, op, extra_reads=()):
        self.em.op(eng, lambda e: e.tensor_tensor(out_ap, a_ap, b_ap, op),
                   reads=[a_t.reg, b_t.reg, *extra_reads], writes=[out_t.reg])

    def ts(self, eng, out_t, out_ap, a_t, a_ap, s1, op0, s2=None, op1=ALU.bypass,
           s_regs=()):
        self.em.op(eng, lambda e: e.tensor_scalar(out_ap, a_ap, s1, s2, op0, op1),
                   reads=[a_t.reg, *s_regs], writes=[out_t.reg])

    def act(self, eng, out_t, out_ap, a_t, a_ap, func, bias=0.0, scale=1.0,
            extra_reads=()):
        self.em.op(eng, lambda e: e.activation(out_ap, a_ap, func, bias=bias,
                                               scale=scale),
                   reads=[a_t.reg, *extra_reads], writes=[out_t.reg])

    def copy(self, eng, out_t, out_ap, a_t, a_ap):
        self.em.op(eng, lambda e: e.tensor_copy(out_ap, a_ap),
                   reads=[a_t.reg], writes=[out_t.reg])

    def ccol(self, val):
        if not hasattr(self, "_ccols"):
            self._ccols = {}
            self._ccol_t = self.sb("cconst", (128, 16), F32)
            self._ccol_n = 0
        if val not in self._ccols:
            k = self._ccol_n
            self._ccol_n += 1
            assert k < 16
            self.em.op("g", lambda g, k=k, val=val: g.memset(
                self._ccol_t[:, k:k + 1], float(val)), writes=[self._ccol_t.reg])
            self._ccols[val] = k
        k = self._ccols[val]
        return self._ccol_t[:, k:k + 1]

    @property
    def ccol_reg(self):
        return self._ccol_t.reg

    def emit_consts(self):
        em = self.em
        self.ones_col = self.sb("ones_col", (128, 2), BF16)
        self.ident = self.sb("ident", (128, 128), BF16)
        self.onesf = self.sb("onesf", (128, 128), F32)
        self.mstrict = self.sb("mstrict", (128, 64), F32)
        self.mdiag = self.sb("mdiag", (128, 64), F32)
        em.op("g", lambda g: g.memset(self.ones_col[:, :], 1.0),
              writes=[self.ones_col.reg])
        em.op("g", lambda g: g.memset(self.onesf[:, :], 1.0), writes=[self.onesf.reg])
        self.onesb = self.sb("onesb", (128, 128), BF16)
        self.identf = self.sb("identf", (128, 128), F32)
        em.op("g", lambda g: g.memset(self.onesb[:, :], 1.0), writes=[self.onesb.reg])
        em.op("g", lambda g: g.affine_select(
            self.ident[:, :], self.onesb[:, :], [[1, 128]],
            ALU.is_equal, 0.0, base=0, channel_multiplier=-1),
            reads=[self.onesb.reg], writes=[self.ident.reg])
        em.op("g", lambda g: g.affine_select(
            self.identf[:, :], self.onesf[:, :], [[1, 128]],
            ALU.is_equal, 0.0, base=0, channel_multiplier=-1),
            reads=[self.onesf.reg], writes=[self.identf.reg])
        em.op("g", lambda g: g.affine_select(
            self.mstrict[0:64, :], self.onesf[0:64, 0:64], [[1, 64]],
            ALU.is_ge, 0.0, base=-1, channel_multiplier=-1),
            reads=[self.onesf.reg], writes=[self.mstrict.reg])
        em.op("g", lambda g: g.affine_select(
            self.mdiag[0:64, :], self.onesf[0:64, 0:64], [[1, 64]],
            ALU.is_equal, 0.0, base=0, channel_multiplier=-1),
            reads=[self.onesf.reg], writes=[self.mdiag.reg])
        em.op("g", lambda g: g.tensor_copy(self.mstrict[64:128, :],
                                           self.mstrict[0:64, :]),
              reads=[self.mstrict.reg], writes=[self.mstrict.reg])
        em.op("g", lambda g: g.tensor_copy(self.mdiag[64:128, :],
                                           self.mdiag[0:64, :]),
              reads=[self.mdiag.reg], writes=[self.mdiag.reg])

    def emit_load_x(self):
        xin = self.din("x_fm", (D, self.Tc))
        self.xdram = xin
        self.xping = [self.dscratch("x_ping0", (D, self.Tc), F32),
                      self.dscratch("x_ping1", (D, self.Tc), F32)]
        self.nping = 0

    def xdram_next(self):
        t = self.xping[self.nping % 2]
        self.nping += 1
        return t

    def rsqrt_refine(self, out_t, out_ap, s_t, s0_ap, d_ap):
        em = self.em
        em.op("v", lambda v: v.reciprocal(out_ap, s0_ap),
              reads=[s_t.reg], writes=[out_t.reg])
        em.op("v", lambda v: v.scalar_tensor_tensor(
            out_ap, d_ap, 0.5, out_ap, ALU.mult, ALU.mult),
            reads=[s_t.reg, out_t.reg], writes=[out_t.reg])
        em.op("v", lambda v: v.scalar_tensor_tensor(
            out_ap, s0_ap, 0.5, out_ap, ALU.mult, ALU.add),
            reads=[s_t.reg, out_t.reg], writes=[out_t.reg])
        em.op("v", lambda v: v.reciprocal(out_ap, out_ap),
              reads=[out_t.reg], writes=[out_t.reg])

    def emit_ln(self, tag, xln=None, xlnf=None, want_f32=False,
                want_bf16=True, X_pre=None, f32_inplace=False):
        em, Tc = self.em, self.Tc
        if xln is None and want_bf16:
            xln = self.sbs(f"xln_{tag}", (128, Tc), BF16, NF)
        if want_f32 and xlnf is None and not f32_inplace:
            xlnf = self.sbs(f"xlnf_{tag}", (128, Tc), F32, NF)
        self.mark()
        X = X_pre if X_pre is not None else self.sbs(
            f"ln_x_{tag}", (128, Tc), F32, NF)
        for f in range(NF):
            self.em.dma("y", X[f][:, :],
                        self.xdram.h[128 * f:128 * (f + 1), :],
                        reads=[self.xdram.reg], writes=[X[f].reg])
        self.X = X
        if f32_inplace:
            xlnf = X
        xb = self.sb(f"ln_xb_{tag}", (128, Tc), BF16)
        sq = self.sb(f"ln_sq_{tag}", (128, Tc), BF16)
        tdiff = self.sb(f"ln_td_{tag}", (128, Tc), F32)
        rows = self.sb(f"ln_rows_{tag}", (1, 2 * Tc), F32)
        bounce = self.dscratch(f"lnb_{tag}", (2 * Tc,), F32)
        mu_b = self.sb(f"ln_mub_{tag}", (128, Tc), F32)
        rs_b = self.sb(f"ln_rsb_{tag}", (128, Tc), F32)

        nch = (Tc + 511) // 512
        assert nch <= 4
        B = [Bank(t) for t in self.pb]
        for f in range(NF):
            self.copy("v", xb, xb[:, :], self.X[f], self.X[f][:, :])
            self.tt("v", sq, sq[:, :], xb, xb[:, :], xb, xb[:, :], ALU.mult)
            for c in range(nch):
                sl = slice(512 * c, min(512 * (c + 1), Tc))
                w = sl.stop - sl.start
                self.mmx(B[2 * c], B[2 * c].h[0:1, 0:w], self.ones_col,
                         self.ones_col[:, 0:1], xb, xb[:, sl],
                         stop=(f == NF - 1))
                self.mmx(B[2 * c + 1], B[2 * c + 1].h[0:1, 0:w], self.ones_col,
                         self.ones_col[:, 0:1], sq, sq[:, sl],
                         stop=(f == NF - 1))
        for c in range(nch):
            sl = slice(512 * c, min(512 * (c + 1), Tc))
            w = sl.stop - sl.start
            self.evac(B[2 * c], rows, rows[0:1, sl], in_ap=B[2 * c].h[0:1, 0:w],
                      func=ACT.Copy, scale=1.0 / D)
            self.evac(B[2 * c + 1], rows, rows[0:1, Tc + sl.start:Tc + sl.stop],
                      in_ap=B[2 * c + 1].h[0:1, 0:w], func=ACT.Copy,
                      scale=1.0 / D)
        self.em.dma("y", bounce.h[:], rows[0:1, :], reads=[rows.reg],
                    writes=[bounce.reg])
        W = Tc // 128
        bounce2 = self.dscratch(f"lnb2_{tag}", (Tc,), F32)
        mu_t = self.sb(f"ln_mut_{tag}", (128, W), F32)
        ms_t = self.sb(f"ln_mst_{tag}", (128, W), F32)
        vt2 = self.sb(f"ln_vt2_{tag}", (128, 2 * W), F32)
        va_t, va_ap = vt2, vt2[:, 0:W]
        rs_t = self.sb(f"ln_rst_{tag}", (128, W), F32)
        self.em.dma("y", mu_t[:, :],
                    bounce.h[0:Tc].rearrange("(p w) -> p w", p=128),
                    reads=[bounce.reg], writes=[mu_t.reg])
        self.em.dma("y", ms_t[:, :],
                    bounce.h[Tc:2 * Tc].rearrange("(p w) -> p w", p=128),
                    reads=[bounce.reg], writes=[ms_t.reg])
        self.em.op("v", lambda v, va_ap=va_ap: v.scalar_tensor_tensor(
            va_ap, mu_t[:, :], 1.0, mu_t[:, :], ALU.mult, ALU.mult),
            reads=[mu_t.reg], writes=[va_t.reg])
        self.tt("v", va_t, va_ap, ms_t, ms_t[:, :], va_t, va_ap, ALU.subtract)
        self.em.op("v", lambda v, va_ap=va_ap: v.tensor_scalar(
            va_ap, va_ap, EPS, None, ALU.add),
            reads=[va_t.reg], writes=[va_t.reg])
        self.act("s", vt2, vt2[:, W:2 * W], va_t, va_ap, ACT.Sqrt)
        self.rsqrt_refine(rs_t, rs_t[:, :], vt2, vt2[:, W:2 * W], va_ap)
        self.em.dma("y", bounce2.h[0:Tc].rearrange("(p w) -> p w", p=128),
                    rs_t[:, :], reads=[rs_t.reg], writes=[bounce2.reg])
        self.em.dma("y", mu_b[:, :],
                    bounce.h[0:Tc].unsqueeze(0).partition_broadcast(128),
                    reads=[bounce.reg], writes=[mu_b.reg])
        self.em.dma("y", rs_b[:, :],
                    bounce2.h[0:Tc].unsqueeze(0).partition_broadcast(128),
                    reads=[bounce2.reg], writes=[rs_b.reg])
        for f in range(NF):
            self.tt("v", tdiff, tdiff[:, :], self.X[f], self.X[f][:, :],
                    mu_b, mu_b[:, :], ALU.subtract)
            if want_bf16:
                self.tt("v", xln[f], xln[f][:, :], tdiff, tdiff[:, :],
                        rs_b, rs_b[:, :], ALU.mult)
            if want_f32:
                self.tt("v", xlnf[f], xlnf[f][:, :], tdiff, tdiff[:, :],
                        rs_b, rs_b[:, :], ALU.mult)
        self.release()
        self.X = None
        return xln, xlnf

    def emit_store_x(self):
        xout = self.dout("y_fm", (D, self.Tc))
        self.em.dma("y", xout.h[:, :], self.xdram.h[:, :],
                    reads=[self.xdram.reg], writes=[xout.reg])

    def finish(self):
        self.em.final_waits("y")
        with self.nc.Block() as block:
            self.em.emit(block)
        return self.nc


def store_tiles(mk, name, tiles, rows=128):
    n = len(tiles)
    cols = tiles[0].h.shape[1]
    out = mk.dout(name, (n * rows, cols))
    for i, t in enumerate(tiles):
        mk.em.dma("g", out.h[rows * i:rows * (i + 1), :], t[:, :],
                  reads=[t.reg], writes=[out.reg])
    return out



class Bank:
    __slots__ = ("t", "started")

    def __init__(self, t):
        self.t = t
        self.started = set()

    @property
    def fresh(self):
        return not self.started

    @fresh.setter
    def fresh(self, val):
        if val:
            self.started = set()

    @property
    def h(self):
        return self.t.h

    @property
    def reg(self):
        return self.t.reg


def _mk_weight_precast(self, names_shapes):
    if not hasattr(self, "_wcast"):
        self._wcast = {}
    for name, (rows, cols) in names_shapes:
        ap = self.inputs[name]
        d = self.dscratch(f"wc_{name}", (rows, cols), BF16)
        self.em.dma("g", d.h[:, :], ap.h[:, :], reads=[ap.reg],
                    writes=[d.reg])
        self._wcast[name] = d


def _mk_load_w_bf16(self, name, rows, cols):
    wc = getattr(self, "_wcast", {}).get(name)
    ap = wc if wc is not None else self.inputs[name]
    q = "y" if wc is not None else "g"
    nt = (rows + 127) // 128
    tiles = self.sbs(f"w_{name}", (min(128, rows), cols), BF16, nt)
    for i in range(nt):
        self.em.dma(q, tiles[i][:, :], ap.h[128 * i:min(128 * (i + 1), rows), :],
                    reads=[ap.reg], writes=[tiles[i].reg])
    return tiles


def _mk_load_cols(self, name):
    ap = self.inputs[name]
    t = self.sb(f"c_{name}", (128, NF), F32)
    for f in range(NF):
        self.em.dma("g", t[:, f:f + 1], ap.h[128 * f:128 * (f + 1), :],
                    reads=[ap.reg], writes=[t.reg])
    return t


def _mk_mmx(self, bank, out_ap, lhsT_t, lhsT_ap, rhs_t, rhs_ap, stop=False,
            tp=None, pbase=0):
    start = pbase not in bank.started
    bank.started.add(pbase)
    self.em.op("t", lambda t: t.matmul(out_ap, lhsT_ap, rhs_ap, start=start,
                                       stop=stop, tile_position=tp,
                                       skip_group_check=True),
               reads=[lhsT_t.reg, rhs_t.reg], writes=[bank.reg], kind="mm")


def _mk_evac(self, bank, out_t, out_ap, in_ap=None, func=None, bias=0.0,
             scale=1.0, extra_reads=()):
    ia = in_ap if in_ap is not None else bank.h[:, :]
    if func is None:
        self.em.op("s", lambda s: s.copy(out_ap, ia),
                   reads=[bank.reg, *extra_reads], writes=[out_t.reg])
    else:
        self.em.op("s", lambda s: s.activation(out_ap, ia, func, bias=bias,
                                               scale=scale),
                   reads=[bank.reg, *extra_reads], writes=[out_t.reg])
    bank.fresh = True


MK.load_w_bf16 = _mk_load_w_bf16
MK.weight_precast = _mk_weight_precast
MK.load_cols = _mk_load_cols
MK.mmx = _mk_mmx
MK.evac = _mk_evac



def emit_l0a_sweep1(self, xln, upto="all", filler=None):
    em, Tc, NCH = self.em, self.Tc, self.NCH
    NG = NCH // 4
    NT = self.NT
    CCW = 512 + 8

    st = self.l0a = {}
    st["outacc"] = outacc = self.dscratch("outacc", (Tc, D), F32)
    st["cc_in"] = self.dscratch("cc_l0a_in", (128, CCW), F32)
    st["cc_out"] = self.dscratch("cc_l0a_out", (8, 128, CCW), F32)

    st["rt"] = rt = self.sbs("rt", (128, Tc), BF16, NF)
    st["eALt"] = eALt = self.sb("eALt", (128, NCH * 8), F32)
    st["Bcol"] = Bcol = [self.sb(f"Bcol{f}", (128, NCH), F32) for f in range(NF)]
    st["carry"] = carry = self.sb("carry", (128, NF), F32)
    st["S_cur"] = S_cur = self.sb("S_cur", (128, 512), F32)
    st["S_tmp"] = S_tmp = self.sb("S_tmp", (128, 512), F32)
    st["S_bf"] = S_bf = self.sb("S_bf", (128, 512), BF16)
    st["S_in"] = self.sb("S_in", (128, 512), F32)
    st["S_inb"] = self.sb("S_inb", (128, 512), BF16)
    st["P_run"] = self.sb("P_run", (128, 512), F32)
    st["Dg"] = self.sb("Dg", (128, 8), F32)

    em.op("v", lambda v: v.memset(S_cur[:, :], 0.0), writes=[S_cur.reg])
    em.op("v", lambda v: v.memset(S_bf[:, :], 0.0), writes=[S_bf.reg])
    em.op("v", lambda v: v.memset(carry[:, :], 0.0), writes=[carry.reg])

    self.mark()
    if upto == "spill":
        self.release()
        return
    wrT = self.load_w_bf16("wrT", D, D)
    wkT = self.load_w_bf16("wkT", D, D)
    wvT = self.load_w_bf16("wvT", D, D)
    ldT = self.load_w_bf16("ldT", D, LR)
    luT = self.load_w_bf16("luT", LR, D)
    u_t = self.load_cols("u_col")
    wb_t = self.load_cols("wbase_col")

    if upto == "weights":
        self.release()
        return
    spA = [self.sb(f"spA{f}", (128, 256), F32) for f in range(NF)]
    spB = [self.sb(f"spB{f}", (128, 256), F32) for f in range(NF)]
    Acum = [self.sb(f"Acum{f}", (128, 256), F32) for f in range(NF)]
    Ek = [self.sb(f"Ek{f}", (128, 256), F32) for f in range(NF)]
    Arel = spA
    Er = Acum
    bases = self.sb("bases", (128, 4 * NF), F32)
    r_g = [self.sb(f"r_g{f}", (128, 256), BF16) for f in range(NF)]
    k_g = [self.sb(f"k_g{f}", (128, 256), BF16) for f in range(NF)]
    k1_g = [self.sb(f"k1g{f}", (128, 256), BF16) for f in range(NF)]
    ku_g = [self.sb(f"kug{f}", (128, 256), BF16) for f in range(NF)]
    v_g = [self.sb(f"v_g{t}", (128, D), BF16) for t in range(2)]
    ktm_g = [self.sb(f"ktmg{t}", (128, D), BF16) for t in range(2)]
    z1 = self.sb("z1", (64, 256), BF16)
    attC2 = [self.sb(f"attC{i}", (128, D), BF16) for i in range(2)]
    mevA = [self.sb("mev0", (128, 512), F32)] * 2
    mevB = [self.sb("mev20", (128, 512), F32)] * 2
    o_ev2 = [self.sb("o_ev0", (128, D), F32)] * 2

    B = [Bank(t) for t in self.pb]
    BA0, BA1, B20, B21, BM, BO0, BO1, BTR = range(8)

    def r3(ap, c=4, l=64):
        return ap.rearrange("p (c l) -> p c l", c=c)

    for g in range(NG):
        gsl = slice(256 * g, 256 * (g + 1))
        for dst, wT in ((r_g, wrT), (k_g, wkT)):
            for o in range(NF):
                for k in range(NF):
                    self.mmx(B[BO0], B[BO0].h[:, 0:256], wT[k],
                             wT[k][:, 128 * o:128 * (o + 1)], xln[k],
                             xln[k][:, gsl], stop=(k == NF - 1))
                self.evac(B[BO0], dst[o], dst[o][:, :],
                          in_ap=B[BO0].h[:, 0:256])
        for ts in range(2):
            tsl = slice(256 * g + 128 * ts, 256 * g + 128 * (ts + 1))
            for n in range(2):
                nsl = slice(512 * n, 512 * (n + 1))
                for k in range(NF):
                    self.mmx(B[BO1], B[BO1].h[:, 0:512], xln[k], xln[k][:, tsl],
                             wvT[k], wvT[k][:, nsl], stop=(k == NF - 1))
                self.evac(B[BO1], v_g[ts], v_g[ts][:, nsl],
                          in_ap=B[BO1].h[:, 0:512])
        if upto == "proj":
            continue
        for k in range(NF):
            self.mmx(B[BM], B[BM].h[0:64, 0:256], ldT[k], ldT[k][:, :],
                     xln[k], xln[k][:, gsl], stop=(k == NF - 1))
        self.evac(B[BM], z1, z1[:, :], in_ap=B[BM].h[0:64, 0:256], func=ACT.Tanh)
        for o in range(NF):
            self.mmx(B[BTR], B[BTR].h[:, 0:256], luT[0],
                     luT[0][:, 128 * o:128 * (o + 1)], z1, z1[:, :], stop=True)
            self.evac(B[BTR], spA[o], spA[o][:, :], in_ap=B[BTR].h[:, 0:256],
                      func=ACT.Exp, bias=wb_t[:, o:o + 1],
                      extra_reads=[wb_t.reg])
            self.act("s", spB[o], spB[o][:, :], spA[o], spA[o][:, :], ACT.Ln,
                     bias=self.ccol(1.0), extra_reads=[self.ccol_reg])
        for f in range(NF):
            em.op("v", lambda v, f=f: v.tensor_tensor_scan(
                Acum[f][:, :], spB[f][:, :], spB[f][:, :], carry[:, f:f + 1],
                ALU.add, ALU.bypass),
                reads=[spB[f].reg, carry.reg], writes=[Acum[f].reg])
            bsl = slice(4 * f, 4 * f + 4)
            self.copy("v", bases, bases[:, 4 * f:4 * f + 1], carry,
                      carry[:, f:f + 1])
            em.op("v", lambda v, f=f: v.tensor_copy(
                bases[:, 4 * f + 1:4 * f + 4], r3(Acum[f][:, :])[:, 0:3, 63]),
                reads=[Acum[f].reg], writes=[bases.reg])
            self.copy("v", carry, carry[:, f:f + 1], Acum[f], Acum[f][:, 255:256])
            em.op("v", lambda v, f=f, bsl=bsl: v.tensor_tensor(
                spB[f][:, 0:4], r3(Acum[f][:, :])[:, :, 63], bases[:, bsl],
                ALU.subtract),
                reads=[Acum[f].reg, bases.reg], writes=[spB[f].reg])
            em.op("s", lambda s, f=f, g=g: s.activation(
                eALt[:, 8 * 4 * g:8 * 4 * (g + 1)].rearrange(
                    "p (c q) -> p c q", q=8)[:, :, f],
                spB[f][:, 0:4], ACT.Exp, scale=-1.0),
                reads=[spB[f].reg], writes=[eALt.reg])
            self.copy("v", Bcol[f], Bcol[f][:, 4 * g:4 * g + 4], bases,
                      bases[:, bsl])
            em.op("v", lambda v, f=f, bsl=bsl: v.tensor_tensor(
                r3(Arel[f][:, :]), r3(Acum[f][:, :]),
                bases[:, bsl].unsqueeze(2).broadcast_to([128, 4, 64]),
                ALU.subtract),
                reads=[Acum[f].reg, bases.reg], writes=[Arel[f].reg])
            self.act("s", Ek[f], Ek[f][:, :], Arel[f], Arel[f][:, :], ACT.Exp)
            self.act("s", Er[f], Er[f][:, :], Arel[f], Arel[f][:, :], ACT.Exp,
                     scale=-1.0)
        if upto == "scan":
            continue
        for f in range(NF):
            em.op("v", lambda v, f=f, gsl=gsl: v.tensor_tensor(
                r3(rt[f][:, gsl])[:, :, 1:64], r3(r_g[f][:, :])[:, :, 1:64],
                r3(Er[f][:, :])[:, :, 0:63], ALU.mult),
                reads=[r_g[f].reg, Er[f].reg], writes=[rt[f].reg])
            em.op("v", lambda v, f=f, gsl=gsl: v.tensor_copy(
                r3(rt[f][:, gsl])[:, :, 0], r3(r_g[f][:, :])[:, :, 0]),
                reads=[r_g[f].reg], writes=[rt[f].reg])
            self.tt("v", k1_g[f], k1_g[f][:, :], k_g[f], k_g[f][:, :],
                    Ek[f], Ek[f][:, :], ALU.mult)
            self.ts("v", ku_g[f], ku_g[f][:, :], k_g[f], k_g[f][:, :],
                    u_t[:, f:f + 1], ALU.mult, s_regs=[u_t.reg])
            em.op("v", lambda v, f=f: v.tensor_tensor(
                r3(ku_g[f][:, :])[:, :, 1:64], r3(ku_g[f][:, :])[:, :, 1:64],
                r3(Ek[f][:, :])[:, :, 0:63], ALU.mult),
                reads=[ku_g[f].reg, Ek[f].reg], writes=[ku_g[f].reg])
        if upto == "trans":
            continue
        for ts in range(2):
            for f in range(NF):
                pv = B[BTR].h[:, 0:64].bitcast(BF16)
                em.op("t", lambda t, f=f, ts=ts, pv=pv: t.transpose(
                    pv, k1_g[f][:, 128 * ts:128 * (ts + 1)], self.ident[:, :]),
                    reads=[k1_g[f].reg, self.ident.reg], writes=[B[BTR].reg])
                self.em.op("s", lambda s, f=f, ts=ts, pv=pv: s.copy(
                    ktm_g[ts][:, 128 * f:128 * (f + 1)], pv),
                    reads=[B[BTR].reg], writes=[ktm_g[ts].reg])
                B[BTR].fresh = True
        if upto == "ktm":
            continue
        for cl in range(4):
            c = 4 * g + cl
            cpar = c % 2
            attC, mev, mev2 = attC2[cpar], mevA[cpar], mevB[cpar]
            o_ev = o_ev2[(c // 2) % 2]
            ts = cl // 2
            csl = slice(64 * cl, 64 * (cl + 1))
            vrow = slice(64 * cpar, 64 * (cpar + 1))
            srow = slice(64 * (1 - cpar), 64 * (2 - cpar))
            for h in range(H):
                ft, par = h // 2, h % 2
                hb = slice(64 * par, 64 * (par + 1))
                ba = B[BA0 + (h // 8)]
                col = slice(64 * (h % 8), 64 * (h % 8 + 1))
                rt_ap = rt[ft][hb, 64 * c:64 * (c + 1)]
                if c > 0:
                    self.mmx(ba, ba.h[srow, col], S_bf,
                             S_bf[hb, 64 * ft:64 * ft + 64], rt[ft], rt_ap,
                             tp=(64 * par, 64 * (1 - cpar)), stop=True,
                             pbase=64 * (1 - cpar))
                self.mmx(ba, ba.h[vrow, col], k1_g[ft], k1_g[ft][hb, csl],
                         rt[ft], rt_ap, tp=(64 * par, 64 * cpar), stop=True,
                         pbase=64 * cpar)
                b2 = B[B20 + (h // 8)]
                self.mmx(b2, b2.h[vrow, col], ku_g[ft], ku_g[ft][hb, csl],
                         rt[ft], rt_ap, tp=(64 * par, 64 * cpar), stop=True,
                         pbase=64 * cpar)
            for b in range(2):
                ba, b2 = B[BA0 + b], B[B20 + b]
                hc = slice(512 * b, 512 * (b + 1))
                if c > 0:
                    self.evac(ba, attC, attC[srow, hc], in_ap=ba.h[srow, :])
                    ba.fresh = False
                else:
                    em.op("v", lambda v, hc=hc, srow=srow, attC=attC: v.memset(
                        attC[srow, hc], 0.0), writes=[attC.reg])
                em.op("v", lambda v, ba=ba, vrow=vrow, mev=mev: v.tensor_tensor(
                    r3(mev[vrow, 0:512], c=8), r3(ba.h[vrow, :], c=8),
                    self.mstrict[vrow, :].unsqueeze(1).broadcast_to([64, 8, 64]),
                    ALU.mult),
                    reads=[ba.reg, self.mstrict.reg], writes=[mev.reg])
                em.op("v", lambda v, b2=b2, vrow=vrow, mev2=mev2: v.tensor_tensor(
                    r3(mev2[vrow, 0:512], c=8), r3(b2.h[vrow, :], c=8),
                    self.mdiag[vrow, :].unsqueeze(1).broadcast_to([64, 8, 64]),
                    ALU.mult),
                    reads=[b2.reg, self.mdiag.reg], writes=[mev2.reg])
                em.op("v", lambda v, hc=hc, vrow=vrow, attC=attC, mev=mev,
                      mev2=mev2: v.tensor_add(
                    attC[vrow, hc], mev[vrow, 0:512], mev2[vrow, 0:512]),
                    reads=[mev.reg, mev2.reg], writes=[attC.reg])
                ba.fresh = True
                b2.fresh = True
            for h in range(H):
                ft, par = h // 2, h % 2
                ob = B[BO0 + (h // 8)]
                ocol = slice(64 * (h % 8), 64 * (h % 8 + 1))
                hcol = slice(64 * h, 64 * (h + 1))
                if c > 0:
                    self.mmx(ob, ob.h[vrow, ocol], attC, attC[srow, hcol],
                             self.ident, self.ident[srow, srow],
                             tp=(64 * (1 - cpar), 64 * cpar), pbase=64 * cpar)
                self.mmx(ob, ob.h[vrow, ocol], attC, attC[vrow, hcol],
                         v_g[ts], v_g[ts][vrow, 64 * h:64 * h + 64],
                         tp=(64 * cpar, 64 * cpar), stop=(cpar == 1),
                         pbase=64 * cpar)
            for h in range(H):
                ft, par = h // 2, h % 2
                self.mmx(B[BM], B[BM].h[64 * par:64 * (par + 1),
                                        64 * ft:64 * ft + 64],
                         ktm_g[ts], ktm_g[ts][vrow, 64 * h:64 * h + 64],
                         v_g[ts], v_g[ts][vrow, 64 * h:64 * h + 64],
                         tp=(64 * cpar, 64 * par), stop=True, pbase=64 * par)
            em.op("v", lambda v: v.tensor_add(S_tmp[:, :], S_cur[:, :],
                                              B[BM].h[:, :]),
                  reads=[S_cur.reg, B[BM].reg], writes=[S_tmp.reg])
            B[BM].fresh = True
            em.op("v", lambda v, c=c: v.tensor_tensor(
                S_cur[:, :].rearrange("p (q l) -> p q l", q=8),
                S_tmp[:, :].rearrange("p (q l) -> p q l", q=8),
                eALt[:, 8 * c:8 * c + 8].unsqueeze(2).broadcast_to([128, 8, 64]),
                ALU.mult),
                reads=[S_tmp.reg, eALt.reg], writes=[S_cur.reg])
            self.copy("v", S_bf, S_bf[:, :], S_cur, S_cur[:, :])
            if cpar == 1:
                tt_idx = c // 2
                for b in range(2):
                    ob = B[BO0 + b]
                    self.evac(ob, o_ev, o_ev[:, 512 * b:512 * (b + 1)])
                em.dma("y", outacc.h[128 * tt_idx:128 * (tt_idx + 1), :],
                       o_ev[:, :], reads=[o_ev.reg], writes=[outacc.reg])
                if filler is not None:
                    filler(-1)
        if filler is not None:
            filler(g)
    self.release()


def emit_l0a_collective(self):
    em = self.em
    st = self.l0a
    S_cur, carry = st["S_cur"], st["carry"]
    S_in, S_inb, P_run, Dg = st["S_in"], st["S_inb"], st["P_run"], st["Dg"]
    cc_in, cc_out = st["cc_in"], st["cc_out"]
    self.act("s", Dg, Dg[:, :], carry, carry[:, :], ACT.Exp, scale=-1.0)
    em.dma("y", cc_in.h[:, 0:512], S_cur[:, :], reads=[S_cur.reg],
           writes=[cc_in.reg])
    em.dma("y", cc_in.h[:, 512:520], Dg[:, :], reads=[Dg.reg],
           writes=[cc_in.reg])
    self.em.collective("AllGather", ALU.bypass, ins=[cc_in.h[:, :]],
                       outs=[cc_out.h[:, :, :]], reads=[cc_in.reg],
                       writes=[cc_out.reg])
    self.mark()
    Sr = self.sb("cc_Sr", (128, 512), F32)
    Dr = self.sb("cc_Dr", (128, 8), F32)
    tmp = self.sb("cc_tmp", (128, 512), F32)
    oh = self.inputs["ohprev_mat"]
    oh_t = self.sb("cc_oh", (128, 8), F32)
    em.dma("y", oh_t[:, :], oh.h[:, :], reads=[oh.reg], writes=[oh_t.reg])
    em.op("v", lambda v: v.memset(P_run[:, :], 0.0), writes=[P_run.reg])
    em.op("v", lambda v: v.memset(S_in[:, :], 0.0), writes=[S_in.reg])
    for r in range(8):
        em.dma("y", Sr[:, :], cc_out.h[r, :, 0:512], reads=[cc_out.reg],
               writes=[Sr.reg])
        em.dma("y", Dr[:, :], cc_out.h[r, :, 512:520], reads=[cc_out.reg],
               writes=[Dr.reg])
        self.ts("v", tmp, tmp[:, :], P_run, P_run[:, :], oh_t[:, r:r + 1],
                ALU.mult, s_regs=[oh_t.reg])
        self.tt("v", S_in, S_in[:, :], S_in, S_in[:, :], tmp, tmp[:, :], ALU.add)
        em.op("v", lambda v: v.tensor_tensor(
            tmp[:, :].rearrange("p (q l) -> p q l", q=8),
            P_run[:, :].rearrange("p (q l) -> p q l", q=8),
            Dr[:, :].unsqueeze(2).broadcast_to([128, 8, 64]), ALU.mult),
            reads=[P_run.reg, Dr.reg], writes=[tmp.reg])
        self.tt("v", P_run, P_run[:, :], tmp, tmp[:, :], Sr, Sr[:, :], ALU.add)
    self.copy("v", S_inb, S_inb[:, :], S_in, S_in[:, :])
    self.release()


def emit_l0a_sweep2(self, xln, upto='all'):
    em, Tc, NCH, NT = self.em, self.Tc, self.NCH, self.NT
    st = self.l0a
    rt, Bcol = st["rt"], st["Bcol"]
    S_inb = st["S_inb"]
    outacc = st["outacc"]
    xsrc = self.xdram
    xdst = self.xdram_next()

    self.mark()
    if upto == "p1":
        self.release()
        return
    wgT = self.load_w_bf16("wgT", D, D)
    woT = self.load_w_bf16("woT", D, D)
    if upto == "p2":
        self.release()
        return
    lng = self.inputs["lng_rep"]
    lnb = self.inputs["lnb_rep"]
    lng_t = self.sb("lng_t", (128, D), F32)
    lnb_t = self.sb("lnb_t", (128, D), F32)
    em.dma("y", lng_t[:, :], lng.h[:, :], reads=[lng.reg], writes=[lng_t.reg])
    em.dma("y", lnb_t[:, :], lnb.h[:, :], reads=[lnb.reg], writes=[lnb_t.reg])
    if upto == "p3":
        self.release()
        return
    ebcT = [self.sb(f"ebcT{f}", (128, NCH), F32) for f in range(NF)]
    for f in range(NF):
        self.act("s", ebcT[f], ebcT[f][:, :], Bcol[f], Bcol[f][:, :], ACT.Exp,
                 scale=-1.0)
    if upto == "p4":
        self.release()
        return
    rtl_t = self.sb("rtl_t", (128, NF * 128), BF16)
    xt_t = self.sb("xt_t", (128, NF * 128), F32)
    xu_t = self.sb("xu_t", (128, NF * 128), F32)
    if upto == "prep":
        self.release()
        return
    DBG = False
    if DBG:
        d_ofull = self.dscratch("d_ofull", (128, D), F32)
        d_lnout = self.dscratch("d_lnout", (128, D), F32)
        d_g = self.dscratch("d_g", (128, D), F32)
        d_sin = self.dscratch("d_sin", (128, 512), F32)
        d_stat = self.dscratch("d_stat", (128, 64), F32)
        d_rtil = self.dscratch("d_rtil", (128, D), F32)
        d_lngT = self.dscratch("d_lngT", (128, 8 * 128), F32)
        d_psum = self.dscratch("d_psum", (128, 512), F32)
        dbg_sb = self.sb("dbg_sb", (128, 512), F32)
    oacc_t = self.sb("oacc_t", (128, D), F32)
    ofull = self.sb("ofull", (128, D), F32)
    g_t = self.sb("g_sig", (128, D), F32)
    stat = self.sb("stat", (128, 64), F32)
    sqv = self.sb("sqv", (128, D), F32)
    lnout = self.sb("lnout", (128, D), BF16)
    lngT = self.sb("lngT", (128, D), BF16)

    B = [Bank(t) for t in self.pb]
    BOUT0, BOUT1, BG0, BG1, BTR, BP0, BP1, BXX = range(8)

    for tt in range(NT):
        tsl = slice(128 * tt, 128 * (tt + 1))
        em.dma("y", oacc_t[:, :], outacc.h[tsl, :], reads=[outacc.reg],
               writes=[oacc_t.reg])
        em.dma("y", xt_t[:, :],
               xsrc.h.rearrange("(f p) t -> p f t", p=128)[:, :, tsl],
               reads=[xsrc.reg], writes=[xt_t.reg])
        for f in range(NF):
            em.op("v", lambda v, f=f, tt=tt: v.tensor_tensor(
                rtl_t[:, 128 * f:128 * (f + 1)].rearrange(
                    "p (c l) -> p c l", l=64),
                rt[f][:, 128 * tt:128 * (tt + 1)].rearrange(
                    "p (c l) -> p c l", l=64),
                ebcT[f][:, 2 * tt:2 * tt + 2].unsqueeze(2).broadcast_to(
                    [128, 2, 64]), ALU.mult),
                reads=[rt[f].reg, ebcT[f].reg], writes=[rtl_t.reg])
        for cl in range(2):
            c = 2 * tt + cl
            vrow = slice(64 * (c % 2), 64 * (c % 2 + 1))
            for h in range(H):
                ft, par = h // 2, h % 2
                hb = slice(64 * par, 64 * (par + 1))
                ob = B[BOUT0 + (h // 8)]
                ocol = slice(64 * (h % 8), 64 * (h % 8 + 1))
                self.mmx(ob, ob.h[vrow, ocol], rtl_t,
                         rtl_t[hb, 128 * ft + 64 * cl:128 * ft + 64 * cl + 64],
                         S_inb, S_inb[hb, 64 * ft:64 * ft + 64],
                         tp=(64 * par, 64 * (c % 2)), stop=True,
                         pbase=64 * (c % 2))
        if upto == "d1b":
            continue
        for n in range(2):
            nsl = slice(512 * n, 512 * (n + 1))
            for k in range(NF):
                self.mmx(B[BG0 + n], B[BG0 + n].h[:, 0:512], xln[k],
                         xln[k][:, tsl], wgT[k], wgT[k][:, nsl],
                         stop=(k == NF - 1))
            self.evac(B[BG0 + n], g_t, g_t[:, nsl], func=ACT.Sigmoid,
                      in_ap=B[BG0 + n].h[:, 0:512])
        for b in range(2):
            ob = B[BOUT0 + b]
            em.op("v", lambda v, b=b, ob=ob: v.tensor_add(
                ofull[:, 512 * b:512 * (b + 1)], ob.h[:, :],
                oacc_t[:, 512 * b:512 * (b + 1)]),
                reads=[ob.reg, oacc_t.reg], writes=[ofull.reg])
            ob.fresh = True
        if upto == "g":
            continue
        if DBG and tt == 0:
            em.dma("g", d_ofull.h[:, :], ofull[:, :], reads=[ofull.reg],
                   writes=[d_ofull.reg])
            em.dma("g", d_g.h[:, :], g_t[:, :], reads=[g_t.reg],
                   writes=[d_g.reg])
            em.dma("g", d_sin.h[:, :], st["S_inb"][:, :], reads=[st["S_inb"].reg],
                   writes=[d_sin.reg])
            em.dma("g", d_rtil.h[:, 0:Tc], rtil[0][:, :], reads=[rtil[0].reg],
                   writes=[d_rtil.reg])
        em.op("v", lambda v: v.reduce_sum(
            stat[:, 0:16], ofull[:, :].rearrange("p (h j) -> p h j", j=64), AX),
            reads=[ofull.reg], writes=[stat.reg])
        self.tt("v", sqv, sqv[:, :], ofull, ofull[:, :], ofull, ofull[:, :],
                ALU.mult)
        em.op("v", lambda v: v.reduce_sum(
            stat[:, 16:32], sqv[:, :].rearrange("p (h j) -> p h j", j=64), AX),
            reads=[sqv.reg], writes=[stat.reg])
        self.ts("v", stat, stat[:, 0:16], stat, stat[:, 0:16], 1.0 / 64, ALU.mult)
        self.ts("v", stat, stat[:, 16:32], stat, stat[:, 16:32], 1.0 / 64,
                ALU.mult)
        em.op("v", lambda v: v.scalar_tensor_tensor(
            stat[:, 32:48], stat[:, 0:16], 1.0, stat[:, 0:16], ALU.mult,
            ALU.mult), reads=[stat.reg], writes=[stat.reg])
        self.tt("v", stat, stat[:, 16:32], stat, stat[:, 16:32], stat,
                stat[:, 32:48], ALU.subtract)
        self.ts("v", stat, stat[:, 16:32], stat, stat[:, 16:32], EPS, ALU.add)
        self.act("s", stat, stat[:, 32:48], stat, stat[:, 16:32], ACT.Sqrt)
        self.rsqrt_refine(stat, stat[:, 48:64], stat, stat[:, 32:48],
                          stat[:, 16:32])
        self.tt("v", oacc_t, oacc_t[:, :], lnb_t, lnb_t[:, :], g_t, g_t[:, :],
                ALU.mult)
        self.tt("v", g_t, g_t[:, :], lng_t, lng_t[:, :], g_t, g_t[:, :],
                ALU.mult)
        em.op("v", lambda v: v.tensor_tensor(
            sqv[:, :].rearrange("p (h j) -> p h j", j=64),
            ofull[:, :].rearrange("p (h j) -> p h j", j=64),
            stat[:, 0:16].unsqueeze(2).broadcast_to([128, 16, 64]),
            ALU.subtract), reads=[ofull.reg, stat.reg], writes=[sqv.reg])
        em.op("v", lambda v: v.tensor_tensor(
            sqv[:, :].rearrange("p (h j) -> p h j", j=64),
            sqv[:, :].rearrange("p (h j) -> p h j", j=64),
            stat[:, 48:64].unsqueeze(2).broadcast_to([128, 16, 64]),
            ALU.mult), reads=[sqv.reg, stat.reg], writes=[sqv.reg])
        em.op("v", lambda v: v.tensor_tensor(
            sqv[:, :], sqv[:, :], g_t[:, :], ALU.mult),
            reads=[sqv.reg, g_t.reg], writes=[sqv.reg])
        self.tt("v", lnout, lnout[:, :], sqv, sqv[:, :], oacc_t, oacc_t[:, :],
                ALU.add)
        if upto == "ln":
            continue
        if DBG and tt == 0:
            em.dma("g", d_lnout.h[:, :], lnout[:, :], reads=[lnout.reg],
                   writes=[d_lnout.reg])
            em.dma("g", d_stat.h[:, :], stat[:, :], reads=[stat.reg],
                   writes=[d_stat.reg])
        for k in range(NF):
            pv = B[BTR].h[:, 0:64].bitcast(BF16)
            em.op("t", lambda t, k=k, pv=pv: t.transpose(
                pv, lnout[:, 128 * k:128 * (k + 1)], self.ident[:, :]),
                reads=[lnout.reg, self.ident.reg], writes=[B[BTR].reg])
            self.em.op("s", lambda s, k=k, pv=pv: s.copy(
                lngT[:, 128 * k:128 * (k + 1)], pv),
                reads=[B[BTR].reg], writes=[lngT.reg])
        if upto == "trans":
            continue
        if DBG and tt == 0:
            em.dma("g", d_lngT.h[:, :], lngT[:, :], reads=[lngT.reg],
                   writes=[d_lngT.reg])
        for o in range(NF):
            pbk = B[BP0 + (o // 4)]
            ocol = slice(128 * (o % 4), 128 * (o % 4 + 1))
            for k in range(NF):
                self.mmx(pbk, pbk.h[:, ocol], woT[k],
                         woT[k][:, 128 * o:128 * (o + 1)], lngT,
                         lngT[:, 128 * k:128 * (k + 1)], stop=(k == NF - 1))
        if DBG and tt == 0:
            self.em.op("s", lambda s: s.copy(dbg_sb[:, :], B[BP0].h[:, :]),
                       reads=[B[BP0].reg], writes=[dbg_sb.reg])
            em.dma("g", d_psum.h[:, :], dbg_sb[:, :], reads=[dbg_sb.reg],
                   writes=[d_psum.reg])
        for b in range(2):
            pbk = B[BP0 + b]
            em.op("v", lambda v, b=b, pbk=pbk: v.tensor_tensor(
                xu_t[:, 512 * b:512 * (b + 1)], pbk.h[:, :],
                xt_t[:, 512 * b:512 * (b + 1)], ALU.add),
                reads=[pbk.reg, xt_t.reg], writes=[xu_t.reg])
        em.dma("y", xdst.h.rearrange("(f p) t -> p f t", p=128)[:, :, tsl],
               xu_t[:, :], reads=[xu_t.reg], writes=[xdst.reg])
        B[BP0].fresh = True
        B[BP1].fresh = True
    self.xdram = xdst
    self.release()


MK.emit_l0a_sweep1 = emit_l0a_sweep1
MK.emit_l0a_collective = emit_l0a_collective
MK.emit_l0a_sweep2 = emit_l0a_sweep2


QMAX = 32767.0
MAGIC = 12582912.0


def _gmax(self, tag, loc_col, loc_t, nvals, cc_sem_note=""):
    em = self.em
    b1 = self.dscratch(f"gm_b1_{tag}", (128,), F32)
    cin = self.dscratch(f"gm_ci_{tag}", (nvals,), F32)
    cout = self.dscratch(f"gm_co_{tag}", (8, nvals), F32)
    t2 = self.sb(f"gm_t2_{tag}", (128, 128), F32)
    res = self.sb(f"gm_res_{tag}", (128, max(nvals, 8)), F32)
    for v in range(nvals):
        em.dma("y", b1.h[:], loc_col[:, v:v + 1], reads=[loc_t.reg],
               writes=[b1.reg])
        em.dma("y", t2[:, :], b1.h[:].unsqueeze(0).partition_broadcast(128),
               reads=[b1.reg], writes=[t2.reg])
        em.op("v", lambda v_, v2=v: v_.reduce_max(res[:, v2:v2 + 1], t2[:, :],
                                                  AX),
              reads=[t2.reg], writes=[res.reg])
    em.dma("y", cin.h[:], res[0:1, 0:nvals], reads=[res.reg], writes=[cin.reg])
    self.em.collective("AllGather", ALU.bypass, ins=[cin.h[:]],
                       outs=[cout.h[:, :]], reads=[cin.reg], writes=[cout.reg])
    t3 = self.sb(f"gm_t3_{tag}", (128, 8 * nvals), F32)
    em.dma("y", t3[:, :],
           cout.h[:, :].flatten().unsqueeze(0).partition_broadcast(128),
           reads=[cout.reg], writes=[t3.reg])
    out = self.sb(f"gm_out_{tag}", (128, nvals), F32)
    em.op("v", lambda v_: v_.reduce_max(
        out[:, 0:nvals].unsqueeze(2),
        t3[:, :].rearrange("p (r v) -> p v r", v=nvals), AX),
        reads=[t3.reg], writes=[out.reg])
    return out


def _qdq_scales(self, sc, c0, gmax_t, col, qmax=QMAX):
    self.em.op("v", lambda v: v.tensor_scalar(
        sc[:, c0:c0 + 1], col, 1.0 / qmax, 1e-12, ALU.mult, ALU.max),
        reads=[gmax_t.reg], writes=[sc.reg])
    self.em.op("v", lambda v: v.reciprocal(sc[:, c0 + 1:c0 + 2],
                                           sc[:, c0:c0 + 1]),
               reads=[sc.reg], writes=[sc.reg])


DQW = {"g": ("wgateN", "wgateT"), "u": ("wupN", "wupT"),
       "d": ("wdownN", "wdownT")}


def emit_dq_scales(self, keys=("g", "u", "d")):
    em = self.em
    if not hasattr(self, "dqctx"):
        self.dqg_d = self.dscratch("dqm_g", (D, FI), BF16)
        self.dqu_d = self.dscratch("dqm_u", (D, FI), BF16)
        self.dqd_d = self.dscratch("dqm_d", (FI, D), BF16)
        self.dqctx = {}
    bank = Bank(self.pb[7])
    self.mark()
    shapes = {"g": (self.dqg_d, FI, D), "u": (self.dqu_d, FI, D),
              "d": (self.dqd_d, D, FI)}
    IMX = max(shapes[k][2] for k in keys)
    wb2 = [self.sb(f"dqs_w{i}_{keys[0]}", (128, IMX), F32) for i in range(2)]
    mbuf = self.sb(f"dqs_m_{keys[0]}", (128, IMX // 32), F32)
    sall = self.sb(f"dqs_s_{keys[0]}", (128, 768), F32)
    s2all = self.sb(f"dqs_s2_{keys[0]}", (128, 768), F32)
    sb_ev = self.sb(f"dqs_ev_{keys[0]}", (128, 128), F32)
    for wkey in keys:
        wnat_n, wT_n = DQW[wkey]
        out_d, O, I = shapes[wkey]
        nat = self.inputs[wnat_n]
        nb = I // 32
        not_ = O // 128
        invT_d = self.dscratch(f"dqs_iT_{wkey}", (nb, O), F32)
        sT_d = self.dscratch(f"dqs_sT_{wkey}", (nb, O), F32)
        self.dqctx[wkey] = (self.inputs[wT_n], invT_d, sT_d, out_d, O, I)
        for ot in range(not_):
            rows = slice(128 * ot, 128 * (ot + 1))
            wbuf = wb2[ot % 2]
            em.dma("y", wbuf[:, 0:I], nat.h[rows, :], reads=[nat.reg],
                   writes=[wbuf.reg])
            em.op("v", lambda v, nb=nb, I=I, wbuf=wbuf: v.reduce_max(
                mbuf[:, 0:nb],
                wbuf[:, 0:I].rearrange("p (b q) -> p b q", q=32), AX,
                apply_absolute_value=True),
                reads=[wbuf.reg], writes=[mbuf.reg])
            em.op("v", lambda v, ot=ot, nb=nb: v.tensor_scalar(
                s2all[:, nb * ot:nb * (ot + 1)], mbuf[:, 0:nb], 1.0 / 7.0,
                1e-12, ALU.mult, ALU.max),
                reads=[mbuf.reg], writes=[s2all.reg])
            em.op("v", lambda v, ot=ot, nb=nb: v.reciprocal(
                sall[:, nb * ot:nb * (ot + 1)],
                s2all[:, nb * ot:nb * (ot + 1)]),
                reads=[s2all.reg], writes=[sall.reg])
        for buf, dst in ((sall, invT_d), (s2all, sT_d)):
            for ot in range(not_):
                rows = slice(128 * ot, 128 * (ot + 1))
                em.op("t", lambda t, buf=buf, ot=ot, nb=nb: t.transpose(
                    bank.h[0:nb, 0:128], buf[0:128, nb * ot:nb * (ot + 1)],
                    self.identf[:, :]),
                    reads=[buf.reg, self.identf.reg], writes=[bank.reg])
                em.op("s", lambda s, nb=nb: s.copy(sb_ev[0:nb, 0:128],
                                                   bank.h[0:nb, 0:128]),
                      reads=[bank.reg], writes=[sb_ev.reg])
                bank.fresh = True
                em.dma("y", dst.h[:, rows], sb_ev[0:nb, 0:128],
                       reads=[sb_ev.reg], writes=[dst.reg])
    self.release()


def emit_dq_piece_bufs(self):
    OCW = 512
    self.dqpb = (self.sb("dqp_wt", (128, OCW), F32),
                 self.sb("dqp_iv", (128, OCW), F32),
                 self.sb("dqp_sb", (128, OCW), F32),
                 self.sb("dqp_ob", (128, OCW), BF16), OCW)


def emit_dq_wt_pieces(self, wkey):
    em = self.em
    wT, invT_d, sT_d, out_d, O, I = self.dqctx[wkey]
    wtb, invb, sbb, ob, OCW = self.dqpb
    nkt = I // 128
    noc = O // OCW
    nbk = 128 // 32
    pieces = []

    def mk(k, oc):
        def go():
            rows = slice(128 * k, 128 * (k + 1))
            ocs = slice(OCW * oc, OCW * (oc + 1))
            em.dma("y", wtb[:, :], wT.h[rows, ocs], reads=[wT.reg],
                   writes=[wtb.reg])
            em.dma("y", invb[:, :],
                   invT_d.h[nbk * k:nbk * (k + 1), ocs].unsqueeze(1)
                   .broadcast_to([nbk, 32, OCW]),
                   reads=[invT_d.reg], writes=[invb.reg])
            em.dma("y", sbb[:, :],
                   sT_d.h[nbk * k:nbk * (k + 1), ocs].unsqueeze(1)
                   .broadcast_to([nbk, 32, OCW]),
                   reads=[sT_d.reg], writes=[sbb.reg])
            em.op("v", lambda v: v.tensor_mul(wtb[:, :], wtb[:, :],
                                              invb[:, :]),
                  reads=[wtb.reg, invb.reg], writes=[wtb.reg])
            em.op("v", lambda v: v.tensor_scalar(
                invb[:, :], wtb[:, :], MAGIC, MAGIC, ALU.add, ALU.subtract),
                reads=[wtb.reg], writes=[invb.reg])
            em.op("v", lambda v: v.tensor_tensor(
                ob[:, :], invb[:, :], sbb[:, :], ALU.mult),
                reads=[invb.reg, sbb.reg], writes=[ob.reg])
            em.dma("y", out_d.h[rows, ocs], ob[:, :], reads=[ob.reg],
                   writes=[out_d.reg])
        return go

    for k in range(nkt):
        for oc in range(noc):
            pieces.append(mk(k, oc))
    return pieces


def emit_l0f(self, xq, wt_pieces=None):
    em, Tc, NT = self.em, self.Tc, self.NT
    B = [Bank(t) for t in self.pb]
    gate_d = self.dscratch("gate_raw", (FI, Tc), F32)
    up_d = self.dscratch("up_raw", (FI, Tc), F32)
    g2_d = self.dscratch("g2_raw", (FI, Tc), F32)
    o_d = self.dscratch("o_raw", (FI, Tc), F32)
    xdst = self.xdram_next()
    xsrc = self.xdram

    NO = FI // 128
    nch = (Tc + 511) // 512
    CW = min(512, Tc)
    SC = self.sb("l0f_scales", (128, 16), F32)
    sc_g, sc_u, sc_g2, sc_o = SC[:, 0:2], SC[:, 2:4], SC[:, 4:6], SC[:, 6:8]

    self.mark()
    dqT = self.sbs("dq_gT", (128, FI), BF16, NF)
    locmax = self.sb("l0f_locmax", (128, 2), F32)
    em.op("v", lambda v: v.memset(locmax[:, :], 0.0), writes=[locmax.reg])
    ev2 = [self.sb(f"l0f_ev{i}", (128, Tc), F32) for i in range(2)]
    mcol = self.sb("l0f_mcol", (128, NO), F32)
    up_list, dn_list = wt_pieces if wt_pieces is not None else ([], [])
    for wi, (dq_d, dst_d, mc) in enumerate(
            ((self.dqg_d, gate_d, 0), (self.dqu_d, up_d, 1))):
        if wi == 1:
            while up_list:
                up_list.pop(0)()
        for k in range(NF):
            em.dma("y", dqT[k][:, :], dq_d.h[128 * k:128 * (k + 1), :],
                   reads=[dq_d.reg], writes=[dqT[k].reg])
        drip = up_list if wi == 0 else dn_list
        for o in range(NO):
            ev = ev2[o % 2]
            for c in range(nch):
                tsl = slice(CW * c, CW * (c + 1))
                bk = B[c % 2]
                for k in range(NF):
                    self.mmx(bk, bk.h[:, 0:CW], dqT[k],
                             dqT[k][:, 128 * o:128 * (o + 1)], xq[k],
                             xq[k][:, tsl], stop=(k == NF - 1))
                self.evac(bk, ev, ev[:, tsl], in_ap=bk.h[:, 0:CW])
            em.op("v", lambda v, o=o, ev=ev: v.reduce_max(
                mcol[:, o:o + 1], ev[:, :], AX, apply_absolute_value=True),
                reads=[ev.reg], writes=[mcol.reg])
            em.dma("y", dst_d.h[128 * o:128 * (o + 1), :], ev[:, :],
                   reads=[ev.reg], writes=[dst_d.reg])
            for _ in range(2):
                if drip:
                    drip.pop(0)()
        em.op("v", lambda v, mc=mc: v.reduce_max(
            locmax[:, mc:mc + 1], mcol[:, :], AX),
            reads=[mcol.reg], writes=[locmax.reg])
    gm = _gmax(self, "q23", locmax[:, 0:2], locmax, 2)
    _qdq_scales(self, SC, 0, gm, gm[:, 0:1])
    _qdq_scales(self, SC, 2, gm, gm[:, 1:2])
    self.em.op("v", lambda v: v.tensor_scalar(
        SC[:, 8:9], SC[:, 0:1], 1.0 / 65536.0, None, ALU.mult),
        reads=[SC.reg], writes=[SC.reg])

    t_f2 = [self.sb(f"l0f_tf{i}", (128, Tc), F32) for i in range(2)]
    kg2 = [self.sb(f"l0f_kg{i}", (128, Tc), F32) for i in range(2)]
    ms2 = [self.sb(f"l0f_ms{i}", (128, Tc), F32) for i in range(2)]
    sigf2 = [self.sb(f"l0f_sig{i}", (128, Tc), F32) for i in range(2)]
    em.op("v", lambda v: v.memset(locmax[:, 0:1], 0.0), writes=[locmax.reg])
    for o in range(NO):
        ev, t_f, kg, ms, sigf = (ev2[o % 2], t_f2[o % 2], kg2[o % 2],
                                 ms2[o % 2], sigf2[o % 2])
        rows = slice(128 * o, 128 * (o + 1))
        em.dma("s", ev[:, :], gate_d.h[rows, :], reads=[gate_d.reg],
               writes=[ev.reg])
        self.ts("v", t_f, t_f[:, :], ev, ev[:, :], SC[:, 1:2], ALU.mult,
                s_regs=[SC.reg])
        self.ts("v", kg, kg[:, :], t_f, t_f[:, :], MAGIC, ALU.add, MAGIC,
                ALU.subtract)
        em.op("s", lambda s, sigf=sigf, kg=kg: s.activation(
            sigf[:, :], kg[:, :], ACT.Sigmoid, scale=SC[:, 0:1]),
              reads=[kg.reg, SC.reg], writes=[sigf.reg])
        self.act("s", t_f, t_f[:, :], sigf, sigf[:, :], ACT.Relu,
                 scale=65536.0, bias=self.ccol(MAGIC),
                 extra_reads=[self.ccol_reg])
        self.act("s", ms, ms[:, :], t_f, t_f[:, :], ACT.Relu,
                 bias=self.ccol(-MAGIC), extra_reads=[self.ccol_reg])
        self.ts("v", t_f, t_f[:, :], ms, ms[:, :], 65535.0, ALU.min)
        em.op("v", lambda v, ev=ev, t_f=t_f, kg=kg: v.scalar_tensor_tensor(
            ev[:, :], t_f[:, :], SC[:, 8:9], kg[:, :], ALU.mult, ALU.mult),
            reads=[t_f.reg, SC.reg, kg.reg], writes=[ev.reg])
        em.op("v", lambda v, o=o, ev=ev: v.reduce_max(
            mcol[:, o:o + 1], ev[:, :], AX, apply_absolute_value=True),
            reads=[ev.reg], writes=[mcol.reg])
        em.dma("y", g2_d.h[rows, :], ev[:, :], reads=[ev.reg], writes=[g2_d.reg])
    em.op("v", lambda v: v.reduce_max(locmax[:, 0:1], mcol[:, :], AX),
          reads=[mcol.reg], writes=[locmax.reg])
    gm4 = _gmax(self, "q4", locmax[:, 0:1], locmax, 1)
    _qdq_scales(self, SC, 4, gm4, gm4[:, 0:1])
    em.op("v", lambda v: v.tensor_tensor(SC[:, 9:10], SC[:, 4:5],
                                         SC[:, 2:3], ALU.mult),
          reads=[SC.reg], writes=[SC.reg])

    em.op("v", lambda v: v.memset(locmax[:, 0:1], 0.0), writes=[locmax.reg])
    for o in range(NO):
        ev, t_f, kg, ms = ev2[o % 2], t_f2[o % 2], kg2[o % 2], ms2[o % 2]
        rows = slice(128 * o, 128 * (o + 1))
        em.dma("s", ev[:, :], g2_d.h[rows, :], reads=[g2_d.reg], writes=[ev.reg])
        self.act("s", t_f, t_f[:, :], ev, ev[:, :], ACT.Relu,
                 scale=SC[:, 5:6], bias=self.ccol(MAGIC),
                 extra_reads=[SC.reg, self.ccol_reg])
        self.ts("v", kg, kg[:, :], t_f, t_f[:, :], -MAGIC, ALU.add)
        em.dma("s", ev[:, :], up_d.h[rows, :], reads=[up_d.reg], writes=[ev.reg])
        self.ts("v", t_f, t_f[:, :], ev, ev[:, :], SC[:, 3:4], ALU.mult,
                s_regs=[SC.reg])
        self.ts("v", ms, ms[:, :], t_f, t_f[:, :], MAGIC, ALU.add, MAGIC,
                ALU.subtract)
        em.op("v", lambda v, ev=ev, kg=kg, ms=ms: v.scalar_tensor_tensor(
            ev[:, :], kg[:, :], SC[:, 9:10], ms[:, :], ALU.mult, ALU.mult),
            reads=[kg.reg, SC.reg, ms.reg], writes=[ev.reg])
        em.op("v", lambda v, o=o, ev=ev: v.reduce_max(
            mcol[:, o:o + 1], ev[:, :], AX, apply_absolute_value=True),
            reads=[ev.reg], writes=[mcol.reg])
        em.dma("y", o_d.h[rows, :], ev[:, :], reads=[ev.reg], writes=[o_d.reg])
    em.op("v", lambda v: v.reduce_max(locmax[:, 0:1], mcol[:, :], AX),
          reads=[mcol.reg], writes=[locmax.reg])
    gm5 = _gmax(self, "q5", locmax[:, 0:1], locmax, 1)
    _qdq_scales(self, SC, 6, gm5, gm5[:, 0:1])
    self.release()

    self.mark()
    while dn_list:
        dn_list.pop(0)()
    dqdT = self.sbs("dq_dT", (128, D), BF16, FI // 128)
    for k in range(FI // 128):
        em.dma("y", dqdT[k][:, :], self.dqd_d.h[128 * k:128 * (k + 1), :],
               reads=[self.dqd_d.reg], writes=[dqdT[k].reg])
    oq = self.sbs("l0f_oq", (128, CW), BF16, FI // 128)
    tq2 = [self.sb(f"l0f_tq{i}", (128, CW), F32) for i in range(2)]
    q22 = [self.sb(f"l0f_q2{i}", (128, CW), F32) for i in range(2)]
    xt2 = [self.sb(f"l0f_xt{i}", (128, NF * CW), F32) for i in range(2)]
    xu2 = [self.sb(f"l0f_xu{i}", (128, NF * CW), F32) for i in range(2)]
    for c in range(nch):
        tsl = slice(CW * c, CW * (c + 1))
        xt, xu = xt2[c % 2], xu2[c % 2]
        for k in range(FI // 128):
            tq, q2 = tq2[k % 2], q22[k % 2]
            em.dma("s", tq[:, :], o_d.h[128 * k:128 * (k + 1), tsl],
                   reads=[o_d.reg], writes=[tq.reg])
            self.ts("v", tq, tq[:, :], tq, tq[:, :], SC[:, 7:8], ALU.mult,
                    s_regs=[SC.reg])
            self.ts("v", q2, q2[:, :], tq, tq[:, :], MAGIC, ALU.add, MAGIC,
                    ALU.subtract)
            self.ts("v", oq[k], oq[k][:, :], q2, q2[:, :], SC[:, 6:7],
                    ALU.mult, s_regs=[SC.reg])
        em.dma("y", xt[:, :],
               xsrc.h.rearrange("(f p) t -> p f t", p=128)[:, :, tsl],
               reads=[xsrc.reg], writes=[xt.reg])
        for o2 in range(NF):
            bk = B[o2 % 2]
            for k in range(FI // 128):
                self.mmx(bk, bk.h[:, 0:CW], dqdT[k],
                         dqdT[k][:, 128 * o2:128 * (o2 + 1)], oq[k],
                         oq[k][:, :], stop=(k == FI // 128 - 1))
            em.op("v", lambda v, o2=o2, bk=bk, xu=xu, xt=xt: v.tensor_tensor(
                xu[:, CW * o2:CW * (o2 + 1)], bk.h[:, 0:CW],
                xt[:, CW * o2:CW * (o2 + 1)], ALU.add),
                reads=[bk.reg, xt.reg], writes=[xu.reg])
            bk.fresh = True
        em.dma("y",
               xdst.h.rearrange("(f p) t -> p f t", p=128)[:, :, tsl],
               xu[:, :], reads=[xu.reg], writes=[xdst.reg])
    self.xdram = xdst
    self.release()


def emit_q1(self):
    em, Tc = self.em, self.Tc
    xq = self.sbs("xq", (128, Tc), BF16, NF)
    self.mark()
    xlnf = self.sbs("q1_x", (128, Tc), F32, NF)
    self.emit_ln("2", xln=None, want_f32=True, want_bf16=False,
                 X_pre=xlnf, f32_inplace=True)
    locmax = self.sb("q1_locmax", (128, NF), F32)
    for f in range(NF):
        em.op("v", lambda v, f=f: v.reduce_max(
            locmax[:, f:f + 1], xlnf[f][:, :], AX, apply_absolute_value=True),
            reads=[xlnf[f].reg], writes=[locmax.reg])
    lm2 = self.sb("q1_lm2", (128, 1), F32)
    em.op("v", lambda v: v.reduce_max(lm2[:, :], locmax[:, :], AX),
          reads=[locmax.reg], writes=[lm2.reg])
    gm = _gmax(self, "q1", lm2[:, 0:1], lm2, 1)
    sc = self.sb("q1_sc", (128, 2), F32)
    _qdq_scales(self, sc, 0, gm, gm[:, 0:1])
    ti = self.sb("q1_ti", (128, Tc), F32)
    tf = self.sb("q1_tf", (128, Tc), F32)
    for f in range(NF):
        self.ts("v", tf, tf[:, :], xlnf[f], xlnf[f][:, :], sc[:, 1:2],
                ALU.mult, s_regs=[sc.reg])
        self.ts("v", ti, ti[:, :], tf, tf[:, :], MAGIC, ALU.add, MAGIC,
                ALU.subtract)
        self.ts("v", xq[f], xq[f][:, :], ti, ti[:, :], sc[:, 0:1], ALU.mult,
                s_regs=[sc.reg])
    self.release()
    return xq


MK.emit_l0f = emit_l0f
MK.emit_q1 = emit_q1
MK.emit_dq_scales = emit_dq_scales
MK.emit_dq_piece_bufs = emit_dq_piece_bufs
MK.emit_dq_wt_pieces = emit_dq_wt_pieces



INV2PI = 1.0 / (2.0 * np.pi)
TWO_PI = 2.0 * np.pi
MAGIC = 12582912.0


def emit_l1a(self, xln):
    em, Tc, NT = self.em, self.Tc, self.NT
    B = [Bank(t) for t in self.pb]
    nch = (Tc + 511) // 512
    CW = min(512, Tc)
    xsrc = self.xdram
    xdst = self.xdram_next()
    cc_in = self.dscratch("cc_l1a_in", (128, 16 * 65), F32)
    cc_out = self.dscratch("cc_l1a_out", (128, 16 * 65), F32)
    zd = self.dscratch("l1a_zd", (H, Tc), F32)
    zdi = self.dscratch("l1a_zdi", (H, Tc), F32)

    phb = self.sb("l1a_phb", (128, 1), F32)
    em.op("g", lambda g: g.memset(phb[0:64, :], 0.25), writes=[phb.reg])
    em.op("g", lambda g: g.memset(phb[64:128, :], 0.0), writes=[phb.reg])
    pc = self.sb("l1a_pc", (128, 128), F32)
    em.op("g", lambda g: g.memset(pc[:, 0:64], 0.25), writes=[pc.reg])
    em.op("g", lambda g: g.memset(pc[:, 64:128], 0.0), writes=[pc.reg])
    rf2 = self.load_w_bf16("rf2", 128, 128)[0]
    kvs = self.sb("l1a_kvs", (128, H * 65), F32)
    kvb = self.sb("l1a_kvb", (128, H * 65), BF16)
    attn = self.sbs("l1a_attn", (128, Tc), BF16, NF)
    onesb = self.ones_col

    self.mark()
    Qp = self.sbs("l1a_qp", (128, Tc), BF16, H)
    self.mark()
    V_tm = self.sbs("l1a_v", (128, D), BF16, NT)
    self.mark()
    wv = self.load_w_bf16("l1_Wv", D, D)
    for tt in range(NT):
        tsl = slice(128 * tt, 128 * (tt + 1))
        for n in range(2):
            nsl = slice(512 * n, 512 * (n + 1))
            bk = B[6]
            for k in range(NF):
                self.mmx(bk, bk.h[:, 0:512], xln[k], xln[k][:, tsl],
                         wv[k], wv[k][:, nsl], stop=(k == NF - 1))
            self.evac(bk, V_tm[tt], V_tm[tt][:, nsl], in_ap=bk.h[:, 0:512])
    self.release()
    wk = self.load_w_bf16("l1_Wk", D, D)
    Kpair2 = [self.sb(f"l1a_kpair{i}", (128, Tc), BF16) for i in range(2)]
    QN = 4 if NT % 4 == 0 else 2
    PW = Tc // QN
    u1 = self.sb("l1a_u1", (128, PW), F32)
    u22 = [self.sb(f"l1a_u2{i}", (128, PW), F32) for i in range(2)]
    kp_h2 = [self.sb(f"l1a_kph{i}", (128, Tc), BF16) for i in range(2)]
    for hp in range(H // 2):
        Kpair = Kpair2[hp % 2]
        jsl = slice(128 * hp, 128 * (hp + 1))
        for c in range(nch):
            tsl = slice(CW * c, CW * (c + 1))
            bk = B[6]
            for k in range(NF):
                self.mmx(bk, bk.h[:, 0:CW], wk[k], wk[k][:, jsl],
                         xln[k], xln[k][:, tsl], stop=(k == NF - 1))
            self.evac(bk, Kpair, Kpair[:, tsl], in_ap=bk.h[:, 0:CW])
        for par in range(2):
            h = 2 * hp + par
            hb = slice(64 * par, 64 * (par + 1))
            kp_h = kp_h2[par]
            for q in range(QN):
                u2 = u22[q % 2]
                hsl = slice(PW * q, PW * (q + 1))
                for tl in range(NT // QN):
                    tt = q * (NT // QN) + tl
                    tsl = slice(128 * tt, 128 * (tt + 1))
                    bk = B[tl % 2]
                    self.mmx(bk, bk.h[:, 0:128], Kpair, Kpair[hb, tsl],
                             rf2, rf2[hb, :], stop=True, tp=(64 * par, 0))
                    em.op("v", lambda v, tl=tl, bk=bk, u1=u1:
                          v.scalar_tensor_tensor(
                        u1[:, 128 * tl:128 * (tl + 1)], bk.h[:, 0:128], INV2PI,
                        pc[:, :], ALU.mult, ALU.add),
                        reads=[bk.reg, pc.reg], writes=[u1.reg])
                    bk.fresh = True
                em.op("v", lambda v, u1=u1, u2=u2: v.tensor_scalar(
                    u2[:, :], u1[:, :], MAGIC, MAGIC, ALU.add, ALU.subtract),
                    reads=[u1.reg], writes=[u2.reg])
                em.op("v", lambda v, u1=u1, u2=u2: v.tensor_sub(
                    u2[:, :], u1[:, :], u2[:, :]),
                    reads=[u1.reg, u2.reg], writes=[u2.reg])
                em.op("s", lambda s, hsl=hsl, kp_h=kp_h, u2=u2: s.activation(
                    kp_h[:, hsl], u2[:, :], ACT.Sin, scale=TWO_PI),
                    reads=[u2.reg], writes=[kp_h.reg])
            bkv = B[2 + par]
            for tt in range(NT):
                tsl = slice(128 * tt, 128 * (tt + 1))
                self.mmx(bkv, bkv.h[:, 0:64], kp_h, kp_h[:, tsl],
                         V_tm[tt], V_tm[tt][:, 64 * h:64 * (h + 1)],
                         stop=(tt == NT - 1))
                self.mmx(bkv, bkv.h[:, 64:65], kp_h, kp_h[:, tsl],
                         onesb, onesb[:, 0:1], stop=(tt == NT - 1))
            self.evac(bkv, kvs, kvs[:, 65 * h:65 * (h + 1)],
                      in_ap=bkv.h[:, 0:65])
    self.release()

    em.dma("y", cc_in.h[:, :], kvs[:, :], reads=[kvs.reg], writes=[cc_in.reg])
    self.em.collective("AllReduce", ALU.add, ins=[cc_in.h[:, :]],
                       outs=[cc_out.h[:, :]], reads=[cc_in.reg],
                       writes=[cc_out.reg])
    em.dma("y", kvs[:, :], cc_out.h[:, :], reads=[cc_out.reg],
           writes=[kvs.reg])
    self.copy("v", kvb, kvb[:, :], kvs, kvs[:, :])

    self.mark()
    wq = self.load_w_bf16("l1_Wq", D, D)
    Qpair2 = [self.sb(f"l1a_qpair{i}", (128, Tc), BF16) for i in range(2)]
    HT = Tc // 2
    u1q = self.sb("l1a_u1q", (128, HT), F32)
    u2q2 = [self.sb(f"l1a_u2q{i}", (128, HT), F32) for i in range(2)]
    for hp in range(H // 2):
        Qpair = Qpair2[hp % 2]
        jsl = slice(128 * hp, 128 * (hp + 1))
        for c in range(nch):
            tsl = slice(CW * c, CW * (c + 1))
            bk = B[6]
            for k in range(NF):
                self.mmx(bk, bk.h[:, 0:CW], wq[k], wq[k][:, jsl],
                         xln[k], xln[k][:, tsl], stop=(k == NF - 1))
            self.evac(bk, Qpair, Qpair[:, tsl], in_ap=bk.h[:, 0:CW])
        for par in range(2):
            h = 2 * hp + par
            hb = slice(64 * par, 64 * (par + 1))
            for half in range(2):
                u2q = u2q2[half]
                hsl = slice(HT * half, HT * (half + 1))
                ncw = max(1, HT // CW)
                for cl in range(ncw):
                    w = min(CW, HT)
                    tsl = slice(HT * half + w * cl, HT * half + w * (cl + 1))
                    bk = B[cl % 2]
                    self.mmx(bk, bk.h[:, 0:w], rf2, rf2[hb, :],
                             Qpair, Qpair[hb, tsl], stop=True, tp=(64 * par, 0))
                    em.op("v", lambda v, w=w, cl=cl, bk=bk, u1q=u1q:
                          v.tensor_scalar(
                        u1q[:, w * cl:w * (cl + 1)], bk.h[:, 0:w], INV2PI,
                        phb[:, 0:1], ALU.mult, ALU.add),
                        reads=[bk.reg, phb.reg], writes=[u1q.reg])
                    bk.fresh = True
                em.op("v", lambda v, u1q=u1q, u2q=u2q: v.tensor_scalar(
                    u2q[:, :], u1q[:, :], MAGIC, MAGIC, ALU.add, ALU.subtract),
                    reads=[u1q.reg], writes=[u2q.reg])
                em.op("v", lambda v, u1q=u1q, u2q=u2q: v.tensor_sub(
                    u2q[:, :], u1q[:, :], u2q[:, :]),
                    reads=[u1q.reg, u2q.reg], writes=[u2q.reg])
                em.op("s", lambda s, h=h, hsl=hsl, u2q=u2q: s.activation(
                    Qp[h][:, hsl], u2q[:, :], ACT.Sin, scale=TWO_PI),
                    reads=[u2q.reg], writes=[Qp[h].reg])
    self.release()

    self.mark()
    zrow2 = [self.sb(f"l1a_zrow{i}", (1, Tc), F32) for i in range(2)]
    rz2 = [self.sb(f"l1a_rz{i}", (128, CW), F32) for i in range(2)]
    for h in range(H):
        zrow = zrow2[h % 2]
        for c in range(nch):
            tsl = slice(CW * c, CW * (c + 1))
            bk = B[4 + c % 2]
            self.mmx(bk, bk.h[0:1, 0:CW], kvb, kvb[:, 65 * h + 64:65 * h + 65],
                     Qp[h], Qp[h][:, tsl], stop=True)
            self.evac(bk, zrow, zrow[0:1, tsl], in_ap=bk.h[0:1, 0:CW])
        em.dma("y", zd.h[h, :], zrow[0:1, :], reads=[zrow.reg], writes=[zd.reg])
    ZW = H * Tc // 128
    zt = self.sb("l1a_zt", (128, ZW), F32)
    em.dma("y", zt[:, :], zd.h.flatten().rearrange("(p w) -> p w", p=128),
           reads=[zd.reg], writes=[zt.reg])
    em.op("v", lambda v: v.tensor_scalar(
        zt[:, :], zt[:, :], 128e-6, None, ALU.max),
        reads=[zt.reg], writes=[zt.reg])
    em.op("v", lambda v: v.reciprocal(zt[:, :], zt[:, :]),
          reads=[zt.reg], writes=[zt.reg])
    em.dma("y", zdi.h.flatten().rearrange("(p w) -> p w", p=128), zt[:, :],
           reads=[zt.reg], writes=[zdi.reg])
    for h in range(H):
        ft, par = h // 2, h % 2
        hb = slice(64 * par, 64 * (par + 1))
        for c in range(nch):
            rz = rz2[c % 2]
            tsl = slice(CW * c, CW * (c + 1))
            bk = B[6 + c % 2]
            self.mmx(bk, bk.h[hb, 0:CW], kvb, kvb[:, 65 * h:65 * h + 64],
                     Qp[h], Qp[h][:, tsl], stop=True, tp=(0, 64 * par),
                     pbase=64 * par)
            em.dma("y", rz[:, 0:CW],
                   zdi.h[h, tsl].unsqueeze(0).partition_broadcast(128),
                   reads=[zdi.reg], writes=[rz.reg])
            em.op("v", lambda v, ft=ft, hb=hb, tsl=tsl, bk=bk, rz=rz:
                  v.tensor_tensor(attn[ft][hb, tsl], bk.h[hb, 0:CW],
                                  rz[hb, 0:CW], ALU.mult),
                  reads=[bk.reg, rz.reg], writes=[attn[ft].reg])
            bk.fresh = True
    self.release()
    self.release()

    self.mark()
    wo = self.load_w_bf16("l1_Wo", D, D)
    xt2 = [self.sb(f"l1a_xt{i}", (128, NF * CW), F32) for i in range(2)]
    xu2 = [self.sb(f"l1a_xu{i}", (128, NF * CW), F32) for i in range(2)]
    for c in range(nch):
        xt, xu = xt2[c % 2], xu2[c % 2]
        tsl = slice(CW * c, CW * (c + 1))
        em.dma("y", xt[:, :],
               xsrc.h.rearrange("(f p) t -> p f t", p=128)[:, :, tsl],
               reads=[xsrc.reg], writes=[xt.reg])
        for o2 in range(NF):
            bk = B[o2 % 2]
            for k in range(NF):
                self.mmx(bk, bk.h[:, 0:CW], wo[k],
                         wo[k][:, 128 * o2:128 * (o2 + 1)], attn[k],
                         attn[k][:, tsl], stop=(k == NF - 1))
            em.op("v", lambda v, o2=o2, bk=bk, xt=xt, xu=xu: v.tensor_tensor(
                xu[:, CW * o2:CW * (o2 + 1)], bk.h[:, 0:CW],
                xt[:, CW * o2:CW * (o2 + 1)], ALU.add),
                reads=[bk.reg, xt.reg], writes=[xu.reg])
            bk.fresh = True
        em.dma("y", xdst.h.rearrange("(f p) t -> p f t", p=128)[:, :, tsl],
               xu[:, :], reads=[xu.reg], writes=[xdst.reg])
    self.xdram = xdst
    self.release()


MK.emit_l1a = emit_l1a



def emit_l1f(self):
    em, Tc, NT = self.em, self.Tc, self.NT
    B = [Bank(t) for t in self.pb]
    nch = (Tc + 511) // 512
    CW = min(512, Tc)
    xsrc_final = None

    cc_in = self.dscratch("cc_l1f_in", (D,), F32)
    cc_out = self.dscratch("cc_l1f_out", (8, D), F32)

    self.mark()
    xr = self.sbs("l1f_xr", (128, Tc), BF16, NF)
    xk = self.sbs("l1f_xk", (128, Tc), BF16, NF)
    self.mark()
    xlnf = self.sbs("l1f_x", (128, Tc), F32, NF)
    tmpm = self.sb("l1f_tmpm", (128, Tc), F32)
    self.emit_ln("4", xln=None, want_f32=True, want_bf16=False,
                 X_pre=xlnf, f32_inplace=True)
    xsrc = self.xdram
    xdst = self.xdram_next()

    lastc = self.sb("l1f_lastc", (128, NF), F32)
    for f in range(NF):
        self.copy("v", lastc, lastc[:, f:f + 1], xlnf[f],
                  xlnf[f][:, Tc - 1:Tc])
    for f in range(NF):
        em.dma("y", cc_in.h[128 * f:128 * (f + 1)], lastc[:, f:f + 1],
               reads=[lastc.reg], writes=[cc_in.reg])
    self.em.collective("AllGather", ALU.bypass, ins=[cc_in.h[:]],
                       outs=[cc_out.h[:, :]], reads=[cc_in.reg],
                       writes=[cc_out.reg])
    ohbt = self.inputs["ohbt_mat"]
    oh_t = self.sb("l1f_oh", (128, 8), F32)
    em.dma("y", oh_t[:, :], ohbt.h[:, :], reads=[ohbt.reg], writes=[oh_t.reg])
    agt = self.sb("l1f_agt", (128, 8), F32)
    bcol = self.sb("l1f_bcol", (128, NF), F32)
    for f in range(NF):
        em.dma("y", agt[:, :],
               cc_out.h[:, 128 * f:128 * (f + 1)].rearrange("r p -> p r"),
               reads=[cc_out.reg], writes=[agt.reg],
               allow_slow_non_contiguous=True)
        em.op("v", lambda v, f=f: v.tensor_mul(agt[:, :], agt[:, :],
                                               oh_t[:, :]),
              reads=[oh_t.reg, agt.reg], writes=[agt.reg])
        em.op("v", lambda v, f=f: v.reduce_sum(bcol[:, f:f + 1], agt[:, :], AX),
              reads=[agt.reg], writes=[bcol.reg])

    mu_r = self.load_cols("mu_r_col")
    mu_k = self.load_cols("mu_k_col")
    om = self.sb("l1f_om", (128, 2 * NF), F32)
    for f in range(NF):
        self.ts("v", om, om[:, f:f + 1], mu_r, mu_r[:, f:f + 1], -1.0,
                ALU.mult, 1.0, ALU.add, s_regs=[mu_r.reg])
        self.ts("v", om, om[:, NF + f:NF + f + 1], mu_k, mu_k[:, f:f + 1],
                -1.0, ALU.mult, 1.0, ALU.add, s_regs=[mu_k.reg])

    def mix(dst, mu_t, om_off, tag, tmp):
        for f in range(NF):
            em.op("v", lambda v, f=f: v.tensor_scalar(
                tmp[:, 1:Tc], xlnf[f][:, 0:Tc - 1],
                om[:, om_off + f:om_off + f + 1], None, ALU.mult),
                reads=[xlnf[f].reg, om.reg], writes=[tmp.reg])
            em.op("v", lambda v, f=f: v.tensor_scalar(
                tmp[:, 0:1], bcol[:, f:f + 1],
                om[:, om_off + f:om_off + f + 1], None, ALU.mult),
                reads=[bcol.reg, om.reg], writes=[tmp.reg])
            em.op("v", lambda v, f=f, dst=dst: v.scalar_tensor_tensor(
                dst[f][:, :], xlnf[f][:, :], mu_t[:, f:f + 1], tmp[:, :],
                ALU.mult, ALU.add),
                reads=[xlnf[f].reg, mu_t.reg, tmp.reg], writes=[dst[f].reg])

    mix(xk, mu_k, NF, "k", tmpm)
    mix(xr, mu_r, 0, "r", tmpm)
    self.release()

    nchf = (Tc + 511) // 512
    CWF = min(512, Tc)
    self.mark()
    wkT = self.load_w_bf16("l1f_WkT", D, D)
    wrT = self.load_w_bf16("l1f_WrT", D, D)
    wvT = self.load_w_bf16("l1f_WvT", D, D)
    kct2 = [self.sb(f"l1f_kct{i}", (128, NF * CWF), BF16) for i in range(2)]
    rsl2 = [self.sb(f"l1f_rsl{i}", (128, NF * CWF), F32) for i in range(2)]
    kr2 = [self.sb(f"l1f_kr{i}", (128, CWF), F32) for i in range(2)]
    xt2 = [self.sb(f"l1f_xt{i}", (128, CWF), F32) for i in range(2)]
    xu2 = [self.sb(f"l1f_xu{i}", (128, CWF), F32) for i in range(2)]
    for c in range(nchf):
        kct, rsl = kct2[c % 2], rsl2[c % 2]
        tsl = slice(CWF * c, CWF * (c + 1))
        for o in range(NF):
            bk = B[o % 2]
            kr = kr2[o % 2]
            for k in range(NF):
                self.mmx(bk, bk.h[:, 0:CWF], wkT[k],
                         wkT[k][:, 128 * o:128 * (o + 1)], xk[k],
                         xk[k][:, tsl], stop=(k == NF - 1))
            self.evac(bk, kr, kr[:, 0:CWF], in_ap=bk.h[:, 0:CWF], func=ACT.Relu)
            em.op("v", lambda v, o=o, kr=kr, kct=kct: v.tensor_tensor(
                kct[:, CWF * o:CWF * (o + 1)], kr[:, 0:CWF], kr[:, 0:CWF],
                ALU.mult),
                reads=[kr.reg], writes=[kct.reg])
        for o in range(NF):
            bk = B[2 + o % 2]
            for k in range(NF):
                self.mmx(bk, bk.h[:, 0:CWF], wrT[k],
                         wrT[k][:, 128 * o:128 * (o + 1)], xr[k],
                         xr[k][:, tsl], stop=(k == NF - 1))
            self.evac(bk, rsl, rsl[:, CWF * o:CWF * (o + 1)],
                      in_ap=bk.h[:, 0:CWF], func=ACT.Sigmoid)
        for o2 in range(NF):
            bk = B[4 + o2 % 2]
            xt, xu = xt2[o2 % 2], xu2[o2 % 2]
            em.dma("y", xt[:, :],
                   xsrc.h.rearrange("(f p) t -> p f t", p=128)[:, o2, tsl],
                   reads=[xsrc.reg], writes=[xt.reg])
            for k in range(NF):
                self.mmx(bk, bk.h[:, 0:CWF], wvT[k],
                         wvT[k][:, 128 * o2:128 * (o2 + 1)], kct,
                         kct[:, CWF * k:CWF * (k + 1)], stop=(k == NF - 1))
            em.op("v", lambda v, o2=o2, bk=bk, xu=xu, rsl=rsl: v.tensor_tensor(
                xu[:, :], bk.h[:, 0:CWF],
                rsl[:, CWF * o2:CWF * (o2 + 1)], ALU.mult),
                reads=[bk.reg, rsl.reg], writes=[xu.reg])
            em.op("v", lambda v, xu=xu, xt=xt: v.tensor_tensor(
                xu[:, :], xu[:, :], xt[:, :], ALU.add),
                reads=[xt.reg, xu.reg], writes=[xu.reg])
            bk.fresh = True
            em.dma("y",
                   xdst.h.rearrange("(f p) t -> p f t", p=128)[:, o2, tsl],
                   xu[:, :], reads=[xu.reg], writes=[xdst.reg])
    self.xdram = xdst
    self.release()
    self.release()


MK.emit_l1f = emit_l1f


NCORES = 8


def prep_in_maps(inp, Tc):
    x = np.asarray(inp["L0_attn_x"])[0]
    T, Dm = x.shape
    assert T == Tc * NCORES
    base = {
        "wrT": np.ascontiguousarray(inp["L0_attn_w_r"].T),
        "wkT": np.ascontiguousarray(inp["L0_attn_w_k"].T),
        "wvT": np.ascontiguousarray(inp["L0_attn_w_v"].T),
        "wgT": np.ascontiguousarray(inp["L0_attn_w_g"].T),
        "woT": np.ascontiguousarray(inp["L0_attn_w_o"].T),
        "ldT": np.ascontiguousarray(inp["L0_attn_w_lora_down"].T),
        "luT": np.ascontiguousarray(inp["L0_attn_w_lora_up"].T),
        "u_col": np.ascontiguousarray(inp["L0_attn_u"].reshape(-1, 1)),
        "wbase_col": np.ascontiguousarray(
            inp["L0_attn_w_base"].reshape(-1, 1)),
        "lng_rep": np.ascontiguousarray(
            np.tile(np.tile(inp["L0_attn_ln_gamma"], 16)[None, :], (128, 1))),
        "lnb_rep": np.ascontiguousarray(
            np.tile(np.tile(inp["L0_attn_ln_beta"], 16)[None, :], (128, 1))),
        "wgateN": inp["L0_ffn_w_gate"],
        "wgateT": np.ascontiguousarray(inp["L0_ffn_w_gate"].T),
        "wupN": inp["L0_ffn_w_up"],
        "wupT": np.ascontiguousarray(inp["L0_ffn_w_up"].T),
        "wdownN": inp["L0_ffn_w_down"],
        "wdownT": np.ascontiguousarray(inp["L0_ffn_w_down"].T),
        "rf2": np.ascontiguousarray(np.vstack([inp["L1_attn_rf"],
                                               inp["L1_attn_rf"]])),
        "l1_Wq": inp["L1_attn_Wq"],
        "l1_Wk": inp["L1_attn_Wk"],
        "l1_Wv": inp["L1_attn_Wv"],
        "l1_Wo": inp["L1_attn_Wo"],
        "l1f_WrT": np.ascontiguousarray(inp["L1_ffn_W_r"].T),
        "l1f_WkT": np.ascontiguousarray(inp["L1_ffn_W_k"].T),
        "l1f_WvT": np.ascontiguousarray(inp["L1_ffn_W_v"].T),
        "mu_r_col": np.ascontiguousarray(inp["L1_ffn_mu_r"].reshape(-1, 1)),
        "mu_k_col": np.ascontiguousarray(inp["L1_ffn_mu_k"].reshape(-1, 1)),
    }
    base = {k: v.astype(np.float32) for k, v in base.items()}
    maps = []
    for c in range(NCORES):
        m = dict(base)
        m["x_fm"] = np.ascontiguousarray(x[Tc * c:Tc * (c + 1), :].T)
        oh = np.zeros((128, 8), np.float32)
        oh[:, c] = 1.0
        m["ohprev_mat"] = oh
        ohb = np.zeros((128, 8), np.float32)
        if c > 0:
            ohb[:, c - 1] = 1.0
        m["ohbt_mat"] = ohb
        maps.append(m)
    return maps

def build(Tc):
    mk = MK(Tc=Tc)
    mk.psum_init()
    mk.emit_consts()
    for n, shp in [("wrT", (D, D)), ("wkT", (D, D)), ("wvT", (D, D)),
                   ("wgT", (D, D)), ("woT", (D, D)), ("ldT", (D, LR)),
                   ("luT", (LR, D)), ("u_col", (D, 1)), ("wbase_col", (D, 1)),
                   ("lng_rep", (128, D)), ("lnb_rep", (128, D)),
                   ("ohprev_mat", (128, 8)), ("ohbt_mat", (128, 8)),
                   ("wgateN", (FI, D)), ("wgateT", (D, FI)),
                   ("wupN", (FI, D)), ("wupT", (D, FI)),
                   ("wdownN", (D, FI)), ("wdownT", (FI, D)),
                   ("rf2", (128, 128)), ("l1_Wq", (D, D)), ("l1_Wk", (D, D)),
                   ("l1_Wv", (D, D)), ("l1_Wo", (D, D)),
                   ("l1f_WrT", (D, D)), ("l1f_WkT", (D, D)),
                   ("l1f_WvT", (D, D)),
                   ("mu_r_col", (D, 1)), ("mu_k_col", (D, 1))]:
        mk.din(n, shp)
    mk.emit_dq_scales(keys=("g",))
    mk.emit_load_x()
    mk.mark()
    mk.emit_dq_piece_bufs()
    mk.mark()
    xln1 = mk.sbs("xln_1", (128, Tc), BF16, NF)
    gate_p = mk.emit_dq_wt_pieces("g")
    mk.emit_ln("1", xln=xln1)

    def filler(g):
        for _ in range(4):
            if gate_p:
                gate_p.pop(0)()

    mk.emit_l0a_sweep1(xln1, filler=filler)
    while gate_p:
        gate_p.pop(0)()
    mk.weight_precast([("wgT", (D, D)), ("woT", (D, D)),
                       ("l1_Wq", (D, D)), ("l1_Wk", (D, D)),
                       ("l1_Wv", (D, D)), ("l1_Wo", (D, D)),
                       ("l1f_WrT", (D, D)), ("l1f_WkT", (D, D)),
                       ("l1f_WvT", (D, D))])
    mk.emit_l0a_collective()
    mk.emit_l0a_sweep2(xln1)
    mk.release()
    mk.mark()
    xq = mk.emit_q1()
    mk.emit_dq_scales(keys=("u", "d"))
    up_p = mk.emit_dq_wt_pieces("u")
    dn_p = mk.emit_dq_wt_pieces("d")
    mk.emit_l0f(xq, wt_pieces=(up_p, dn_p))
    mk.release()
    mk.release()
    mk.mark()
    xln3 = mk.sbs("xln_3", (128, Tc), BF16, NF)
    mk.emit_ln("3", xln=xln3)
    mk.emit_l1a(xln3)
    mk.release()
    mk.emit_l1f()
    mk.emit_store_x()
    return mk, mk.finish()




_CACHE = {}


def kernel(**inputs):
    import numpy as np
    from concourse.bass_utils import run_bass_kernel_spmd
    Tc = 2048
    if "nc" not in _CACHE:
        _CACHE["mk"], _CACHE["nc"] = build(Tc)
    nc = _CACHE["nc"]
    maps = prep_in_maps(inputs, Tc)
    res = run_bass_kernel_spmd(nc, maps, core_ids=list(range(8)))
    out = np.concatenate([np.asarray(res.results[c]["y_fm"]).T
                          for c in range(8)], axis=0)[None]
    return np.ascontiguousarray(out.astype(np.float32))


# version 23
# speedup vs baseline: 1.0230x; 1.0050x over previous
import numpy as np

import math
import ml_dtypes
import concourse.bass as bass
import concourse.mybir as mybir

F32 = mybir.dt.float32
BF16 = mybir.dt.bfloat16
I32 = mybir.dt.int32
ALU = mybir.AluOpType
ACT = mybir.ActivationFunctionType
AX = mybir.AxisListType.X

D = 1024
H = 16
DH = 64
LR = 64
FI = 3072
FD = 128
L = 64
NF = 8
EPS = 1e-5
QMAX16 = 32767.0


class Reg:
    __slots__ = ("name", "w", "r")

    def __init__(self, name="r"):
        self.name = name
        self.w = None
        self.r = []


class Em:
    ENG = ("t", "v", "s", "g", "y")
    QUEUES = {"y": "qy", "s": "qs", "g": "qg"}

    KSLOT = 6

    def __init__(self, nc):
        self.nc = nc
        self.plan = {k: [] for k in self.ENG}
        self.planned_inc = {k: {} for k in self.ENG}
        self.ninc = {k: 0 for k in self.ENG}
        self.ndma = {q: 0 for q in ("qy", "qs", "qg")}
        self.ncc = 0
        self.waited = {k: {} for k in self.ENG}
        self.nop = 0
        self._pending_waits = {}
        self.sems = {}
        self.semkeys = list(self.ENG) + ["cc"]
        for q in ("qy", "qs", "qg"):
            for i in range(self.KSLOT):
                self.semkeys.append(f"{q}{i}")
        for k in self.semkeys:
            self.sems[k] = nc.alloc_semaphore(f"sem_{k}")
        self.slot_uses = {k: 0 for k in self.semkeys}

    def _ensure_inc(self, eng, idx):
        ks = [j for j in self.planned_inc[eng] if j >= idx]
        if ks:
            return self.planned_inc[eng][min(ks)]
        self.ninc[eng] += 1
        val = self.ninc[eng]
        self.planned_inc[eng][idx] = val
        self.plan[eng][idx][2].append((eng, 1))
        return val

    def _wait(self, consumer, token):
        if token is None:
            return
        kind, x = token
        if kind in self.ENG:
            val = self._ensure_inc(kind, x)
            skey = kind
        elif kind == "cc":
            val, skey = self.ncc, kind
        else:
            val, skey = x, kind
        prev = self.waited[consumer].get(skey, -1)
        if val <= prev:
            return
        self._pending_waits.setdefault(consumer, []).append((skey, val))
        self.waited[consumer][skey] = val

    def _deps(self, consumer, reads, writes):
        for r in reads:
            self._wait(consumer, r.w)
        for w in writes:
            self._wait(consumer, w.w)
            for t in w.r:
                self._wait(consumer, t)

    def _update(self, token, reads, writes):
        for r in reads:
            r.r.append(token)
        for w in writes:
            w.w = token
            w.r = []

    def _push(self, eng, fn, kind="op"):
        waits = self._pending_waits.pop(eng, [])
        idx = len(self.plan[eng])
        self.plan[eng].append([fn, waits, [], kind])
        self.nop += 1
        return idx

    def op(self, eng, fn, reads=(), writes=(), kind="op"):
        self._deps(eng, reads, writes)
        idx = self._push(eng, fn, kind)
        self._update((eng, idx), reads, writes)
        return (eng, idx)

    def dma(self, eng, out, in_, reads=(), writes=(), **kw):
        q = self.QUEUES[eng]
        self._deps(eng, reads, writes)
        n = self.ndma[q]
        self.ndma[q] += 1
        skey = f"{q}{n % self.KSLOT}"
        self.slot_uses[skey] += 1
        uses = self.slot_uses[skey]
        if uses > 1:
            prev = 16 * (uses - 1)
            if prev > self.waited[eng].get(skey, -1):
                self._pending_waits.setdefault(eng, []).append((skey, prev))
                self.waited[eng][skey] = prev
        idx = self._push(eng, lambda e, out=out, in_=in_, kw=kw: e.dma_start(
            out=out, in_=in_, **kw))
        self.plan[eng][idx][2].append((skey, 16))
        self._update((skey, 16 * uses), reads, writes)
        return (skey, 16 * uses)

    def collective(self, kind, op, ins, outs, reads=(), writes=()):
        self._deps("g", reads, writes)
        self.ncc += 1
        n = self.ncc
        idx = self._push(
            "g", lambda e, k=kind, o=op, i=ins, u=outs: e.collective_compute(
                k, o, replica_groups=[list(range(8))], ins=i, outs=u))
        self.plan["g"][idx][2].append(("cc", 1))
        self._update(("cc", n), reads, writes)
        return ("cc", n)

    def sync_engine(self, eng, regs):
        for r in regs:
            self._wait(eng, r.w)
            for t in r.r:
                self._wait(eng, t)

    def final_waits(self, eng):
        for k in self.semkeys:
            if k in self.ENG or k == "cc":
                continue
            v = 16 * self.slot_uses[k]
            if v > self.waited[eng].get(k, -1) and v > 0:
                self._pending_waits.setdefault(eng, []).append((k, v))

    def emit(self, block):
        sems = self.sems

        def run(e, eng):
            for fn, waits, incs, kind in self.plan[eng]:
                for skey, val in waits:
                    e.wait_ge(sems[skey], val)
                inst = fn(e)
                for skey, n in incs:
                    inst.then_inc(sems[skey], n)
            for skey, val in self._pending_waits.pop(eng, []):
                e.wait_ge(sems[skey], val)

        block.tensor(lambda e: run(e, "t"))
        block.vector(lambda e: run(e, "v"))
        block.scalar(lambda e: run(e, "s"))
        block.gpsimd(lambda e: run(e, "g"))
        block.sync(lambda e: run(e, "y"))


class Arena:
    def __init__(self, nc):
        self.nc = nc
        self.base = (nc.sbuf_base + 31) // 32 * 32
        self.top = nc.sbuf_top
        self.off = self.base
        self.marks = []
        self.peak = self.base

    def alloc_h(self, name, shape, dtype):
        esz = 2 if dtype in (BF16, mybir.dt.float16) else 4
        per = (int(shape[1]) * esz + 31) // 32 * 32
        h = self.nc.alloc_sbuf_tensor_at(name, list(shape), dtype, offset=self.off)
        self.off += per
        self.peak = max(self.peak, self.off)
        assert self.off <= self.top, f"SBUF overflow at {name}: {(self.off-self.base)/1024:.0f}KB"
        return h

    def mark(self):
        self.marks.append(self.off)

    def release(self):
        self.off = self.marks.pop()

    def kb(self):
        return (self.off - self.base) / 1024.0


class T:
    __slots__ = ("h", "reg")

    def __init__(self, h, reg):
        self.h = h
        self.reg = reg

    def __getitem__(self, key):
        return self.h[key]

    @property
    def ap(self):
        return self.h[:, :] if not isinstance(self.h, bass.AP) else self.h


class MK:

    def __init__(self, Tc=2048, blocks=("l0a", "l0f", "l1a", "l1f")):
        self.Tc = Tc
        self.blocks = blocks
        self.NT = Tc // 128
        self.NCH = Tc // L
        self.nc = bass.Bass("TRN2", target_bir_lowering=False, debug=False)
        self.em = Em(self.nc)
        self.ar = Arena(self.nc)
        self.inputs = {}
        self.n_dram = 0

    def din(self, name, shape):
        ap = self.nc.dram_tensor(name, list(shape), F32, kind="ExternalInput").ap()
        t = T(ap, Reg(name))
        self.inputs[name] = t
        return t

    def dout(self, name, shape, dtype=F32):
        ap = self.nc.dram_tensor(name, list(shape), dtype, kind="ExternalOutput").ap()
        return T(ap, Reg(name))

    def dscratch(self, name, shape, dtype=F32):
        ap = self.nc.dram_tensor(name, list(shape), dtype).ap()
        return T(ap, Reg(name))

    def sb(self, name, shape, dtype):
        t = T(self.ar.alloc_h(name, shape, dtype), Reg(name))
        if not hasattr(self, "_scopes"):
            self._scopes = []
            self._fence = []
        t.reg.r = list(self._fence)
        if self._scopes:
            self._scopes[-1].append(t)
        return t

    def mark(self):
        if not hasattr(self, "_scopes"):
            self._scopes = []
            self._fence = []
        self.ar.mark()
        self._scopes.append([])

    def release(self):
        ts = self._scopes.pop()
        for t in ts:
            if t.reg.w is not None:
                self._fence.append(t.reg.w)
            self._fence.extend(t.reg.r)
        best = {}
        for k, v in self._fence:
            if k not in best or v > best[k]:
                best[k] = v
        self._fence = [(k, v) for k, v in best.items()]
        self.ar.release()

    def sbs(self, name, shape, dtype, n):
        return [self.sb(f"{name}{i}", shape, dtype) for i in range(n)]

    def psum_init(self):
        self.pb = []
        for b in range(8):
            h = self.nc.alloc_psum_tensor(f"pbank{b}", [128, 512], F32)
            self.pb.append(T(h, Reg(f"pb{b}")))

    def mm(self, out_ap, lhsT, lhsT_ap, rhs, rhs_ap, bank, start, stop):
        self.em.op("t", lambda t: t.matmul(out_ap, lhsT_ap, rhs_ap,
                                           start=start, stop=stop),
                   reads=[lhsT.reg, rhs.reg], writes=[bank.reg])

    def tt(self, eng, out_t, out_ap, a_t, a_ap, b_t, b_ap, op, extra_reads=()):
        self.em.op(eng, lambda e: e.tensor_tensor(out_ap, a_ap, b_ap, op),
                   reads=[a_t.reg, b_t.reg, *extra_reads], writes=[out_t.reg])

    def ts(self, eng, out_t, out_ap, a_t, a_ap, s1, op0, s2=None, op1=ALU.bypass,
           s_regs=()):
        self.em.op(eng, lambda e: e.tensor_scalar(out_ap, a_ap, s1, s2, op0, op1),
                   reads=[a_t.reg, *s_regs], writes=[out_t.reg])

    def act(self, eng, out_t, out_ap, a_t, a_ap, func, bias=0.0, scale=1.0,
            extra_reads=()):
        self.em.op(eng, lambda e: e.activation(out_ap, a_ap, func, bias=bias,
                                               scale=scale),
                   reads=[a_t.reg, *extra_reads], writes=[out_t.reg])

    def copy(self, eng, out_t, out_ap, a_t, a_ap):
        self.em.op(eng, lambda e: e.tensor_copy(out_ap, a_ap),
                   reads=[a_t.reg], writes=[out_t.reg])

    def ccol(self, val):
        if not hasattr(self, "_ccols"):
            self._ccols = {}
            self._ccol_t = self.sb("cconst", (128, 16), F32)
            self._ccol_n = 0
        if val not in self._ccols:
            k = self._ccol_n
            self._ccol_n += 1
            assert k < 16
            self.em.op("g", lambda g, k=k, val=val: g.memset(
                self._ccol_t[:, k:k + 1], float(val)), writes=[self._ccol_t.reg])
            self._ccols[val] = k
        k = self._ccols[val]
        return self._ccol_t[:, k:k + 1]

    @property
    def ccol_reg(self):
        return self._ccol_t.reg

    def emit_consts(self):
        em = self.em
        self.ones_col = self.sb("ones_col", (128, 2), BF16)
        self.ident = self.sb("ident", (128, 128), BF16)
        self.onesf = self.sb("onesf", (128, 128), F32)
        self.mstrict = self.sb("mstrict", (128, 64), F32)
        self.mdiag = self.sb("mdiag", (128, 64), F32)
        em.op("g", lambda g: g.memset(self.ones_col[:, :], 1.0),
              writes=[self.ones_col.reg])
        em.op("g", lambda g: g.memset(self.onesf[:, :], 1.0), writes=[self.onesf.reg])
        self.onesb = self.sb("onesb", (128, 128), BF16)
        self.identf = self.sb("identf", (128, 128), F32)
        em.op("g", lambda g: g.memset(self.onesb[:, :], 1.0), writes=[self.onesb.reg])
        em.op("g", lambda g: g.affine_select(
            self.ident[:, :], self.onesb[:, :], [[1, 128]],
            ALU.is_equal, 0.0, base=0, channel_multiplier=-1),
            reads=[self.onesb.reg], writes=[self.ident.reg])
        em.op("g", lambda g: g.affine_select(
            self.identf[:, :], self.onesf[:, :], [[1, 128]],
            ALU.is_equal, 0.0, base=0, channel_multiplier=-1),
            reads=[self.onesf.reg], writes=[self.identf.reg])
        em.op("g", lambda g: g.affine_select(
            self.mstrict[0:64, :], self.onesf[0:64, 0:64], [[1, 64]],
            ALU.is_ge, 0.0, base=-1, channel_multiplier=-1),
            reads=[self.onesf.reg], writes=[self.mstrict.reg])
        em.op("g", lambda g: g.affine_select(
            self.mdiag[0:64, :], self.onesf[0:64, 0:64], [[1, 64]],
            ALU.is_equal, 0.0, base=0, channel_multiplier=-1),
            reads=[self.onesf.reg], writes=[self.mdiag.reg])
        em.op("g", lambda g: g.tensor_copy(self.mstrict[64:128, :],
                                           self.mstrict[0:64, :]),
              reads=[self.mstrict.reg], writes=[self.mstrict.reg])
        em.op("g", lambda g: g.tensor_copy(self.mdiag[64:128, :],
                                           self.mdiag[0:64, :]),
              reads=[self.mdiag.reg], writes=[self.mdiag.reg])

    def emit_load_x(self):
        xin = self.din("x_fm", (D, self.Tc))
        self.xdram = xin
        self.xping = [self.dscratch("x_ping0", (D, self.Tc), F32),
                      self.dscratch("x_ping1", (D, self.Tc), F32)]
        self.nping = 0

    def xdram_next(self):
        t = self.xping[self.nping % 2]
        self.nping += 1
        return t

    def rsqrt_refine(self, out_t, out_ap, s_t, s0_ap, d_ap):
        em = self.em
        em.op("v", lambda v: v.reciprocal(out_ap, s0_ap),
              reads=[s_t.reg], writes=[out_t.reg])
        em.op("v", lambda v: v.scalar_tensor_tensor(
            out_ap, d_ap, 0.5, out_ap, ALU.mult, ALU.mult),
            reads=[s_t.reg, out_t.reg], writes=[out_t.reg])
        em.op("v", lambda v: v.scalar_tensor_tensor(
            out_ap, s0_ap, 0.5, out_ap, ALU.mult, ALU.add),
            reads=[s_t.reg, out_t.reg], writes=[out_t.reg])
        em.op("v", lambda v: v.reciprocal(out_ap, out_ap),
              reads=[out_t.reg], writes=[out_t.reg])

    def emit_ln(self, tag, xln=None, xlnf=None, want_f32=False,
                want_bf16=True, X_pre=None, f32_inplace=False):
        em, Tc = self.em, self.Tc
        if xln is None and want_bf16:
            xln = self.sbs(f"xln_{tag}", (128, Tc), BF16, NF)
        if want_f32 and xlnf is None and not f32_inplace:
            xlnf = self.sbs(f"xlnf_{tag}", (128, Tc), F32, NF)
        self.mark()
        X = X_pre if X_pre is not None else self.sbs(
            f"ln_x_{tag}", (128, Tc), F32, NF)
        for f in range(NF):
            self.em.dma("y", X[f][:, :],
                        self.xdram.h[128 * f:128 * (f + 1), :],
                        reads=[self.xdram.reg], writes=[X[f].reg])
        self.X = X
        if f32_inplace:
            xlnf = X
        xb = self.sb(f"ln_xb_{tag}", (128, Tc), BF16)
        sq = self.sb(f"ln_sq_{tag}", (128, Tc), BF16)
        tdiff = self.sb(f"ln_td_{tag}", (128, Tc), F32)
        rows = self.sb(f"ln_rows_{tag}", (1, 2 * Tc), F32)
        bounce = self.dscratch(f"lnb_{tag}", (2 * Tc,), F32)
        mu_b = self.sb(f"ln_mub_{tag}", (128, Tc), F32)
        rs_b = self.sb(f"ln_rsb_{tag}", (128, Tc), F32)

        nch = (Tc + 511) // 512
        assert nch <= 4
        B = [Bank(t) for t in self.pb]
        for f in range(NF):
            self.copy("v", xb, xb[:, :], self.X[f], self.X[f][:, :])
            self.tt("v", sq, sq[:, :], xb, xb[:, :], xb, xb[:, :], ALU.mult)
            for c in range(nch):
                sl = slice(512 * c, min(512 * (c + 1), Tc))
                w = sl.stop - sl.start
                self.mmx(B[2 * c], B[2 * c].h[0:1, 0:w], self.ones_col,
                         self.ones_col[:, 0:1], xb, xb[:, sl],
                         stop=(f == NF - 1))
                self.mmx(B[2 * c + 1], B[2 * c + 1].h[0:1, 0:w], self.ones_col,
                         self.ones_col[:, 0:1], sq, sq[:, sl],
                         stop=(f == NF - 1))
        for c in range(nch):
            sl = slice(512 * c, min(512 * (c + 1), Tc))
            w = sl.stop - sl.start
            self.evac(B[2 * c], rows, rows[0:1, sl], in_ap=B[2 * c].h[0:1, 0:w],
                      func=ACT.Copy, scale=1.0 / D)
            self.evac(B[2 * c + 1], rows, rows[0:1, Tc + sl.start:Tc + sl.stop],
                      in_ap=B[2 * c + 1].h[0:1, 0:w], func=ACT.Copy,
                      scale=1.0 / D)
        self.em.dma("y", bounce.h[:], rows[0:1, :], reads=[rows.reg],
                    writes=[bounce.reg])
        W = Tc // 128
        bounce2 = self.dscratch(f"lnb2_{tag}", (Tc,), F32)
        mu_t = self.sb(f"ln_mut_{tag}", (128, W), F32)
        ms_t = self.sb(f"ln_mst_{tag}", (128, W), F32)
        vt2 = self.sb(f"ln_vt2_{tag}", (128, 2 * W), F32)
        va_t, va_ap = vt2, vt2[:, 0:W]
        rs_t = self.sb(f"ln_rst_{tag}", (128, W), F32)
        self.em.dma("y", mu_t[:, :],
                    bounce.h[0:Tc].rearrange("(p w) -> p w", p=128),
                    reads=[bounce.reg], writes=[mu_t.reg])
        self.em.dma("y", ms_t[:, :],
                    bounce.h[Tc:2 * Tc].rearrange("(p w) -> p w", p=128),
                    reads=[bounce.reg], writes=[ms_t.reg])
        self.em.op("v", lambda v, va_ap=va_ap: v.scalar_tensor_tensor(
            va_ap, mu_t[:, :], 1.0, mu_t[:, :], ALU.mult, ALU.mult),
            reads=[mu_t.reg], writes=[va_t.reg])
        self.tt("v", va_t, va_ap, ms_t, ms_t[:, :], va_t, va_ap, ALU.subtract)
        self.em.op("v", lambda v, va_ap=va_ap: v.tensor_scalar(
            va_ap, va_ap, EPS, None, ALU.add),
            reads=[va_t.reg], writes=[va_t.reg])
        self.act("s", vt2, vt2[:, W:2 * W], va_t, va_ap, ACT.Sqrt)
        self.rsqrt_refine(rs_t, rs_t[:, :], vt2, vt2[:, W:2 * W], va_ap)
        self.em.dma("y", bounce2.h[0:Tc].rearrange("(p w) -> p w", p=128),
                    rs_t[:, :], reads=[rs_t.reg], writes=[bounce2.reg])
        self.em.dma("y", mu_b[:, :],
                    bounce.h[0:Tc].unsqueeze(0).partition_broadcast(128),
                    reads=[bounce.reg], writes=[mu_b.reg])
        self.em.dma("y", rs_b[:, :],
                    bounce2.h[0:Tc].unsqueeze(0).partition_broadcast(128),
                    reads=[bounce2.reg], writes=[rs_b.reg])
        for f in range(NF):
            self.tt("v", tdiff, tdiff[:, :], self.X[f], self.X[f][:, :],
                    mu_b, mu_b[:, :], ALU.subtract)
            if want_bf16:
                self.tt("v", xln[f], xln[f][:, :], tdiff, tdiff[:, :],
                        rs_b, rs_b[:, :], ALU.mult)
            if want_f32:
                self.tt("v", xlnf[f], xlnf[f][:, :], tdiff, tdiff[:, :],
                        rs_b, rs_b[:, :], ALU.mult)
        self.release()
        self.X = None
        return xln, xlnf

    def emit_store_x(self):
        if getattr(self, "xout_override", None) is not None:
            return
        xout = self.dout("y_fm", (D, self.Tc))
        self.em.dma("y", xout.h[:, :], self.xdram.h[:, :],
                    reads=[self.xdram.reg], writes=[xout.reg])

    def finish(self):
        self.em.final_waits("y")
        with self.nc.Block() as block:
            self.em.emit(block)
        return self.nc


def store_tiles(mk, name, tiles, rows=128):
    n = len(tiles)
    cols = tiles[0].h.shape[1]
    out = mk.dout(name, (n * rows, cols))
    for i, t in enumerate(tiles):
        mk.em.dma("g", out.h[rows * i:rows * (i + 1), :], t[:, :],
                  reads=[t.reg], writes=[out.reg])
    return out



class Bank:
    __slots__ = ("t", "started")

    def __init__(self, t):
        self.t = t
        self.started = set()

    @property
    def fresh(self):
        return not self.started

    @fresh.setter
    def fresh(self, val):
        if val:
            self.started = set()

    @property
    def h(self):
        return self.t.h

    @property
    def reg(self):
        return self.t.reg


def _mk_weight_precast(self, names_shapes):
    if not hasattr(self, "_wcast"):
        self._wcast = {}
    for name, (rows, cols) in names_shapes:
        ap = self.inputs[name]
        d = self.dscratch(f"wc_{name}", (rows, cols), BF16)
        self.em.dma("g", d.h[:, :], ap.h[:, :], reads=[ap.reg],
                    writes=[d.reg])
        self._wcast[name] = d


def _mk_load_w_bf16(self, name, rows, cols):
    wc = getattr(self, "_wcast", {}).get(name)
    ap = wc if wc is not None else self.inputs[name]
    q = "y" if wc is not None else "g"
    nt = (rows + 127) // 128
    tiles = self.sbs(f"w_{name}", (min(128, rows), cols), BF16, nt)
    for i in range(nt):
        self.em.dma(q, tiles[i][:, :], ap.h[128 * i:min(128 * (i + 1), rows), :],
                    reads=[ap.reg], writes=[tiles[i].reg])
    return tiles


def _mk_load_cols(self, name):
    ap = self.inputs[name]
    t = self.sb(f"c_{name}", (128, NF), F32)
    for f in range(NF):
        self.em.dma("g", t[:, f:f + 1], ap.h[128 * f:128 * (f + 1), :],
                    reads=[ap.reg], writes=[t.reg])
    return t


def _mk_mmx(self, bank, out_ap, lhsT_t, lhsT_ap, rhs_t, rhs_ap, stop=False,
            tp=None, pbase=0):
    start = pbase not in bank.started
    bank.started.add(pbase)
    self.em.op("t", lambda t: t.matmul(out_ap, lhsT_ap, rhs_ap, start=start,
                                       stop=stop, tile_position=tp,
                                       skip_group_check=True),
               reads=[lhsT_t.reg, rhs_t.reg], writes=[bank.reg], kind="mm")


def _mk_evac(self, bank, out_t, out_ap, in_ap=None, func=None, bias=0.0,
             scale=1.0, extra_reads=()):
    ia = in_ap if in_ap is not None else bank.h[:, :]
    if func is None:
        self.em.op("s", lambda s: s.copy(out_ap, ia),
                   reads=[bank.reg, *extra_reads], writes=[out_t.reg])
    else:
        self.em.op("s", lambda s: s.activation(out_ap, ia, func, bias=bias,
                                               scale=scale),
                   reads=[bank.reg, *extra_reads], writes=[out_t.reg])
    bank.fresh = True


MK.load_w_bf16 = _mk_load_w_bf16
MK.weight_precast = _mk_weight_precast
MK.load_cols = _mk_load_cols
MK.mmx = _mk_mmx
MK.evac = _mk_evac



def emit_l0a_sweep1(self, xln, upto="all", filler=None):
    em, Tc, NCH = self.em, self.Tc, self.NCH
    NG = NCH // 4
    NT = self.NT
    CCW = 512 + 8

    st = self.l0a = {}
    st["outacc"] = outacc = self.dscratch("outacc", (Tc, D), F32)
    st["cc_in"] = self.dscratch("cc_l0a_in", (128, CCW), F32)
    st["cc_out"] = self.dscratch("cc_l0a_out", (8, 128, CCW), F32)

    st["rt"] = rt = self.sbs("rt", (128, Tc), BF16, NF)
    st["eALt"] = eALt = self.sb("eALt", (128, NCH * 8), F32)
    st["Bcol"] = Bcol = [self.sb(f"Bcol{f}", (128, NCH), F32) for f in range(NF)]
    st["carry"] = carry = self.sb("carry", (128, NF), F32)
    st["S_cur"] = S_cur = self.sb("S_cur", (128, 512), F32)
    st["S_tmp"] = S_tmp = self.sb("S_tmp", (128, 512), F32)
    st["S_bf"] = S_bf = self.sb("S_bf", (128, 512), BF16)
    st["S_in"] = self.sb("S_in", (128, 512), F32)
    st["S_inb"] = self.sb("S_inb", (128, 512), BF16)
    st["P_run"] = self.sb("P_run", (128, 512), F32)
    st["Dg"] = self.sb("Dg", (128, 8), F32)

    em.op("v", lambda v: v.memset(S_cur[:, :], 0.0), writes=[S_cur.reg])
    em.op("v", lambda v: v.memset(S_bf[:, :], 0.0), writes=[S_bf.reg])
    em.op("v", lambda v: v.memset(carry[:, :], 0.0), writes=[carry.reg])

    self.mark()
    if upto == "spill":
        self.release()
        return
    wrT = self.load_w_bf16("wrT", D, D)
    wkT = self.load_w_bf16("wkT", D, D)
    wvT = self.load_w_bf16("wvT", D, D)
    ldT = self.load_w_bf16("ldT", D, LR)
    luT = self.load_w_bf16("luT", LR, D)
    u_t = self.load_cols("u_col")
    wb_t = self.load_cols("wbase_col")

    if upto == "weights":
        self.release()
        return
    spA = [self.sb(f"spA{f}", (128, 256), F32) for f in range(NF)]
    spB = [self.sb(f"spB{f}", (128, 256), F32) for f in range(NF)]
    Acum = [self.sb(f"Acum{f}", (128, 256), F32) for f in range(NF)]
    Ek = [self.sb(f"Ek{f}", (128, 256), F32) for f in range(NF)]
    Arel = spA
    Er = Acum
    bases = self.sb("bases", (128, 4 * NF), F32)
    r_g = [self.sb(f"r_g{f}", (128, 256), BF16) for f in range(NF)]
    k_g = [self.sb(f"k_g{f}", (128, 256), BF16) for f in range(NF)]
    k1_g = [self.sb(f"k1g{f}", (128, 256), BF16) for f in range(NF)]
    ku_g = [self.sb(f"kug{f}", (128, 256), BF16) for f in range(NF)]
    v_g = [self.sb(f"v_g{t}", (128, D), BF16) for t in range(2)]
    ktm_g = [self.sb(f"ktmg{t}", (128, D), BF16) for t in range(2)]
    z1 = self.sb("z1", (64, 256), BF16)
    attC2 = [self.sb(f"attC{i}", (128, D), BF16) for i in range(2)]
    mevA = [self.sb("mev0", (128, 512), F32)] * 2
    mevB = [self.sb("mev20", (128, 512), F32)] * 2
    o_ev2 = [self.sb("o_ev0", (128, D), F32)] * 2

    B = [Bank(t) for t in self.pb]
    BA0, BA1, B20, B21, BM, BO0, BO1, BTR = range(8)

    def r3(ap, c=4, l=64):
        return ap.rearrange("p (c l) -> p c l", c=c)

    for g in range(NG):
        gsl = slice(256 * g, 256 * (g + 1))
        for dst, wT in ((r_g, wrT), (k_g, wkT)):
            for o in range(NF):
                for k in range(NF):
                    self.mmx(B[BO0], B[BO0].h[:, 0:256], wT[k],
                             wT[k][:, 128 * o:128 * (o + 1)], xln[k],
                             xln[k][:, gsl], stop=(k == NF - 1))
                self.evac(B[BO0], dst[o], dst[o][:, :],
                          in_ap=B[BO0].h[:, 0:256])
        for ts in range(2):
            tsl = slice(256 * g + 128 * ts, 256 * g + 128 * (ts + 1))
            for n in range(2):
                nsl = slice(512 * n, 512 * (n + 1))
                for k in range(NF):
                    self.mmx(B[BO1], B[BO1].h[:, 0:512], xln[k], xln[k][:, tsl],
                             wvT[k], wvT[k][:, nsl], stop=(k == NF - 1))
                self.evac(B[BO1], v_g[ts], v_g[ts][:, nsl],
                          in_ap=B[BO1].h[:, 0:512])
        if upto == "proj":
            continue
        for k in range(NF):
            self.mmx(B[BM], B[BM].h[0:64, 0:256], ldT[k], ldT[k][:, :],
                     xln[k], xln[k][:, gsl], stop=(k == NF - 1))
        self.evac(B[BM], z1, z1[:, :], in_ap=B[BM].h[0:64, 0:256], func=ACT.Tanh)
        for o in range(NF):
            self.mmx(B[BTR], B[BTR].h[:, 0:256], luT[0],
                     luT[0][:, 128 * o:128 * (o + 1)], z1, z1[:, :], stop=True)
            self.evac(B[BTR], spA[o], spA[o][:, :], in_ap=B[BTR].h[:, 0:256],
                      func=ACT.Exp, bias=wb_t[:, o:o + 1],
                      extra_reads=[wb_t.reg])
            self.act("s", spB[o], spB[o][:, :], spA[o], spA[o][:, :], ACT.Ln,
                     bias=self.ccol(1.0), extra_reads=[self.ccol_reg])
        for f in range(NF):
            em.op("v", lambda v, f=f: v.tensor_tensor_scan(
                Acum[f][:, :], spB[f][:, :], spB[f][:, :], carry[:, f:f + 1],
                ALU.add, ALU.bypass),
                reads=[spB[f].reg, carry.reg], writes=[Acum[f].reg])
            bsl = slice(4 * f, 4 * f + 4)
            self.copy("v", bases, bases[:, 4 * f:4 * f + 1], carry,
                      carry[:, f:f + 1])
            em.op("v", lambda v, f=f: v.tensor_copy(
                bases[:, 4 * f + 1:4 * f + 4], r3(Acum[f][:, :])[:, 0:3, 63]),
                reads=[Acum[f].reg], writes=[bases.reg])
            self.copy("v", carry, carry[:, f:f + 1], Acum[f], Acum[f][:, 255:256])
            em.op("v", lambda v, f=f, bsl=bsl: v.tensor_tensor(
                spB[f][:, 0:4], r3(Acum[f][:, :])[:, :, 63], bases[:, bsl],
                ALU.subtract),
                reads=[Acum[f].reg, bases.reg], writes=[spB[f].reg])
            em.op("s", lambda s, f=f, g=g: s.activation(
                eALt[:, 8 * 4 * g:8 * 4 * (g + 1)].rearrange(
                    "p (c q) -> p c q", q=8)[:, :, f],
                spB[f][:, 0:4], ACT.Exp, scale=-1.0),
                reads=[spB[f].reg], writes=[eALt.reg])
            self.copy("v", Bcol[f], Bcol[f][:, 4 * g:4 * g + 4], bases,
                      bases[:, bsl])
            em.op("v", lambda v, f=f, bsl=bsl: v.tensor_tensor(
                r3(Arel[f][:, :]), r3(Acum[f][:, :]),
                bases[:, bsl].unsqueeze(2).broadcast_to([128, 4, 64]),
                ALU.subtract),
                reads=[Acum[f].reg, bases.reg], writes=[Arel[f].reg])
            self.act("s", Ek[f], Ek[f][:, :], Arel[f], Arel[f][:, :], ACT.Exp)
            self.act("s", Er[f], Er[f][:, :], Arel[f], Arel[f][:, :], ACT.Exp,
                     scale=-1.0)
        if upto == "scan":
            continue
        for f in range(NF):
            em.op("v", lambda v, f=f, gsl=gsl: v.tensor_tensor(
                r3(rt[f][:, gsl])[:, :, 1:64], r3(r_g[f][:, :])[:, :, 1:64],
                r3(Er[f][:, :])[:, :, 0:63], ALU.mult),
                reads=[r_g[f].reg, Er[f].reg], writes=[rt[f].reg])
            em.op("v", lambda v, f=f, gsl=gsl: v.tensor_copy(
                r3(rt[f][:, gsl])[:, :, 0], r3(r_g[f][:, :])[:, :, 0]),
                reads=[r_g[f].reg], writes=[rt[f].reg])
            self.tt("v", k1_g[f], k1_g[f][:, :], k_g[f], k_g[f][:, :],
                    Ek[f], Ek[f][:, :], ALU.mult)
            self.ts("v", ku_g[f], ku_g[f][:, :], k_g[f], k_g[f][:, :],
                    u_t[:, f:f + 1], ALU.mult, s_regs=[u_t.reg])
            em.op("v", lambda v, f=f: v.tensor_tensor(
                r3(ku_g[f][:, :])[:, :, 1:64], r3(ku_g[f][:, :])[:, :, 1:64],
                r3(Ek[f][:, :])[:, :, 0:63], ALU.mult),
                reads=[ku_g[f].reg, Ek[f].reg], writes=[ku_g[f].reg])
        if upto == "trans":
            continue
        for ts in range(2):
            for f in range(NF):
                pv = B[BTR].h[:, 0:64].bitcast(BF16)
                em.op("t", lambda t, f=f, ts=ts, pv=pv: t.transpose(
                    pv, k1_g[f][:, 128 * ts:128 * (ts + 1)], self.ident[:, :]),
                    reads=[k1_g[f].reg, self.ident.reg], writes=[B[BTR].reg])
                self.em.op("s", lambda s, f=f, ts=ts, pv=pv: s.copy(
                    ktm_g[ts][:, 128 * f:128 * (f + 1)], pv),
                    reads=[B[BTR].reg], writes=[ktm_g[ts].reg])
                B[BTR].fresh = True
        if upto == "ktm":
            continue
        for cl in range(4):
            c = 4 * g + cl
            cpar = c % 2
            attC, mev, mev2 = attC2[cpar], mevA[cpar], mevB[cpar]
            o_ev = o_ev2[(c // 2) % 2]
            ts = cl // 2
            csl = slice(64 * cl, 64 * (cl + 1))
            vrow = slice(64 * cpar, 64 * (cpar + 1))
            srow = slice(64 * (1 - cpar), 64 * (2 - cpar))
            for h in range(H):
                ft, par = h // 2, h % 2
                hb = slice(64 * par, 64 * (par + 1))
                ba = B[BA0 + (h // 8)]
                col = slice(64 * (h % 8), 64 * (h % 8 + 1))
                rt_ap = rt[ft][hb, 64 * c:64 * (c + 1)]
                if c > 0:
                    self.mmx(ba, ba.h[srow, col], S_bf,
                             S_bf[hb, 64 * ft:64 * ft + 64], rt[ft], rt_ap,
                             tp=(64 * par, 64 * (1 - cpar)), stop=True,
                             pbase=64 * (1 - cpar))
                self.mmx(ba, ba.h[vrow, col], k1_g[ft], k1_g[ft][hb, csl],
                         rt[ft], rt_ap, tp=(64 * par, 64 * cpar), stop=True,
                         pbase=64 * cpar)
                b2 = B[B20 + (h // 8)]
                self.mmx(b2, b2.h[vrow, col], ku_g[ft], ku_g[ft][hb, csl],
                         rt[ft], rt_ap, tp=(64 * par, 64 * cpar), stop=True,
                         pbase=64 * cpar)
            for b in range(2):
                ba, b2 = B[BA0 + b], B[B20 + b]
                hc = slice(512 * b, 512 * (b + 1))
                if c > 0:
                    self.evac(ba, attC, attC[srow, hc], in_ap=ba.h[srow, :])
                    ba.fresh = False
                else:
                    em.op("v", lambda v, hc=hc, srow=srow, attC=attC: v.memset(
                        attC[srow, hc], 0.0), writes=[attC.reg])
                em.op("v", lambda v, ba=ba, vrow=vrow, mev=mev: v.tensor_tensor(
                    r3(mev[vrow, 0:512], c=8), r3(ba.h[vrow, :], c=8),
                    self.mstrict[vrow, :].unsqueeze(1).broadcast_to([64, 8, 64]),
                    ALU.mult),
                    reads=[ba.reg, self.mstrict.reg], writes=[mev.reg])
                em.op("v", lambda v, b2=b2, vrow=vrow, mev2=mev2: v.tensor_tensor(
                    r3(mev2[vrow, 0:512], c=8), r3(b2.h[vrow, :], c=8),
                    self.mdiag[vrow, :].unsqueeze(1).broadcast_to([64, 8, 64]),
                    ALU.mult),
                    reads=[b2.reg, self.mdiag.reg], writes=[mev2.reg])
                em.op("v", lambda v, hc=hc, vrow=vrow, attC=attC, mev=mev,
                      mev2=mev2: v.tensor_add(
                    attC[vrow, hc], mev[vrow, 0:512], mev2[vrow, 0:512]),
                    reads=[mev.reg, mev2.reg], writes=[attC.reg])
                ba.fresh = True
                b2.fresh = True
            for h in range(H):
                ft, par = h // 2, h % 2
                ob = B[BO0 + (h // 8)]
                ocol = slice(64 * (h % 8), 64 * (h % 8 + 1))
                hcol = slice(64 * h, 64 * (h + 1))
                if c > 0:
                    self.mmx(ob, ob.h[vrow, ocol], attC, attC[srow, hcol],
                             self.ident, self.ident[srow, srow],
                             tp=(64 * (1 - cpar), 64 * cpar), pbase=64 * cpar)
                self.mmx(ob, ob.h[vrow, ocol], attC, attC[vrow, hcol],
                         v_g[ts], v_g[ts][vrow, 64 * h:64 * h + 64],
                         tp=(64 * cpar, 64 * cpar), stop=(cpar == 1),
                         pbase=64 * cpar)
            for h in range(H):
                ft, par = h // 2, h % 2
                self.mmx(B[BM], B[BM].h[64 * par:64 * (par + 1),
                                        64 * ft:64 * ft + 64],
                         ktm_g[ts], ktm_g[ts][vrow, 64 * h:64 * h + 64],
                         v_g[ts], v_g[ts][vrow, 64 * h:64 * h + 64],
                         tp=(64 * cpar, 64 * par), stop=True, pbase=64 * par)
            em.op("v", lambda v: v.tensor_add(S_tmp[:, :], S_cur[:, :],
                                              B[BM].h[:, :]),
                  reads=[S_cur.reg, B[BM].reg], writes=[S_tmp.reg])
            B[BM].fresh = True
            em.op("v", lambda v, c=c: v.tensor_tensor(
                S_cur[:, :].rearrange("p (q l) -> p q l", q=8),
                S_tmp[:, :].rearrange("p (q l) -> p q l", q=8),
                eALt[:, 8 * c:8 * c + 8].unsqueeze(2).broadcast_to([128, 8, 64]),
                ALU.mult),
                reads=[S_tmp.reg, eALt.reg], writes=[S_cur.reg])
            self.copy("v", S_bf, S_bf[:, :], S_cur, S_cur[:, :])
            if cpar == 1:
                tt_idx = c // 2
                for b in range(2):
                    ob = B[BO0 + b]
                    self.evac(ob, o_ev, o_ev[:, 512 * b:512 * (b + 1)])
                em.dma("y", outacc.h[128 * tt_idx:128 * (tt_idx + 1), :],
                       o_ev[:, :], reads=[o_ev.reg], writes=[outacc.reg])
                if filler is not None:
                    filler(-1)
        if filler is not None:
            filler(g)
    self.release()


def emit_l0a_collective(self):
    em = self.em
    st = self.l0a
    S_cur, carry = st["S_cur"], st["carry"]
    S_in, S_inb, P_run, Dg = st["S_in"], st["S_inb"], st["P_run"], st["Dg"]
    cc_in, cc_out = st["cc_in"], st["cc_out"]
    self.act("s", Dg, Dg[:, :], carry, carry[:, :], ACT.Exp, scale=-1.0)
    em.dma("y", cc_in.h[:, 0:512], S_cur[:, :], reads=[S_cur.reg],
           writes=[cc_in.reg])
    em.dma("y", cc_in.h[:, 512:520], Dg[:, :], reads=[Dg.reg],
           writes=[cc_in.reg])
    self.em.collective("AllGather", ALU.bypass, ins=[cc_in.h[:, :]],
                       outs=[cc_out.h[:, :, :]], reads=[cc_in.reg],
                       writes=[cc_out.reg])
    self.mark()
    Sr = self.sb("cc_Sr", (128, 512), F32)
    Dr = self.sb("cc_Dr", (128, 8), F32)
    tmp = self.sb("cc_tmp", (128, 512), F32)
    oh = self.inputs["ohprev_mat"]
    oh_t = self.sb("cc_oh", (128, 8), F32)
    em.dma("y", oh_t[:, :], oh.h[:, :], reads=[oh.reg], writes=[oh_t.reg])
    em.op("v", lambda v: v.memset(P_run[:, :], 0.0), writes=[P_run.reg])
    em.op("v", lambda v: v.memset(S_in[:, :], 0.0), writes=[S_in.reg])
    for r in range(8):
        em.dma("y", Sr[:, :], cc_out.h[r, :, 0:512], reads=[cc_out.reg],
               writes=[Sr.reg])
        em.dma("y", Dr[:, :], cc_out.h[r, :, 512:520], reads=[cc_out.reg],
               writes=[Dr.reg])
        self.ts("v", tmp, tmp[:, :], P_run, P_run[:, :], oh_t[:, r:r + 1],
                ALU.mult, s_regs=[oh_t.reg])
        self.tt("v", S_in, S_in[:, :], S_in, S_in[:, :], tmp, tmp[:, :], ALU.add)
        em.op("v", lambda v: v.tensor_tensor(
            tmp[:, :].rearrange("p (q l) -> p q l", q=8),
            P_run[:, :].rearrange("p (q l) -> p q l", q=8),
            Dr[:, :].unsqueeze(2).broadcast_to([128, 8, 64]), ALU.mult),
            reads=[P_run.reg, Dr.reg], writes=[tmp.reg])
        self.tt("v", P_run, P_run[:, :], tmp, tmp[:, :], Sr, Sr[:, :], ALU.add)
    self.copy("v", S_inb, S_inb[:, :], S_in, S_in[:, :])
    self.release()


def emit_l0a_sweep2(self, xln, upto='all'):
    em, Tc, NCH, NT = self.em, self.Tc, self.NCH, self.NT
    st = self.l0a
    rt, Bcol = st["rt"], st["Bcol"]
    S_inb = st["S_inb"]
    outacc = st["outacc"]
    xsrc = self.xdram
    xdst = self.xdram_next()

    self.mark()
    if upto == "p1":
        self.release()
        return
    wgT = self.load_w_bf16("wgT", D, D)
    woT = self.load_w_bf16("woT", D, D)
    if upto == "p2":
        self.release()
        return
    lng = self.inputs["lng_rep"]
    lnb = self.inputs["lnb_rep"]
    lng_t = self.sb("lng_t", (128, D), F32)
    lnb_t = self.sb("lnb_t", (128, D), F32)
    em.dma("y", lng_t[:, :], lng.h[:, :], reads=[lng.reg], writes=[lng_t.reg])
    em.dma("y", lnb_t[:, :], lnb.h[:, :], reads=[lnb.reg], writes=[lnb_t.reg])
    if upto == "p3":
        self.release()
        return
    ebcT = [self.sb(f"ebcT{f}", (128, NCH), F32) for f in range(NF)]
    for f in range(NF):
        self.act("s", ebcT[f], ebcT[f][:, :], Bcol[f], Bcol[f][:, :], ACT.Exp,
                 scale=-1.0)
    if upto == "p4":
        self.release()
        return
    rtl_t = self.sb("rtl_t", (128, NF * 128), BF16)
    xt_t = self.sb("xt_t", (128, NF * 128), F32)
    xu_t = self.sb("xu_t", (128, NF * 128), F32)
    if upto == "prep":
        self.release()
        return
    DBG = False
    if DBG:
        d_ofull = self.dscratch("d_ofull", (128, D), F32)
        d_lnout = self.dscratch("d_lnout", (128, D), F32)
        d_g = self.dscratch("d_g", (128, D), F32)
        d_sin = self.dscratch("d_sin", (128, 512), F32)
        d_stat = self.dscratch("d_stat", (128, 64), F32)
        d_rtil = self.dscratch("d_rtil", (128, D), F32)
        d_lngT = self.dscratch("d_lngT", (128, 8 * 128), F32)
        d_psum = self.dscratch("d_psum", (128, 512), F32)
        dbg_sb = self.sb("dbg_sb", (128, 512), F32)
    oacc_t = self.sb("oacc_t", (128, D), F32)
    ofull = self.sb("ofull", (128, D), F32)
    g_t = self.sb("g_sig", (128, D), F32)
    stat = self.sb("stat", (128, 64), F32)
    sqv = self.sb("sqv", (128, D), F32)
    lnout = self.sb("lnout", (128, D), BF16)
    lngT = self.sb("lngT", (128, D), BF16)

    B = [Bank(t) for t in self.pb]
    BOUT0, BOUT1, BG0, BG1, BTR, BP0, BP1, BXX = range(8)

    for tt in range(NT):
        tsl = slice(128 * tt, 128 * (tt + 1))
        em.dma("y", oacc_t[:, :], outacc.h[tsl, :], reads=[outacc.reg],
               writes=[oacc_t.reg])
        em.dma("y", xt_t[:, :],
               xsrc.h.rearrange("(f p) t -> p f t", p=128)[:, :, tsl],
               reads=[xsrc.reg], writes=[xt_t.reg])
        for f in range(NF):
            em.op("v", lambda v, f=f, tt=tt: v.tensor_tensor(
                rtl_t[:, 128 * f:128 * (f + 1)].rearrange(
                    "p (c l) -> p c l", l=64),
                rt[f][:, 128 * tt:128 * (tt + 1)].rearrange(
                    "p (c l) -> p c l", l=64),
                ebcT[f][:, 2 * tt:2 * tt + 2].unsqueeze(2).broadcast_to(
                    [128, 2, 64]), ALU.mult),
                reads=[rt[f].reg, ebcT[f].reg], writes=[rtl_t.reg])
        for cl in range(2):
            c = 2 * tt + cl
            vrow = slice(64 * (c % 2), 64 * (c % 2 + 1))
            for h in range(H):
                ft, par = h // 2, h % 2
                hb = slice(64 * par, 64 * (par + 1))
                ob = B[BOUT0 + (h // 8)]
                ocol = slice(64 * (h % 8), 64 * (h % 8 + 1))
                self.mmx(ob, ob.h[vrow, ocol], rtl_t,
                         rtl_t[hb, 128 * ft + 64 * cl:128 * ft + 64 * cl + 64],
                         S_inb, S_inb[hb, 64 * ft:64 * ft + 64],
                         tp=(64 * par, 64 * (c % 2)), stop=True,
                         pbase=64 * (c % 2))
        if upto == "d1b":
            continue
        for n in range(2):
            nsl = slice(512 * n, 512 * (n + 1))
            for k in range(NF):
                self.mmx(B[BG0 + n], B[BG0 + n].h[:, 0:512], xln[k],
                         xln[k][:, tsl], wgT[k], wgT[k][:, nsl],
                         stop=(k == NF - 1))
            self.evac(B[BG0 + n], g_t, g_t[:, nsl], func=ACT.Sigmoid,
                      in_ap=B[BG0 + n].h[:, 0:512])
        for b in range(2):
            ob = B[BOUT0 + b]
            em.op("v", lambda v, b=b, ob=ob: v.tensor_add(
                ofull[:, 512 * b:512 * (b + 1)], ob.h[:, :],
                oacc_t[:, 512 * b:512 * (b + 1)]),
                reads=[ob.reg, oacc_t.reg], writes=[ofull.reg])
            ob.fresh = True
        if upto == "g":
            continue
        if DBG and tt == 0:
            em.dma("g", d_ofull.h[:, :], ofull[:, :], reads=[ofull.reg],
                   writes=[d_ofull.reg])
            em.dma("g", d_g.h[:, :], g_t[:, :], reads=[g_t.reg],
                   writes=[d_g.reg])
            em.dma("g", d_sin.h[:, :], st["S_inb"][:, :], reads=[st["S_inb"].reg],
                   writes=[d_sin.reg])
            em.dma("g", d_rtil.h[:, 0:Tc], rtil[0][:, :], reads=[rtil[0].reg],
                   writes=[d_rtil.reg])
        em.op("v", lambda v: v.reduce_sum(
            stat[:, 0:16], ofull[:, :].rearrange("p (h j) -> p h j", j=64), AX),
            reads=[ofull.reg], writes=[stat.reg])
        self.tt("v", sqv, sqv[:, :], ofull, ofull[:, :], ofull, ofull[:, :],
                ALU.mult)
        em.op("v", lambda v: v.reduce_sum(
            stat[:, 16:32], sqv[:, :].rearrange("p (h j) -> p h j", j=64), AX),
            reads=[sqv.reg], writes=[stat.reg])
        self.ts("v", stat, stat[:, 0:16], stat, stat[:, 0:16], 1.0 / 64, ALU.mult)
        self.ts("v", stat, stat[:, 16:32], stat, stat[:, 16:32], 1.0 / 64,
                ALU.mult)
        em.op("v", lambda v: v.scalar_tensor_tensor(
            stat[:, 32:48], stat[:, 0:16], 1.0, stat[:, 0:16], ALU.mult,
            ALU.mult), reads=[stat.reg], writes=[stat.reg])
        self.tt("v", stat, stat[:, 16:32], stat, stat[:, 16:32], stat,
                stat[:, 32:48], ALU.subtract)
        self.ts("v", stat, stat[:, 16:32], stat, stat[:, 16:32], EPS, ALU.add)
        self.act("s", stat, stat[:, 32:48], stat, stat[:, 16:32], ACT.Sqrt)
        self.rsqrt_refine(stat, stat[:, 48:64], stat, stat[:, 32:48],
                          stat[:, 16:32])
        self.tt("v", oacc_t, oacc_t[:, :], lnb_t, lnb_t[:, :], g_t, g_t[:, :],
                ALU.mult)
        self.tt("v", g_t, g_t[:, :], lng_t, lng_t[:, :], g_t, g_t[:, :],
                ALU.mult)
        em.op("v", lambda v: v.tensor_tensor(
            sqv[:, :].rearrange("p (h j) -> p h j", j=64),
            ofull[:, :].rearrange("p (h j) -> p h j", j=64),
            stat[:, 0:16].unsqueeze(2).broadcast_to([128, 16, 64]),
            ALU.subtract), reads=[ofull.reg, stat.reg], writes=[sqv.reg])
        em.op("v", lambda v: v.tensor_tensor(
            sqv[:, :].rearrange("p (h j) -> p h j", j=64),
            sqv[:, :].rearrange("p (h j) -> p h j", j=64),
            stat[:, 48:64].unsqueeze(2).broadcast_to([128, 16, 64]),
            ALU.mult), reads=[sqv.reg, stat.reg], writes=[sqv.reg])
        em.op("v", lambda v: v.tensor_tensor(
            sqv[:, :], sqv[:, :], g_t[:, :], ALU.mult),
            reads=[sqv.reg, g_t.reg], writes=[sqv.reg])
        self.tt("v", lnout, lnout[:, :], sqv, sqv[:, :], oacc_t, oacc_t[:, :],
                ALU.add)
        if upto == "ln":
            continue
        if DBG and tt == 0:
            em.dma("g", d_lnout.h[:, :], lnout[:, :], reads=[lnout.reg],
                   writes=[d_lnout.reg])
            em.dma("g", d_stat.h[:, :], stat[:, :], reads=[stat.reg],
                   writes=[d_stat.reg])
        for k in range(NF):
            pv = B[BTR].h[:, 0:64].bitcast(BF16)
            em.op("t", lambda t, k=k, pv=pv: t.transpose(
                pv, lnout[:, 128 * k:128 * (k + 1)], self.ident[:, :]),
                reads=[lnout.reg, self.ident.reg], writes=[B[BTR].reg])
            self.em.op("s", lambda s, k=k, pv=pv: s.copy(
                lngT[:, 128 * k:128 * (k + 1)], pv),
                reads=[B[BTR].reg], writes=[lngT.reg])
        if upto == "trans":
            continue
        if DBG and tt == 0:
            em.dma("g", d_lngT.h[:, :], lngT[:, :], reads=[lngT.reg],
                   writes=[d_lngT.reg])
        for o in range(NF):
            pbk = B[BP0 + (o // 4)]
            ocol = slice(128 * (o % 4), 128 * (o % 4 + 1))
            for k in range(NF):
                self.mmx(pbk, pbk.h[:, ocol], woT[k],
                         woT[k][:, 128 * o:128 * (o + 1)], lngT,
                         lngT[:, 128 * k:128 * (k + 1)], stop=(k == NF - 1))
        if DBG and tt == 0:
            self.em.op("s", lambda s: s.copy(dbg_sb[:, :], B[BP0].h[:, :]),
                       reads=[B[BP0].reg], writes=[dbg_sb.reg])
            em.dma("g", d_psum.h[:, :], dbg_sb[:, :], reads=[dbg_sb.reg],
                   writes=[d_psum.reg])
        for b in range(2):
            pbk = B[BP0 + b]
            em.op("v", lambda v, b=b, pbk=pbk: v.tensor_tensor(
                xu_t[:, 512 * b:512 * (b + 1)], pbk.h[:, :],
                xt_t[:, 512 * b:512 * (b + 1)], ALU.add),
                reads=[pbk.reg, xt_t.reg], writes=[xu_t.reg])
        em.dma("y", xdst.h.rearrange("(f p) t -> p f t", p=128)[:, :, tsl],
               xu_t[:, :], reads=[xu_t.reg], writes=[xdst.reg])
        B[BP0].fresh = True
        B[BP1].fresh = True
    self.xdram = xdst
    self.release()


MK.emit_l0a_sweep1 = emit_l0a_sweep1
MK.emit_l0a_collective = emit_l0a_collective
MK.emit_l0a_sweep2 = emit_l0a_sweep2


QMAX = 32767.0
MAGIC = 12582912.0


def _gmax(self, tag, loc_col, loc_t, nvals, cc_sem_note=""):
    em = self.em
    b1 = self.dscratch(f"gm_b1_{tag}", (128,), F32)
    cin = self.dscratch(f"gm_ci_{tag}", (nvals,), F32)
    cout = self.dscratch(f"gm_co_{tag}", (8, nvals), F32)
    t2 = self.sb(f"gm_t2_{tag}", (128, 128), F32)
    res = self.sb(f"gm_res_{tag}", (128, max(nvals, 8)), F32)
    for v in range(nvals):
        em.dma("y", b1.h[:], loc_col[:, v:v + 1], reads=[loc_t.reg],
               writes=[b1.reg])
        em.dma("y", t2[:, :], b1.h[:].unsqueeze(0).partition_broadcast(128),
               reads=[b1.reg], writes=[t2.reg])
        em.op("v", lambda v_, v2=v: v_.reduce_max(res[:, v2:v2 + 1], t2[:, :],
                                                  AX),
              reads=[t2.reg], writes=[res.reg])
    em.dma("y", cin.h[:], res[0:1, 0:nvals], reads=[res.reg], writes=[cin.reg])
    self.em.collective("AllGather", ALU.bypass, ins=[cin.h[:]],
                       outs=[cout.h[:, :]], reads=[cin.reg], writes=[cout.reg])
    t3 = self.sb(f"gm_t3_{tag}", (128, 8 * nvals), F32)
    em.dma("y", t3[:, :],
           cout.h[:, :].flatten().unsqueeze(0).partition_broadcast(128),
           reads=[cout.reg], writes=[t3.reg])
    out = self.sb(f"gm_out_{tag}", (128, nvals), F32)
    em.op("v", lambda v_: v_.reduce_max(
        out[:, 0:nvals].unsqueeze(2),
        t3[:, :].rearrange("p (r v) -> p v r", v=nvals), AX),
        reads=[t3.reg], writes=[out.reg])
    return out


def _qdq_scales(self, sc, c0, gmax_t, col, qmax=QMAX):
    self.em.op("v", lambda v: v.tensor_scalar(
        sc[:, c0:c0 + 1], col, 1.0 / qmax, 1e-12, ALU.mult, ALU.max),
        reads=[gmax_t.reg], writes=[sc.reg])
    self.em.op("v", lambda v: v.reciprocal(sc[:, c0 + 1:c0 + 2],
                                           sc[:, c0:c0 + 1]),
               reads=[sc.reg], writes=[sc.reg])


DQW = {"g": ("wgateN", "wgateT"), "u": ("wupN", "wupT"),
       "d": ("wdownN", "wdownT")}


def emit_dq_scales(self, keys=("g", "u", "d")):
    em = self.em
    if not hasattr(self, "dqctx"):
        self.dqg_d = self.dscratch("dqm_g", (D, FI), BF16)
        self.dqu_d = self.dscratch("dqm_u", (D, FI), BF16)
        self.dqd_d = self.dscratch("dqm_d", (FI, D), BF16)
        self.dqctx = {}
    bank = Bank(self.pb[7])
    self.mark()
    shapes = {"g": (self.dqg_d, FI, D), "u": (self.dqu_d, FI, D),
              "d": (self.dqd_d, D, FI)}
    IMX = max(shapes[k][2] for k in keys)
    wb2 = [self.sb(f"dqs_w{i}_{keys[0]}", (128, IMX), F32) for i in range(2)]
    mbuf = self.sb(f"dqs_m_{keys[0]}", (128, IMX // 32), F32)
    sall = self.sb(f"dqs_s_{keys[0]}", (128, 768), F32)
    s2all = self.sb(f"dqs_s2_{keys[0]}", (128, 768), F32)
    sb_ev = self.sb(f"dqs_ev_{keys[0]}", (128, 128), F32)
    for wkey in keys:
        wnat_n, wT_n = DQW[wkey]
        out_d, O, I = shapes[wkey]
        nat = self.inputs[wnat_n]
        nb = I // 32
        not_ = O // 128
        invT_d = self.dscratch(f"dqs_iT_{wkey}", (nb, O), F32)
        sT_d = self.dscratch(f"dqs_sT_{wkey}", (nb, O), F32)
        self.dqctx[wkey] = (self.inputs[wT_n], invT_d, sT_d, out_d, O, I)
        for ot in range(not_):
            rows = slice(128 * ot, 128 * (ot + 1))
            wbuf = wb2[ot % 2]
            em.dma("y", wbuf[:, 0:I], nat.h[rows, :], reads=[nat.reg],
                   writes=[wbuf.reg])
            em.op("v", lambda v, nb=nb, I=I, wbuf=wbuf: v.reduce_max(
                mbuf[:, 0:nb],
                wbuf[:, 0:I].rearrange("p (b q) -> p b q", q=32), AX,
                apply_absolute_value=True),
                reads=[wbuf.reg], writes=[mbuf.reg])
            em.op("v", lambda v, ot=ot, nb=nb: v.tensor_scalar(
                s2all[:, nb * ot:nb * (ot + 1)], mbuf[:, 0:nb], 1.0 / 7.0,
                1e-12, ALU.mult, ALU.max),
                reads=[mbuf.reg], writes=[s2all.reg])
            em.op("v", lambda v, ot=ot, nb=nb: v.reciprocal(
                sall[:, nb * ot:nb * (ot + 1)],
                s2all[:, nb * ot:nb * (ot + 1)]),
                reads=[s2all.reg], writes=[sall.reg])
        for buf, dst in ((sall, invT_d), (s2all, sT_d)):
            for ot in range(not_):
                rows = slice(128 * ot, 128 * (ot + 1))
                em.op("t", lambda t, buf=buf, ot=ot, nb=nb: t.transpose(
                    bank.h[0:nb, 0:128], buf[0:128, nb * ot:nb * (ot + 1)],
                    self.identf[:, :]),
                    reads=[buf.reg, self.identf.reg], writes=[bank.reg])
                em.op("s", lambda s, nb=nb: s.copy(sb_ev[0:nb, 0:128],
                                                   bank.h[0:nb, 0:128]),
                      reads=[bank.reg], writes=[sb_ev.reg])
                bank.fresh = True
                em.dma("y", dst.h[:, rows], sb_ev[0:nb, 0:128],
                       reads=[sb_ev.reg], writes=[dst.reg])
    self.release()


def emit_dq_piece_bufs(self):
    OCW = 512
    self.dqpb = (self.sb("dqp_wt", (128, OCW), F32),
                 self.sb("dqp_iv", (128, OCW), F32),
                 self.sb("dqp_sb", (128, OCW), F32),
                 self.sb("dqp_ob", (128, OCW), BF16), OCW)


def emit_dq_wt_pieces(self, wkey):
    em = self.em
    wT, invT_d, sT_d, out_d, O, I = self.dqctx[wkey]
    wtb, invb, sbb, ob, OCW = self.dqpb
    nkt = I // 128
    noc = O // OCW
    nbk = 128 // 32
    pieces = []

    def mk(k, oc):
        def go():
            rows = slice(128 * k, 128 * (k + 1))
            ocs = slice(OCW * oc, OCW * (oc + 1))
            em.dma("y", wtb[:, :], wT.h[rows, ocs], reads=[wT.reg],
                   writes=[wtb.reg])
            em.dma("y", invb[:, :],
                   invT_d.h[nbk * k:nbk * (k + 1), ocs].unsqueeze(1)
                   .broadcast_to([nbk, 32, OCW]),
                   reads=[invT_d.reg], writes=[invb.reg])
            em.dma("y", sbb[:, :],
                   sT_d.h[nbk * k:nbk * (k + 1), ocs].unsqueeze(1)
                   .broadcast_to([nbk, 32, OCW]),
                   reads=[sT_d.reg], writes=[sbb.reg])
            em.op("v", lambda v: v.tensor_mul(wtb[:, :], wtb[:, :],
                                              invb[:, :]),
                  reads=[wtb.reg, invb.reg], writes=[wtb.reg])
            em.op("v", lambda v: v.tensor_scalar(
                invb[:, :], wtb[:, :], MAGIC, MAGIC, ALU.add, ALU.subtract),
                reads=[wtb.reg], writes=[invb.reg])
            em.op("v", lambda v: v.tensor_tensor(
                ob[:, :], invb[:, :], sbb[:, :], ALU.mult),
                reads=[invb.reg, sbb.reg], writes=[ob.reg])
            em.dma("y", out_d.h[rows, ocs], ob[:, :], reads=[ob.reg],
                   writes=[out_d.reg])
        return go

    for k in range(nkt):
        for oc in range(noc):
            pieces.append(mk(k, oc))
    return pieces


def emit_l0f(self, xq, wt_pieces=None):
    em, Tc, NT = self.em, self.Tc, self.NT
    B = [Bank(t) for t in self.pb]
    gate_d = self.dscratch("gate_raw", (FI, Tc), F32)
    up_d = self.dscratch("up_raw", (FI, Tc), F32)
    g2_d = self.dscratch("g2_raw", (FI, Tc), F32)
    o_d = self.dscratch("o_raw", (FI, Tc), F32)
    xdst = self.xdram_next()
    xsrc = self.xdram

    NO = FI // 128
    nch = (Tc + 511) // 512
    CW = min(512, Tc)
    SC = self.sb("l0f_scales", (128, 16), F32)
    sc_g, sc_u, sc_g2, sc_o = SC[:, 0:2], SC[:, 2:4], SC[:, 4:6], SC[:, 6:8]

    self.mark()
    dqT = self.sbs("dq_gT", (128, FI), BF16, NF)
    locmax = self.sb("l0f_locmax", (128, 2), F32)
    em.op("v", lambda v: v.memset(locmax[:, :], 0.0), writes=[locmax.reg])
    ev2 = [self.sb(f"l0f_ev{i}", (128, Tc), F32) for i in range(2)]
    mcol = self.sb("l0f_mcol", (128, NO), F32)
    up_list, dn_list = wt_pieces if wt_pieces is not None else ([], [])
    for wi, (dq_d, dst_d, mc) in enumerate(
            ((self.dqg_d, gate_d, 0), (self.dqu_d, up_d, 1))):
        if wi == 1:
            while up_list:
                up_list.pop(0)()
        for k in range(NF):
            em.dma("y", dqT[k][:, :], dq_d.h[128 * k:128 * (k + 1), :],
                   reads=[dq_d.reg], writes=[dqT[k].reg])
        drip = up_list if wi == 0 else dn_list
        for o in range(NO):
            ev = ev2[o % 2]
            for c in range(nch):
                tsl = slice(CW * c, CW * (c + 1))
                bk = B[c % 2]
                for k in range(NF):
                    self.mmx(bk, bk.h[:, 0:CW], dqT[k],
                             dqT[k][:, 128 * o:128 * (o + 1)], xq[k],
                             xq[k][:, tsl], stop=(k == NF - 1))
                self.evac(bk, ev, ev[:, tsl], in_ap=bk.h[:, 0:CW])
            em.op("v", lambda v, o=o, ev=ev: v.reduce_max(
                mcol[:, o:o + 1], ev[:, :], AX, apply_absolute_value=True),
                reads=[ev.reg], writes=[mcol.reg])
            em.dma("y", dst_d.h[128 * o:128 * (o + 1), :], ev[:, :],
                   reads=[ev.reg], writes=[dst_d.reg])
            for _ in range(2):
                if drip:
                    drip.pop(0)()
        em.op("v", lambda v, mc=mc: v.reduce_max(
            locmax[:, mc:mc + 1], mcol[:, :], AX),
            reads=[mcol.reg], writes=[locmax.reg])
    gm = _gmax(self, "q23", locmax[:, 0:2], locmax, 2)
    _qdq_scales(self, SC, 0, gm, gm[:, 0:1])
    _qdq_scales(self, SC, 2, gm, gm[:, 1:2])
    self.em.op("v", lambda v: v.tensor_scalar(
        SC[:, 8:9], SC[:, 0:1], 1.0 / 65536.0, None, ALU.mult),
        reads=[SC.reg], writes=[SC.reg])

    t_f2 = [self.sb(f"l0f_tf{i}", (128, Tc), F32) for i in range(2)]
    kg2 = [self.sb(f"l0f_kg{i}", (128, Tc), F32) for i in range(2)]
    ms2 = [self.sb(f"l0f_ms{i}", (128, Tc), F32) for i in range(2)]
    sigf2 = [self.sb(f"l0f_sig{i}", (128, Tc), F32) for i in range(2)]
    em.op("v", lambda v: v.memset(locmax[:, 0:1], 0.0), writes=[locmax.reg])
    for o in range(NO):
        ev, t_f, kg, ms, sigf = (ev2[o % 2], t_f2[o % 2], kg2[o % 2],
                                 ms2[o % 2], sigf2[o % 2])
        rows = slice(128 * o, 128 * (o + 1))
        em.dma("s", ev[:, :], gate_d.h[rows, :], reads=[gate_d.reg],
               writes=[ev.reg])
        self.ts("v", t_f, t_f[:, :], ev, ev[:, :], SC[:, 1:2], ALU.mult,
                s_regs=[SC.reg])
        self.ts("v", kg, kg[:, :], t_f, t_f[:, :], MAGIC, ALU.add, MAGIC,
                ALU.subtract)
        em.op("s", lambda s, sigf=sigf, kg=kg: s.activation(
            sigf[:, :], kg[:, :], ACT.Sigmoid, scale=SC[:, 0:1]),
              reads=[kg.reg, SC.reg], writes=[sigf.reg])
        self.act("s", t_f, t_f[:, :], sigf, sigf[:, :], ACT.Relu,
                 scale=65536.0, bias=self.ccol(MAGIC),
                 extra_reads=[self.ccol_reg])
        self.act("s", ms, ms[:, :], t_f, t_f[:, :], ACT.Relu,
                 bias=self.ccol(-MAGIC), extra_reads=[self.ccol_reg])
        self.ts("v", t_f, t_f[:, :], ms, ms[:, :], 65535.0, ALU.min)
        em.op("v", lambda v, ev=ev, t_f=t_f, kg=kg: v.scalar_tensor_tensor(
            ev[:, :], t_f[:, :], SC[:, 8:9], kg[:, :], ALU.mult, ALU.mult),
            reads=[t_f.reg, SC.reg, kg.reg], writes=[ev.reg])
        em.op("v", lambda v, o=o, ev=ev: v.reduce_max(
            mcol[:, o:o + 1], ev[:, :], AX, apply_absolute_value=True),
            reads=[ev.reg], writes=[mcol.reg])
        em.dma("y", g2_d.h[rows, :], ev[:, :], reads=[ev.reg], writes=[g2_d.reg])
    em.op("v", lambda v: v.reduce_max(locmax[:, 0:1], mcol[:, :], AX),
          reads=[mcol.reg], writes=[locmax.reg])
    gm4 = _gmax(self, "q4", locmax[:, 0:1], locmax, 1)
    _qdq_scales(self, SC, 4, gm4, gm4[:, 0:1])
    em.op("v", lambda v: v.tensor_tensor(SC[:, 9:10], SC[:, 4:5],
                                         SC[:, 2:3], ALU.mult),
          reads=[SC.reg], writes=[SC.reg])

    em.op("v", lambda v: v.memset(locmax[:, 0:1], 0.0), writes=[locmax.reg])
    for o in range(NO):
        ev, t_f, kg, ms = ev2[o % 2], t_f2[o % 2], kg2[o % 2], ms2[o % 2]
        rows = slice(128 * o, 128 * (o + 1))
        em.dma("s", ev[:, :], g2_d.h[rows, :], reads=[g2_d.reg], writes=[ev.reg])
        self.act("s", t_f, t_f[:, :], ev, ev[:, :], ACT.Relu,
                 scale=SC[:, 5:6], bias=self.ccol(MAGIC),
                 extra_reads=[SC.reg, self.ccol_reg])
        self.ts("v", kg, kg[:, :], t_f, t_f[:, :], -MAGIC, ALU.add)
        em.dma("s", ev[:, :], up_d.h[rows, :], reads=[up_d.reg], writes=[ev.reg])
        self.ts("v", t_f, t_f[:, :], ev, ev[:, :], SC[:, 3:4], ALU.mult,
                s_regs=[SC.reg])
        self.ts("v", ms, ms[:, :], t_f, t_f[:, :], MAGIC, ALU.add, MAGIC,
                ALU.subtract)
        em.op("v", lambda v, ev=ev, kg=kg, ms=ms: v.scalar_tensor_tensor(
            ev[:, :], kg[:, :], SC[:, 9:10], ms[:, :], ALU.mult, ALU.mult),
            reads=[kg.reg, SC.reg, ms.reg], writes=[ev.reg])
        em.op("v", lambda v, o=o, ev=ev: v.reduce_max(
            mcol[:, o:o + 1], ev[:, :], AX, apply_absolute_value=True),
            reads=[ev.reg], writes=[mcol.reg])
        em.dma("y", o_d.h[rows, :], ev[:, :], reads=[ev.reg], writes=[o_d.reg])
    em.op("v", lambda v: v.reduce_max(locmax[:, 0:1], mcol[:, :], AX),
          reads=[mcol.reg], writes=[locmax.reg])
    gm5 = _gmax(self, "q5", locmax[:, 0:1], locmax, 1)
    _qdq_scales(self, SC, 6, gm5, gm5[:, 0:1])
    self.release()

    self.mark()
    while dn_list:
        dn_list.pop(0)()
    dqdT = self.sbs("dq_dT", (128, D), BF16, FI // 128)
    for k in range(FI // 128):
        em.dma("y", dqdT[k][:, :], self.dqd_d.h[128 * k:128 * (k + 1), :],
               reads=[self.dqd_d.reg], writes=[dqdT[k].reg])
    oq = self.sbs("l0f_oq", (128, CW), BF16, FI // 128)
    tq2 = [self.sb(f"l0f_tq{i}", (128, CW), F32) for i in range(2)]
    q22 = [self.sb(f"l0f_q2{i}", (128, CW), F32) for i in range(2)]
    xt2 = [self.sb(f"l0f_xt{i}", (128, NF * CW), F32) for i in range(2)]
    xu2 = [self.sb(f"l0f_xu{i}", (128, NF * CW), F32) for i in range(2)]
    for c in range(nch):
        tsl = slice(CW * c, CW * (c + 1))
        xt, xu = xt2[c % 2], xu2[c % 2]
        for k in range(FI // 128):
            tq, q2 = tq2[k % 2], q22[k % 2]
            em.dma("s", tq[:, :], o_d.h[128 * k:128 * (k + 1), tsl],
                   reads=[o_d.reg], writes=[tq.reg])
            self.ts("v", tq, tq[:, :], tq, tq[:, :], SC[:, 7:8], ALU.mult,
                    s_regs=[SC.reg])
            self.ts("v", q2, q2[:, :], tq, tq[:, :], MAGIC, ALU.add, MAGIC,
                    ALU.subtract)
            self.ts("v", oq[k], oq[k][:, :], q2, q2[:, :], SC[:, 6:7],
                    ALU.mult, s_regs=[SC.reg])
        em.dma("y", xt[:, :],
               xsrc.h.rearrange("(f p) t -> p f t", p=128)[:, :, tsl],
               reads=[xsrc.reg], writes=[xt.reg])
        for o2 in range(NF):
            bk = B[o2 % 2]
            for k in range(FI // 128):
                self.mmx(bk, bk.h[:, 0:CW], dqdT[k],
                         dqdT[k][:, 128 * o2:128 * (o2 + 1)], oq[k],
                         oq[k][:, :], stop=(k == FI // 128 - 1))
            em.op("v", lambda v, o2=o2, bk=bk, xu=xu, xt=xt: v.tensor_tensor(
                xu[:, CW * o2:CW * (o2 + 1)], bk.h[:, 0:CW],
                xt[:, CW * o2:CW * (o2 + 1)], ALU.add),
                reads=[bk.reg, xt.reg], writes=[xu.reg])
            bk.fresh = True
        em.dma("y",
               xdst.h.rearrange("(f p) t -> p f t", p=128)[:, :, tsl],
               xu[:, :], reads=[xu.reg], writes=[xdst.reg])
    self.xdram = xdst
    self.release()


def emit_q1(self):
    em, Tc = self.em, self.Tc
    xq = self.sbs("xq", (128, Tc), BF16, NF)
    self.mark()
    xlnf = self.sbs("q1_x", (128, Tc), F32, NF)
    self.emit_ln("2", xln=None, want_f32=True, want_bf16=False,
                 X_pre=xlnf, f32_inplace=True)
    locmax = self.sb("q1_locmax", (128, NF), F32)
    for f in range(NF):
        em.op("v", lambda v, f=f: v.reduce_max(
            locmax[:, f:f + 1], xlnf[f][:, :], AX, apply_absolute_value=True),
            reads=[xlnf[f].reg], writes=[locmax.reg])
    lm2 = self.sb("q1_lm2", (128, 1), F32)
    em.op("v", lambda v: v.reduce_max(lm2[:, :], locmax[:, :], AX),
          reads=[locmax.reg], writes=[lm2.reg])
    gm = _gmax(self, "q1", lm2[:, 0:1], lm2, 1)
    sc = self.sb("q1_sc", (128, 2), F32)
    _qdq_scales(self, sc, 0, gm, gm[:, 0:1])
    ti = self.sb("q1_ti", (128, Tc), F32)
    tf = self.sb("q1_tf", (128, Tc), F32)
    for f in range(NF):
        self.ts("v", tf, tf[:, :], xlnf[f], xlnf[f][:, :], sc[:, 1:2],
                ALU.mult, s_regs=[sc.reg])
        self.ts("v", ti, ti[:, :], tf, tf[:, :], MAGIC, ALU.add, MAGIC,
                ALU.subtract)
        self.ts("v", xq[f], xq[f][:, :], ti, ti[:, :], sc[:, 0:1], ALU.mult,
                s_regs=[sc.reg])
    self.release()
    return xq


MK.emit_l0f = emit_l0f
MK.emit_q1 = emit_q1
MK.emit_dq_scales = emit_dq_scales
MK.emit_dq_piece_bufs = emit_dq_piece_bufs
MK.emit_dq_wt_pieces = emit_dq_wt_pieces



INV2PI = 1.0 / (2.0 * np.pi)
TWO_PI = 2.0 * np.pi
MAGIC = 12582912.0


def emit_l1a(self, xln):
    em, Tc, NT = self.em, self.Tc, self.NT
    B = [Bank(t) for t in self.pb]
    nch = (Tc + 511) // 512
    CW = min(512, Tc)
    xsrc = self.xdram
    xdst = self.xdram_next()
    cc_in = self.dscratch("cc_l1a_in", (128, 16 * 65), F32)
    cc_out = self.dscratch("cc_l1a_out", (128, 16 * 65), F32)
    zd = self.dscratch("l1a_zd", (H, Tc), F32)
    zdi = self.dscratch("l1a_zdi", (H, Tc), F32)

    phb = self.sb("l1a_phb", (128, 1), F32)
    em.op("g", lambda g: g.memset(phb[0:64, :], 0.25), writes=[phb.reg])
    em.op("g", lambda g: g.memset(phb[64:128, :], 0.0), writes=[phb.reg])
    pc = self.sb("l1a_pc", (128, 128), F32)
    em.op("g", lambda g: g.memset(pc[:, 0:64], 0.25), writes=[pc.reg])
    em.op("g", lambda g: g.memset(pc[:, 64:128], 0.0), writes=[pc.reg])
    rf2 = self.load_w_bf16("rf2", 128, 128)[0]
    kvs = self.sb("l1a_kvs", (128, H * 65), F32)
    kvb = self.sb("l1a_kvb", (128, H * 65), BF16)
    attn = self.sbs("l1a_attn", (128, Tc), BF16, NF)
    onesb = self.ones_col

    self.mark()
    Qp = self.sbs("l1a_qp", (128, Tc), BF16, H)
    self.mark()
    V_tm = self.sbs("l1a_v", (128, D), BF16, NT)
    self.mark()
    wv = self.load_w_bf16("l1_Wv", D, D)
    for tt in range(NT):
        tsl = slice(128 * tt, 128 * (tt + 1))
        for n in range(2):
            nsl = slice(512 * n, 512 * (n + 1))
            bk = B[6]
            for k in range(NF):
                self.mmx(bk, bk.h[:, 0:512], xln[k], xln[k][:, tsl],
                         wv[k], wv[k][:, nsl], stop=(k == NF - 1))
            self.evac(bk, V_tm[tt], V_tm[tt][:, nsl], in_ap=bk.h[:, 0:512])
    self.release()
    wk = self.load_w_bf16("l1_Wk", D, D)
    Kpair2 = [self.sb(f"l1a_kpair{i}", (128, Tc), BF16) for i in range(2)]
    QN = 4 if NT % 4 == 0 else 2
    PW = Tc // QN
    u1 = self.sb("l1a_u1", (128, PW), F32)
    u22 = [self.sb(f"l1a_u2{i}", (128, PW), F32) for i in range(2)]
    kp_h2 = [self.sb(f"l1a_kph{i}", (128, Tc), BF16) for i in range(2)]
    for hp in range(H // 2):
        Kpair = Kpair2[hp % 2]
        jsl = slice(128 * hp, 128 * (hp + 1))
        for c in range(nch):
            tsl = slice(CW * c, CW * (c + 1))
            bk = B[6]
            for k in range(NF):
                self.mmx(bk, bk.h[:, 0:CW], wk[k], wk[k][:, jsl],
                         xln[k], xln[k][:, tsl], stop=(k == NF - 1))
            self.evac(bk, Kpair, Kpair[:, tsl], in_ap=bk.h[:, 0:CW])
        for par in range(2):
            h = 2 * hp + par
            hb = slice(64 * par, 64 * (par + 1))
            kp_h = kp_h2[par]
            for q in range(QN):
                u2 = u22[q % 2]
                hsl = slice(PW * q, PW * (q + 1))
                for tl in range(NT // QN):
                    tt = q * (NT // QN) + tl
                    tsl = slice(128 * tt, 128 * (tt + 1))
                    bk = B[tl % 2]
                    self.mmx(bk, bk.h[:, 0:128], Kpair, Kpair[hb, tsl],
                             rf2, rf2[hb, :], stop=True, tp=(64 * par, 0))
                    em.op("v", lambda v, tl=tl, bk=bk, u1=u1:
                          v.scalar_tensor_tensor(
                        u1[:, 128 * tl:128 * (tl + 1)], bk.h[:, 0:128], INV2PI,
                        pc[:, :], ALU.mult, ALU.add),
                        reads=[bk.reg, pc.reg], writes=[u1.reg])
                    bk.fresh = True
                em.op("v", lambda v, u1=u1, u2=u2: v.tensor_scalar(
                    u2[:, :], u1[:, :], MAGIC, MAGIC, ALU.add, ALU.subtract),
                    reads=[u1.reg], writes=[u2.reg])
                em.op("v", lambda v, u1=u1, u2=u2: v.tensor_sub(
                    u2[:, :], u1[:, :], u2[:, :]),
                    reads=[u1.reg, u2.reg], writes=[u2.reg])
                em.op("s", lambda s, hsl=hsl, kp_h=kp_h, u2=u2: s.activation(
                    kp_h[:, hsl], u2[:, :], ACT.Sin, scale=TWO_PI),
                    reads=[u2.reg], writes=[kp_h.reg])
            bkv = B[2 + par]
            for tt in range(NT):
                tsl = slice(128 * tt, 128 * (tt + 1))
                self.mmx(bkv, bkv.h[:, 0:64], kp_h, kp_h[:, tsl],
                         V_tm[tt], V_tm[tt][:, 64 * h:64 * (h + 1)],
                         stop=(tt == NT - 1))
                self.mmx(bkv, bkv.h[:, 64:65], kp_h, kp_h[:, tsl],
                         onesb, onesb[:, 0:1], stop=(tt == NT - 1))
            self.evac(bkv, kvs, kvs[:, 65 * h:65 * (h + 1)],
                      in_ap=bkv.h[:, 0:65])
    self.release()

    em.dma("y", cc_in.h[:, :], kvs[:, :], reads=[kvs.reg], writes=[cc_in.reg])
    self.em.collective("AllReduce", ALU.add, ins=[cc_in.h[:, :]],
                       outs=[cc_out.h[:, :]], reads=[cc_in.reg],
                       writes=[cc_out.reg])
    em.dma("y", kvs[:, :], cc_out.h[:, :], reads=[cc_out.reg],
           writes=[kvs.reg])
    self.copy("v", kvb, kvb[:, :], kvs, kvs[:, :])

    self.mark()
    wq = self.load_w_bf16("l1_Wq", D, D)
    Qpair2 = [self.sb(f"l1a_qpair{i}", (128, Tc), BF16) for i in range(2)]
    HT = Tc // 2
    u1q = self.sb("l1a_u1q", (128, HT), F32)
    u2q2 = [self.sb(f"l1a_u2q{i}", (128, HT), F32) for i in range(2)]
    for hp in range(H // 2):
        Qpair = Qpair2[hp % 2]
        jsl = slice(128 * hp, 128 * (hp + 1))
        for c in range(nch):
            tsl = slice(CW * c, CW * (c + 1))
            bk = B[6]
            for k in range(NF):
                self.mmx(bk, bk.h[:, 0:CW], wq[k], wq[k][:, jsl],
                         xln[k], xln[k][:, tsl], stop=(k == NF - 1))
            self.evac(bk, Qpair, Qpair[:, tsl], in_ap=bk.h[:, 0:CW])
        for par in range(2):
            h = 2 * hp + par
            hb = slice(64 * par, 64 * (par + 1))
            for half in range(2):
                u2q = u2q2[half]
                hsl = slice(HT * half, HT * (half + 1))
                ncw = max(1, HT // CW)
                for cl in range(ncw):
                    w = min(CW, HT)
                    tsl = slice(HT * half + w * cl, HT * half + w * (cl + 1))
                    bk = B[cl % 2]
                    self.mmx(bk, bk.h[:, 0:w], rf2, rf2[hb, :],
                             Qpair, Qpair[hb, tsl], stop=True, tp=(64 * par, 0))
                    em.op("v", lambda v, w=w, cl=cl, bk=bk, u1q=u1q:
                          v.tensor_scalar(
                        u1q[:, w * cl:w * (cl + 1)], bk.h[:, 0:w], INV2PI,
                        phb[:, 0:1], ALU.mult, ALU.add),
                        reads=[bk.reg, phb.reg], writes=[u1q.reg])
                    bk.fresh = True
                em.op("v", lambda v, u1q=u1q, u2q=u2q: v.tensor_scalar(
                    u2q[:, :], u1q[:, :], MAGIC, MAGIC, ALU.add, ALU.subtract),
                    reads=[u1q.reg], writes=[u2q.reg])
                em.op("v", lambda v, u1q=u1q, u2q=u2q: v.tensor_sub(
                    u2q[:, :], u1q[:, :], u2q[:, :]),
                    reads=[u1q.reg, u2q.reg], writes=[u2q.reg])
                em.op("s", lambda s, h=h, hsl=hsl, u2q=u2q: s.activation(
                    Qp[h][:, hsl], u2q[:, :], ACT.Sin, scale=TWO_PI),
                    reads=[u2q.reg], writes=[Qp[h].reg])
    self.release()

    self.mark()
    zrow2 = [self.sb(f"l1a_zrow{i}", (1, Tc), F32) for i in range(2)]
    rz2 = [self.sb(f"l1a_rz{i}", (128, CW), F32) for i in range(2)]
    for h in range(H):
        zrow = zrow2[h % 2]
        for c in range(nch):
            tsl = slice(CW * c, CW * (c + 1))
            bk = B[4 + c % 2]
            self.mmx(bk, bk.h[0:1, 0:CW], kvb, kvb[:, 65 * h + 64:65 * h + 65],
                     Qp[h], Qp[h][:, tsl], stop=True)
            self.evac(bk, zrow, zrow[0:1, tsl], in_ap=bk.h[0:1, 0:CW])
        em.dma("y", zd.h[h, :], zrow[0:1, :], reads=[zrow.reg], writes=[zd.reg])
    ZW = H * Tc // 128
    zt = self.sb("l1a_zt", (128, ZW), F32)
    em.dma("y", zt[:, :], zd.h.flatten().rearrange("(p w) -> p w", p=128),
           reads=[zd.reg], writes=[zt.reg])
    em.op("v", lambda v: v.tensor_scalar(
        zt[:, :], zt[:, :], 128e-6, None, ALU.max),
        reads=[zt.reg], writes=[zt.reg])
    em.op("v", lambda v: v.reciprocal(zt[:, :], zt[:, :]),
          reads=[zt.reg], writes=[zt.reg])
    em.dma("y", zdi.h.flatten().rearrange("(p w) -> p w", p=128), zt[:, :],
           reads=[zt.reg], writes=[zdi.reg])
    for h in range(H):
        ft, par = h // 2, h % 2
        hb = slice(64 * par, 64 * (par + 1))
        for c in range(nch):
            rz = rz2[c % 2]
            tsl = slice(CW * c, CW * (c + 1))
            bk = B[6 + c % 2]
            self.mmx(bk, bk.h[hb, 0:CW], kvb, kvb[:, 65 * h:65 * h + 64],
                     Qp[h], Qp[h][:, tsl], stop=True, tp=(0, 64 * par),
                     pbase=64 * par)
            em.dma("y", rz[:, 0:CW],
                   zdi.h[h, tsl].unsqueeze(0).partition_broadcast(128),
                   reads=[zdi.reg], writes=[rz.reg])
            em.op("v", lambda v, ft=ft, hb=hb, tsl=tsl, bk=bk, rz=rz:
                  v.tensor_tensor(attn[ft][hb, tsl], bk.h[hb, 0:CW],
                                  rz[hb, 0:CW], ALU.mult),
                  reads=[bk.reg, rz.reg], writes=[attn[ft].reg])
            bk.fresh = True
    self.release()
    self.release()

    self.mark()
    wo = self.load_w_bf16("l1_Wo", D, D)
    xt2 = [self.sb(f"l1a_xt{i}", (128, NF * CW), F32) for i in range(2)]
    xu2 = [self.sb(f"l1a_xu{i}", (128, NF * CW), F32) for i in range(2)]
    for c in range(nch):
        xt, xu = xt2[c % 2], xu2[c % 2]
        tsl = slice(CW * c, CW * (c + 1))
        em.dma("y", xt[:, :],
               xsrc.h.rearrange("(f p) t -> p f t", p=128)[:, :, tsl],
               reads=[xsrc.reg], writes=[xt.reg])
        for o2 in range(NF):
            bk = B[o2 % 2]
            for k in range(NF):
                self.mmx(bk, bk.h[:, 0:CW], wo[k],
                         wo[k][:, 128 * o2:128 * (o2 + 1)], attn[k],
                         attn[k][:, tsl], stop=(k == NF - 1))
            em.op("v", lambda v, o2=o2, bk=bk, xt=xt, xu=xu: v.tensor_tensor(
                xu[:, CW * o2:CW * (o2 + 1)], bk.h[:, 0:CW],
                xt[:, CW * o2:CW * (o2 + 1)], ALU.add),
                reads=[bk.reg, xt.reg], writes=[xu.reg])
            bk.fresh = True
        em.dma("y", xdst.h.rearrange("(f p) t -> p f t", p=128)[:, :, tsl],
               xu[:, :], reads=[xu.reg], writes=[xdst.reg])
    self.xdram = xdst
    self.release()


MK.emit_l1a = emit_l1a



def emit_l1f(self):
    em, Tc, NT = self.em, self.Tc, self.NT
    B = [Bank(t) for t in self.pb]
    nch = (Tc + 511) // 512
    CW = min(512, Tc)
    xsrc_final = None

    cc_in = self.dscratch("cc_l1f_in", (D,), F32)
    cc_out = self.dscratch("cc_l1f_out", (8, D), F32)

    self.mark()
    xr = self.sbs("l1f_xr", (128, Tc), BF16, NF)
    xk = self.sbs("l1f_xk", (128, Tc), BF16, NF)
    self.mark()
    xlnf = self.sbs("l1f_x", (128, Tc), F32, NF)
    tmpm = self.sb("l1f_tmpm", (128, Tc), F32)
    self.emit_ln("4", xln=None, want_f32=True, want_bf16=False,
                 X_pre=xlnf, f32_inplace=True)
    xsrc = self.xdram
    xdst = getattr(self, "xout_override", None)
    if xdst is None:
        xdst = self.xdram_next()

    lastc = self.sb("l1f_lastc", (128, NF), F32)
    for f in range(NF):
        self.copy("v", lastc, lastc[:, f:f + 1], xlnf[f],
                  xlnf[f][:, Tc - 1:Tc])
    for f in range(NF):
        em.dma("y", cc_in.h[128 * f:128 * (f + 1)], lastc[:, f:f + 1],
               reads=[lastc.reg], writes=[cc_in.reg])
    self.em.collective("AllGather", ALU.bypass, ins=[cc_in.h[:]],
                       outs=[cc_out.h[:, :]], reads=[cc_in.reg],
                       writes=[cc_out.reg])
    ohbt = self.inputs["ohbt_mat"]
    oh_t = self.sb("l1f_oh", (128, 8), F32)
    em.dma("y", oh_t[:, :], ohbt.h[:, :], reads=[ohbt.reg], writes=[oh_t.reg])
    agt = self.sb("l1f_agt", (128, 8), F32)
    bcol = self.sb("l1f_bcol", (128, NF), F32)
    for f in range(NF):
        em.dma("y", agt[:, :],
               cc_out.h[:, 128 * f:128 * (f + 1)].rearrange("r p -> p r"),
               reads=[cc_out.reg], writes=[agt.reg],
               allow_slow_non_contiguous=True)
        em.op("v", lambda v, f=f: v.tensor_mul(agt[:, :], agt[:, :],
                                               oh_t[:, :]),
              reads=[oh_t.reg, agt.reg], writes=[agt.reg])
        em.op("v", lambda v, f=f: v.reduce_sum(bcol[:, f:f + 1], agt[:, :], AX),
              reads=[agt.reg], writes=[bcol.reg])

    mu_r = self.load_cols("mu_r_col")
    mu_k = self.load_cols("mu_k_col")
    om = self.sb("l1f_om", (128, 2 * NF), F32)
    for f in range(NF):
        self.ts("v", om, om[:, f:f + 1], mu_r, mu_r[:, f:f + 1], -1.0,
                ALU.mult, 1.0, ALU.add, s_regs=[mu_r.reg])
        self.ts("v", om, om[:, NF + f:NF + f + 1], mu_k, mu_k[:, f:f + 1],
                -1.0, ALU.mult, 1.0, ALU.add, s_regs=[mu_k.reg])

    def mix(dst, mu_t, om_off, tag, tmp):
        for f in range(NF):
            em.op("v", lambda v, f=f: v.tensor_scalar(
                tmp[:, 1:Tc], xlnf[f][:, 0:Tc - 1],
                om[:, om_off + f:om_off + f + 1], None, ALU.mult),
                reads=[xlnf[f].reg, om.reg], writes=[tmp.reg])
            em.op("v", lambda v, f=f: v.tensor_scalar(
                tmp[:, 0:1], bcol[:, f:f + 1],
                om[:, om_off + f:om_off + f + 1], None, ALU.mult),
                reads=[bcol.reg, om.reg], writes=[tmp.reg])
            em.op("v", lambda v, f=f, dst=dst: v.scalar_tensor_tensor(
                dst[f][:, :], xlnf[f][:, :], mu_t[:, f:f + 1], tmp[:, :],
                ALU.mult, ALU.add),
                reads=[xlnf[f].reg, mu_t.reg, tmp.reg], writes=[dst[f].reg])

    mix(xk, mu_k, NF, "k", tmpm)
    mix(xr, mu_r, 0, "r", tmpm)
    self.release()

    nchf = (Tc + 511) // 512
    CWF = min(512, Tc)
    self.mark()
    wkT = self.load_w_bf16("l1f_WkT", D, D)
    wrT = self.load_w_bf16("l1f_WrT", D, D)
    wvT = self.load_w_bf16("l1f_WvT", D, D)
    kct2 = [self.sb(f"l1f_kct{i}", (128, NF * CWF), BF16) for i in range(2)]
    rsl2 = [self.sb(f"l1f_rsl{i}", (128, NF * CWF), F32) for i in range(2)]
    kr2 = [self.sb(f"l1f_kr{i}", (128, CWF), F32) for i in range(2)]
    xt2 = [self.sb(f"l1f_xt{i}", (128, CWF), F32) for i in range(2)]
    xu2 = [self.sb(f"l1f_xu{i}", (128, CWF), F32) for i in range(2)]
    for c in range(nchf):
        kct, rsl = kct2[c % 2], rsl2[c % 2]
        tsl = slice(CWF * c, CWF * (c + 1))
        for o in range(NF):
            bk = B[o % 2]
            kr = kr2[o % 2]
            for k in range(NF):
                self.mmx(bk, bk.h[:, 0:CWF], wkT[k],
                         wkT[k][:, 128 * o:128 * (o + 1)], xk[k],
                         xk[k][:, tsl], stop=(k == NF - 1))
            self.evac(bk, kr, kr[:, 0:CWF], in_ap=bk.h[:, 0:CWF], func=ACT.Relu)
            em.op("v", lambda v, o=o, kr=kr, kct=kct: v.tensor_tensor(
                kct[:, CWF * o:CWF * (o + 1)], kr[:, 0:CWF], kr[:, 0:CWF],
                ALU.mult),
                reads=[kr.reg], writes=[kct.reg])
        for o in range(NF):
            bk = B[2 + o % 2]
            for k in range(NF):
                self.mmx(bk, bk.h[:, 0:CWF], wrT[k],
                         wrT[k][:, 128 * o:128 * (o + 1)], xr[k],
                         xr[k][:, tsl], stop=(k == NF - 1))
            self.evac(bk, rsl, rsl[:, CWF * o:CWF * (o + 1)],
                      in_ap=bk.h[:, 0:CWF], func=ACT.Sigmoid)
        for o2 in range(NF):
            bk = B[4 + o2 % 2]
            xt, xu = xt2[o2 % 2], xu2[o2 % 2]
            em.dma("y", xt[:, :],
                   xsrc.h.rearrange("(f p) t -> p f t", p=128)[:, o2, tsl],
                   reads=[xsrc.reg], writes=[xt.reg])
            for k in range(NF):
                self.mmx(bk, bk.h[:, 0:CWF], wvT[k],
                         wvT[k][:, 128 * o2:128 * (o2 + 1)], kct,
                         kct[:, CWF * k:CWF * (k + 1)], stop=(k == NF - 1))
            em.op("v", lambda v, o2=o2, bk=bk, xu=xu, rsl=rsl: v.tensor_tensor(
                xu[:, :], bk.h[:, 0:CWF],
                rsl[:, CWF * o2:CWF * (o2 + 1)], ALU.mult),
                reads=[bk.reg, rsl.reg], writes=[xu.reg])
            em.op("v", lambda v, xu=xu, xt=xt: v.tensor_tensor(
                xu[:, :], xu[:, :], xt[:, :], ALU.add),
                reads=[xt.reg, xu.reg], writes=[xu.reg])
            bk.fresh = True
            em.dma("y",
                   xdst.h.rearrange("(f p) t -> p f t", p=128)[:, o2, tsl],
                   xu[:, :], reads=[xu.reg], writes=[xdst.reg])
    self.xdram = xdst
    self.release()
    self.release()


MK.emit_l1f = emit_l1f


NCORES = 8


def prep_in_maps(inp, Tc):
    x = np.asarray(inp["L0_attn_x"])[0]
    T, Dm = x.shape
    assert T == Tc * NCORES
    base = {
        "wrT": np.ascontiguousarray(inp["L0_attn_w_r"].T),
        "wkT": np.ascontiguousarray(inp["L0_attn_w_k"].T),
        "wvT": np.ascontiguousarray(inp["L0_attn_w_v"].T),
        "wgT": np.ascontiguousarray(inp["L0_attn_w_g"].T),
        "woT": np.ascontiguousarray(inp["L0_attn_w_o"].T),
        "ldT": np.ascontiguousarray(inp["L0_attn_w_lora_down"].T),
        "luT": np.ascontiguousarray(inp["L0_attn_w_lora_up"].T),
        "u_col": np.ascontiguousarray(inp["L0_attn_u"].reshape(-1, 1)),
        "wbase_col": np.ascontiguousarray(
            inp["L0_attn_w_base"].reshape(-1, 1)),
        "lng_rep": np.ascontiguousarray(
            np.tile(np.tile(inp["L0_attn_ln_gamma"], 16)[None, :], (128, 1))),
        "lnb_rep": np.ascontiguousarray(
            np.tile(np.tile(inp["L0_attn_ln_beta"], 16)[None, :], (128, 1))),
        "wgateN": inp["L0_ffn_w_gate"],
        "wgateT": np.ascontiguousarray(inp["L0_ffn_w_gate"].T),
        "wupN": inp["L0_ffn_w_up"],
        "wupT": np.ascontiguousarray(inp["L0_ffn_w_up"].T),
        "wdownN": inp["L0_ffn_w_down"],
        "wdownT": np.ascontiguousarray(inp["L0_ffn_w_down"].T),
        "rf2": np.ascontiguousarray(np.vstack([inp["L1_attn_rf"],
                                               inp["L1_attn_rf"]])),
        "l1_Wq": inp["L1_attn_Wq"],
        "l1_Wk": inp["L1_attn_Wk"],
        "l1_Wv": inp["L1_attn_Wv"],
        "l1_Wo": inp["L1_attn_Wo"],
        "l1f_WrT": np.ascontiguousarray(inp["L1_ffn_W_r"].T),
        "l1f_WkT": np.ascontiguousarray(inp["L1_ffn_W_k"].T),
        "l1f_WvT": np.ascontiguousarray(inp["L1_ffn_W_v"].T),
        "mu_r_col": np.ascontiguousarray(inp["L1_ffn_mu_r"].reshape(-1, 1)),
        "mu_k_col": np.ascontiguousarray(inp["L1_ffn_mu_k"].reshape(-1, 1)),
    }
    base = {k: v.astype(np.float32) for k, v in base.items()}
    maps = []
    for c in range(NCORES):
        m = dict(base)
        m["x_fm"] = np.ascontiguousarray(x[Tc * c:Tc * (c + 1), :].T)
        oh = np.zeros((128, 8), np.float32)
        oh[:, c] = 1.0
        m["ohprev_mat"] = oh
        ohb = np.zeros((128, 8), np.float32)
        if c > 0:
            ohb[:, c - 1] = 1.0
        m["ohbt_mat"] = ohb
        maps.append(m)
    return maps

def build(Tc):
    mk = MK(Tc=Tc)
    mk.psum_init()
    mk.emit_consts()
    for n, shp in [("wrT", (D, D)), ("wkT", (D, D)), ("wvT", (D, D)),
                   ("wgT", (D, D)), ("woT", (D, D)), ("ldT", (D, LR)),
                   ("luT", (LR, D)), ("u_col", (D, 1)), ("wbase_col", (D, 1)),
                   ("lng_rep", (128, D)), ("lnb_rep", (128, D)),
                   ("ohprev_mat", (128, 8)), ("ohbt_mat", (128, 8)),
                   ("wgateN", (FI, D)), ("wgateT", (D, FI)),
                   ("wupN", (FI, D)), ("wupT", (D, FI)),
                   ("wdownN", (D, FI)), ("wdownT", (FI, D)),
                   ("rf2", (128, 128)), ("l1_Wq", (D, D)), ("l1_Wk", (D, D)),
                   ("l1_Wv", (D, D)), ("l1_Wo", (D, D)),
                   ("l1f_WrT", (D, D)), ("l1f_WkT", (D, D)),
                   ("l1f_WvT", (D, D)),
                   ("mu_r_col", (D, 1)), ("mu_k_col", (D, 1))]:
        mk.din(n, shp)
    mk.emit_dq_scales(keys=("g",))
    mk.emit_load_x()
    mk.mark()
    mk.emit_dq_piece_bufs()
    mk.mark()
    xln1 = mk.sbs("xln_1", (128, Tc), BF16, NF)
    gate_p = mk.emit_dq_wt_pieces("g")
    mk.emit_ln("1", xln=xln1)

    def filler(g):
        for _ in range(4):
            if gate_p:
                gate_p.pop(0)()

    mk.emit_l0a_sweep1(xln1, filler=filler)
    while gate_p:
        gate_p.pop(0)()
    mk.weight_precast([("wgT", (D, D)), ("woT", (D, D)),
                       ("l1_Wq", (D, D)), ("l1_Wk", (D, D)),
                       ("l1_Wv", (D, D)), ("l1_Wo", (D, D)),
                       ("l1f_WrT", (D, D)), ("l1f_WkT", (D, D)),
                       ("l1f_WvT", (D, D))])
    mk.emit_l0a_collective()
    mk.emit_l0a_sweep2(xln1)
    mk.release()
    mk.mark()
    xq = mk.emit_q1()
    mk.emit_dq_scales(keys=("u", "d"))
    up_p = mk.emit_dq_wt_pieces("u")
    dn_p = mk.emit_dq_wt_pieces("d")
    mk.emit_l0f(xq, wt_pieces=(up_p, dn_p))
    mk.release()
    mk.release()
    mk.mark()
    xln3 = mk.sbs("xln_3", (128, Tc), BF16, NF)
    mk.emit_ln("3", xln=xln3)
    mk.emit_l1a(xln3)
    mk.release()
    mk.xout_override = mk.dout("y_fm", (D, Tc))
    mk.emit_l1f()
    mk.emit_store_x()
    return mk, mk.finish()




_CACHE = {}


def kernel(**inputs):
    import numpy as np
    from concourse.bass_utils import run_bass_kernel_spmd
    Tc = 2048
    if "nc" not in _CACHE:
        _CACHE["mk"], _CACHE["nc"] = build(Tc)
    nc = _CACHE["nc"]
    maps = prep_in_maps(inputs, Tc)
    res = run_bass_kernel_spmd(nc, maps, core_ids=list(range(8)))
    out = np.concatenate([np.asarray(res.results[c]["y_fm"]).T
                          for c in range(8)], axis=0)[None]
    return np.ascontiguousarray(out.astype(np.float32))
